# version 17
# baseline (speedup 1.0000x reference)
import numpy as np
import ml_dtypes
from contextlib import ExitStack

import concourse.bass as bass
import concourse.mybir as mybir
from concourse.bass_utils import run_bass_kernel_spmd

F32 = mybir.dt.float32
BF16 = mybir.dt.bfloat16
I32 = mybir.dt.int32
AF = mybir.ActivationFunctionType
ALU = mybir.AluOpType
AX = mybir.AxisListType.X
BF = ml_dtypes.bfloat16

P = 128
B = 8
T = 2048
D = 1024
HID = 2048
FF = 4096
GAMMA = 0.96875
C2 = 256
N_SUB = 4
DBG_ATTNB = False

NT = T // P
ND = D // P
NDC = T // C2
NCH = B * (NT // 2)

GELU_A = float(2.0 * np.sqrt(2.0 / np.pi))
GELU_C = 0.044715


def configure(T_=2048, n_sub=4):
    global T, N_SUB, NT, NDC, NCH
    T = T_
    N_SUB = n_sub
    NT = T // P
    NDC = T // C2
    NCH = B * (NT // 2)


def build_consts():
    g = np.float64(GAMMA)
    c = {}
    c["ident16"] = np.eye(P, dtype=np.float32).astype(BF)
    c["ident32"] = np.eye(P, dtype=np.float32)
    c["ones32row"] = np.ones((1, P), dtype=np.float32)
    u = np.arange(P)[:, None].astype(np.float64)
    i = np.arange(P)[None, :].astype(np.float64)

    def tbl(off, ksub):
        e = off + 2 * i - (u + P * ksub)
        return np.power(g, np.maximum(e, 0.0)) * (e >= 0)

    c["gle0"] = tbl(0.0, 0).astype(BF)
    c["gle1"] = tbl(0.0, 1).astype(BF)
    c["gro0"] = tbl(1.0, 0).astype(BF)
    c["gro1"] = tbl(1.0, 1).astype(BF)
    gle0f = tbl(0.0, 0)
    gro0f = tbl(1.0, 0)
    gle0f[0, :] = 0.0
    gro0f[0, :] = 0.0
    gle0f[:, 0] = 0.0
    gle0f[1, 0] = 1.0
    c["gle0f"] = gle0f.astype(BF)
    c["gro0f"] = gro0f.astype(BF)
    e0 = (C2 - 1) - np.arange(P).astype(np.float64)
    e1 = (P - 1) - np.arange(P).astype(np.float64)
    c["gcar0"] = np.power(g, e0)[:, None].astype(BF)
    c["gcar1"] = np.power(g, e1)[:, None].astype(BF)
    g0f = np.power(g, e0)
    g0f[0] = 0.0
    c["gcar0f"] = g0f[:, None].astype(BF)
    c["ginjL"] = np.power(g, 2 * np.arange(P) + 1)[None, :].astype(BF)
    c["ginjR"] = np.power(g, 2 * np.arange(P) + 2)[None, :].astype(BF)
    c["g256"] = np.array([[g ** C2]], dtype=np.float64).astype(BF)
    jj = np.arange(P)[:, None]
    ii = np.arange(P)[None, :]
    c["mask_jle"] = (jj <= ii).astype(np.float32).astype(BF)
    c["mask_jlt"] = (jj < ii).astype(np.float32).astype(BF)
    c["eps_ln"] = np.full((P, 1), 1e-5, dtype=np.float32)
    c["eps_rms"] = np.full((P, 1), 1e-6, dtype=np.float32)
    return c


DMA_SEM_NAMES = {
    "ld_x", "ld_c", "ld_wv", "ld_wk", "ld_wq", "xb_ld0", "xb_ld1", "yacc0",
    "yacc1", "ybk_ld0", "ybk_ld1", "ld_fb1", "md_ldr", "md_ldfb2", "md_ldw1",
    "md_ldw2", "md_scat0", "md_scat1", "md_selld", "md_upda0", "md_upda1",
    "md_gat0", "md_gat1", "md_gat2", "md_gat3", "ab_vecs", "ab_wq_ld", "ab_wk_ld", "ab_wv_ld",
    "ab_wo_ld", "ab_qst0", "ab_qst1", "ab_kst0", "ab_kst1", "ab_vst0",
    "ab_vst1", "ab_dldk", "ab_dldq", "ab_dldv", "ab_wg0", "ab_wg1", "ab_wu0",
    "ab_wu1", "ab_wd0", "ab_wd1", "ab_pk0", "ab_pk1", "ab_park", "st_out",
    "ab_okf", "ab_ovt", "ab_omt", "ab_ot2", "ab_ikf", "ab_ikt", "ab_iq",
    "ab_ivt", "ab_imt", "ab_it2",
}


class Bld:
    def __init__(self, nc, es):
        self.nc = nc
        self.es = es
        self._s = {}
        self._pool = []

    def sem(self, name):
        if name not in self._s:
            if self._pool and name not in DMA_SEM_NAMES:
                h, cnt = self._pool.pop()
                self._s[name] = [h, cnt, cnt]
            else:
                self._s[name] = [self.es.enter_context(self.nc.semaphore(name)), 0, 0]
        return self._s[name][0]

    def free(self, *names):
        for nm in list(names):
            if nm in self._s and nm not in DMA_SEM_NAMES:
                e = self._s.pop(nm)
                self._pool.append((e[0], e[1]))

    def free_prefix(self, pfx, keep=()):
        for nm in [n for n in self._s if n.startswith(pfx) and n not in keep
                   and n not in DMA_SEM_NAMES]:
            e = self._s.pop(nm)
            self._pool.append((e[0], e[1]))

    def rebase(self, *names):
        for nm in names:
            if nm in self._s:
                self._s[nm][2] = self._s[nm][1]

    def inc(self, instr, name, amt=1):
        h = self.sem(name)
        self._s[name][1] += amt
        instr.then_inc(h, amt)
        return instr

    def n(self, name):
        self.sem(name)
        e = self._s[name]
        return e[1] - e[2]

    def wait(self, eng, name, val=None):
        h = self.sem(name)
        e = self._s[name]
        eng.wait_ge(h, e[1] if val is None else e[2] + val)

    def sb(self, name, shape, dtype):
        return self.es.enter_context(self.nc.sbuf_tensor(name, list(shape), dtype))

    def ps(self, name, shape, dtype=F32):
        return self.es.enter_context(self.nc.psum_tensor(name, list(shape), dtype))


def build_nc():
    consts = build_consts()
    nc = bass.Bass("TRN2", target_bir_lowering=False)

    x_own = nc.dram_tensor("x_own", [T, D], F32, kind="ExternalInput").ap()
    x_all = nc.dram_tensor("x_all", [B, T, D], F32, kind="ExternalInput").ap()
    wD = {}
    for nm, shape in [
        ("aWq", [D, D]), ("aWk", [D, D]), ("aWv", [D, D]),
        ("f_router", [1, D]), ("fW1", [D, FF]), ("fb1", [P, FF // P]),
        ("fW2", [FF, D]), ("fb2", [1, D]),
        ("b_n1", [P, D // P]), ("bWq", [D, D]), ("bWk", [D, D]), ("bWv", [D, D]),
        ("b_betaw", [P, D // P]), ("b_betab", [1, 1]), ("bWo", [D, D]),
        ("b_alpha", [P, D // P]), ("b_n2", [P, D // P]),
        ("b_gate", [D, HID]), ("b_up", [D, HID]), ("b_down", [HID, D]),
    ]:
        wD[nm] = nc.dram_tensor(nm, shape, F32, kind="ExternalInput").ap()
    cD = {}
    for nm, arr in consts.items():
        dt = BF16 if arr.dtype == BF else F32
        cD[nm] = nc.dram_tensor("c_" + nm, list(arr.shape), dt, kind="ExternalInput").ap()
    out = nc.dram_tensor("out", [T, D], F32, kind="ExternalOutput").ap()
    dbg = {}
    if DBG_ATTNB:
        dbg["q"] = nc.dram_tensor("dbg_q", [D, T], BF16, kind="ExternalOutput").ap()
        dbg["k"] = nc.dram_tensor("dbg_k", [T, D], BF16, kind="ExternalOutput").ap()
        dbg["v"] = nc.dram_tensor("dbg_v", [D, T], BF16, kind="ExternalOutput").ap()
        dbg["S"] = nc.dram_tensor("dbg_S", [P, 8 * D], F32, kind="ExternalOutput").ap()
        dbg["beta"] = nc.dram_tensor("dbg_beta", [P, 128], F32, kind="ExternalOutput").ap()
    sel_dram = nc.dram_tensor("sel_dram", [T // 2, D], BF16)
    upd_dram = nc.dram_tensor("upd_dram", [T // 2, D], F32)
    y_dram = nc.dram_tensor("y_dram", [T, D], F32)
    q_dram = nc.dram_tensor("q_dram", [D, T], BF16)
    k_dram = nc.dram_tensor("k_dram", [T, D], BF16)
    vema_dram = nc.dram_tensor("vema_dram", [D, T], BF16)
    kv_scr = nc.dram_tensor("kv_scr", [2 * T, 2 * D], BF16)

    es = ExitStack()
    with es:
        b = Bld(nc, es)
        emit(b, consts, cD, wD, x_own, x_all, out, sel_dram, upd_dram, y_dram,
             q_dram, k_dram, vema_dram, kv_scr, dbg)
    return nc


def emit(b, consts, cD, wD, x_own, x_all, out, sel_dram, upd_dram, y_dram,
         q_dram, k_dram, vema_dram, kv_scr=None, dbg=None):
    nc = b.nc
    blk = b.es.enter_context(nc.Block())

    bc_reg = nc.gpsimd.alloc_register("bc_reg")

    x_tok = b.sb("x_tok", (P, NT * D), F32)
    m_tok = b.sb("m_tok", (P, NT * D), BF16)
    xln_fm = b.sb("xln_fm", (P, ND * T), BF16)
    wbufA = b.sb("wbufA", (P, ND * D), BF16)
    xb_buf_ = b.sb("xb_buf", (P, 2 * 2 * D), F32)
    xb_buf = [xb_buf_[:, 0:2 * D], xb_buf_[:, 2 * D:4 * D]]
    q_scr = xb_buf_[:, 0:T]
    prod_scr = xb_buf_[:, T:2 * T]
    tmp_buf_ = b.sb("tmp_buf", (P, 2 * 2 * D), F32)
    tmp_buf = [tmp_buf_[:, 0:2 * D], tmp_buf_[:, 2 * D:4 * D]]
    ybk_buf = [tmp_buf_[:, 0:D], tmp_buf_[:, D:2 * D]]
    rot_ = b.sb("rot", (P, 2 * D), BF16)
    xln_rot = [rot_[:, 0:D], rot_[:, D:2 * D]]
    trbuf = rot_[:, 0:8 * P]

    cS = {}
    for nm, arr in consts.items():
        dt = BF16 if arr.dtype == BF else F32
        cS[nm] = b.sb("cs_" + nm, arr.shape, dt)

    st = {nm: b.sb("st_" + nm, (P, NT), F32) for nm in ["rstd", "nmr"]}
    bn12 = b.sb("st_bn12", (P, 12 * NT), F32)
    ag2 = b.sb("st_ag2", (P, 2 * NT), F32)
    yst = {nm: b.sb("yst_" + nm, (P, 2 * NCH), F32)
           for nm in ["rstd", "s0", "vsum", "aco", "bco"]}
    ybn = b.sb("yst_bn", (P, 48), F32)
    yag = b.sb("yst_ag", (P, 4 * NCH), F32)

    wvs_col = b.sb("wvs_col", (P, ND), F32)
    wvs_row = b.sb("wvs_row", (1, D), F32)
    wvs_bc = b.sb("wvs_bc", (P, D), F32)
    swv_tmp = b.sb("swv_tmp", (1, 1), F32)
    swv_bc = b.sb("swv_bc", (P, 1), F32)
    carry = b.sb("carry", (1, 2 * D), BF16)
    junk = m_tok.bitcast(F32)[:, 0:2 * D]

    psA = [b.ps(f"psA{i}", (P, 512), F32) for i in range(6)]
    psR = b.ps("psR", (P, 1024), F32)

    @blk.sync
    def _(s):
        b.inc(s.dma_start(
            out=x_tok[:, :].rearrange("p (tt d) -> p tt d", tt=NT),
            in_=x_own.rearrange("(tt p) d -> p tt d", p=P)), "ld_x", 16)

    @blk.gpsimd
    def _(g):
        g.reg_mov(bc_reg, T // 2 - 1)
        for nm in consts:
            b.inc(g.dma_start(out=cS[nm][:, :], in_=cD[nm]), "ld_c", 16)
        b.inc(g.dma_start(
            out=wbufA[:, :].rearrange("p (k n) -> p k n", k=ND),
            in_=wD["aWv"].rearrange("(k p) n -> p k n", p=P)), "ld_wv", 16)

    @blk.vector
    def _(v):
        b.wait(v, "ld_wv")
        for dt in range(ND):
            ii = v.tensor_reduce(out=wvs_col[:, dt:dt + 1],
                                 in_=wbufA[:, dt * D:(dt + 1) * D], axis=AX, op=ALU.add)
        b.inc(ii, "wvs", 1)
        v.memset(carry[0:1, :], 0.0)

    @blk.gpsimd
    def _(g):
        b.wait(g, "wvs")
        b.inc(g.dma_start(
            out=wbufA[:, :].rearrange("p (k n) -> p k n", k=ND),
            in_=wD["aWk"].rearrange("(k p) n -> p k n", p=P)), "ld_wk", 16)

    @blk.tensor
    def _(t):
        b.wait(t, "wvs")
        b.wait(t, "ld_c")
        for dt in range(ND):
            ii = t.matmul(psA[dt // 4][0:1, (dt % 4) * P:(dt % 4 + 1) * P],
                          lhsT=wvs_col[:, dt:dt + 1], rhs=cS["ident32"][:, :],
                          start=True, stop=True)
        b.inc(ii, "wvs_tr", 1)

    @blk.vector
    def _(v):
        b.wait(v, "wvs_tr")
        v.tensor_copy(wvs_row[0:1, 0:512], psA[0][0:1, 0:512])
        v.tensor_copy(wvs_row[0:1, 512:1024], psA[1][0:1, 0:512])
        v.drain()
        b.inc(v.tensor_reduce(out=swv_tmp[0:1, 0:1], in_=wvs_row[0:1, :],
                              axis=AX, op=ALU.add), "swv0", 1)

    @blk.tensor
    def _(t):
        b.wait(t, "swv0")
        t.matmul(psA[2][:, 0:512], lhsT=cS["ones32row"][:, :],
                 rhs=wvs_row[0:1, 0:512], start=True, stop=True)
        t.matmul(psA[3][:, 0:512], lhsT=cS["ones32row"][:, :],
                 rhs=wvs_row[0:1, 512:1024], start=True, stop=True)
        ii = t.matmul(psA[4][:, 0:1], lhsT=cS["ones32row"][:, :],
                      rhs=swv_tmp[0:1, 0:1], start=True, stop=True)
        b.inc(ii, "bc_mm", 1)

    @blk.vector
    def _(v):
        b.wait(v, "bc_mm")
        v.tensor_scalar(out=wvs_bc[:, 0:512], in0=psA[2][:, 0:512],
                        scalar1=-1.0, scalar2=None, op0=ALU.mult)
        v.tensor_scalar(out=wvs_bc[:, 512:1024], in0=psA[3][:, 0:512],
                        scalar1=-1.0, scalar2=None, op0=ALU.mult)
        b.inc(v.tensor_copy(swv_bc[:, :], psA[4][:, 0:1]), "bc_done", 1)

    @blk.vector
    def _(v):
        b.wait(v, "ld_x")
        for tt in range(NT):
            v.bn_stats(bn12[:, 12 * tt:12 * tt + 6], x_tok[:, tt * D:tt * D + 512])
            v.bn_stats(bn12[:, 12 * tt + 6:12 * tt + 12],
                       x_tok[:, tt * D + 512:(tt + 1) * D])
        v.drain()
        for tt in range(NT):
            ii = v.bn_aggr(ag2[:, 2 * tt:2 * tt + 2], bn12[:, 12 * tt:12 * tt + 12])
        b.inc(ii, "own_var", 1)

    @blk.scalar
    def _(s):
        b.wait(s, "own_var")
        b.wait(s, "ld_c")
        s.activation(out=st["rstd"][:, :], in_=ag2[:, 1::2], func=AF.Ln,
                     bias=cS["eps_ln"][:, :])
        s.drain()
        b.inc(s.activation(out=st["rstd"][:, :], in_=st["rstd"][:, :], func=AF.Exp,
                           scale=-0.5), "own_rstd", 1)

    @blk.vector
    def _(v):
        b.wait(v, "own_rstd")
        v.tensor_tensor(out=st["nmr"][:, :], in0=ag2[:, 0::2],
                        in1=st["rstd"][:, :], op=ALU.mult)
        v.drain()
        b.inc(v.tensor_scalar(out=st["nmr"][:, :], in0=st["nmr"][:, :], scalar1=-1.0,
                              scalar2=None, op0=ALU.mult), "own_nmr", 1)

    @blk.scalar
    def _(s):
        b.wait(s, "own_nmr")
        for tt in range(NT):
            if tt >= 2:
                b.wait(s, "xln_ev", 2 * (tt - 1))
            b.inc(s.activation(out=xln_rot[tt % 2],
                               in_=x_tok[:, tt * D:(tt + 1) * D], func=AF.Identity,
                               scale=st["rstd"][:, tt:tt + 1],
                               bias=st["nmr"][:, tt:tt + 1]), "xln_em", 1)

    @blk.tensor
    def _(t):
        b.wait(t, "bc_done")
        for tt in range(NT):
            b.wait(t, "xln_em", tt + 1)
            if tt >= 1:
                b.wait(t, "xln_ev", 2 * tt)
            for dt in range(ND):
                ii = t.matmul(psA[4 + dt // 4][:, (dt % 4) * P:(dt % 4 + 1) * P],
                              lhsT=xln_rot[tt % 2][:, dt * P:(dt + 1) * P],
                              rhs=cS["ident16"][:, :], start=True, stop=True)
            b.inc(ii, "xln_tr", 1)

    @blk.vector
    def _(v):
        xfm = xln_fm[:, :].rearrange("p (dt t) -> p dt t", dt=ND)
        for tt in range(NT):
            b.wait(v, "xln_tr", tt + 1)
            for h in range(2):
                b.inc(v.tensor_copy(
                    xfm[:, 4 * h:4 * h + 4, tt * P:(tt + 1) * P],
                    psA[4 + h][:, 0:512].rearrange("p (dt t) -> p dt t", dt=4)),
                    "xln_ev", 1)

    xa4 = x_all.rearrange("b (cc q p) d -> b p cc q d", p=P, q=2)
    NREG = NT // 2
    YORD = [(ch_, bi_) for ch_ in range(NREG) for bi_ in range(B)]
    assert len(YORD) == NCH
    ADDCNT = []
    _c = 0
    for pos, (ch_, bi_) in enumerate(YORD):
        if bi_ >= 1:
            _c += 1
        ADDCNT.append(_c)
    ysb = [m_tok.bitcast(F32)[:, 2 * D:4 * D], m_tok.bitcast(F32)[:, 4 * D:6 * D]]

    @blk.sync
    def _(s):
        for pos, (ch, bi) in enumerate(YORD):
            if pos >= 2:
                b.wait(s, "xb_fv", pos - 1)
                b.wait(s, "xb_fs", pos - 1)
            b.inc(s.dma_start(
                out=xb_buf[pos % 2].rearrange("p (q d) -> p q d", q=2),
                in_=xa4[bi, :, ch, :, :]), f"xb_ld{pos % 2}", 16)

    @blk.vector
    def _(v):
        for pos, (ch, bi) in enumerate(YORD):
            par = pos % 2
            b.wait(v, f"xb_ld{pos % 2}", 16 * (pos // 2 + 1))
            v.drain()
            xb = xb_buf[par]
            c0 = 2 * pos
            sl = slice(c0, c0 + 2)
            yb0 = 24 * (pos % 2)
            for q in range(2):
                v.bn_stats(ybn[:, yb0 + 12 * q:yb0 + 12 * q + 6],
                           xb[:, q * D:q * D + 512])
                v.bn_stats(ybn[:, yb0 + 12 * q + 6:yb0 + 12 * q + 12],
                           xb[:, q * D + 512:(q + 1) * D])
            v.drain()
            for q in range(2):
                ii = v.bn_aggr(yag[:, 4 * pos + 2 * q:4 * pos + 2 * q + 2],
                               ybn[:, yb0 + 12 * q:yb0 + 12 * q + 12])
            b.inc(ii, "y_var", 1)
            v.scalar_tensor_tensor(out=junk[:, 0:D], in0=xb[:, 0:D], scalar=1.0,
                                   in1=wvs_bc[:, :], op0=ALU.mult, op1=ALU.mult,
                                   accum_out=yst["s0"][:, c0:c0 + 1])
            ii = v.scalar_tensor_tensor(out=junk[:, D:2 * D], in0=xb[:, D:2 * D],
                                        scalar=1.0, in1=wvs_bc[:, :],
                                        op0=ALU.mult, op1=ALU.mult,
                                        accum_out=yst["s0"][:, c0 + 1:c0 + 2])
            b.inc(ii, "xb_fv", 1)
            if pos >= 1 and YORD[pos - 1][1] >= 1:
                chp = YORD[pos - 1][0]
                b.wait(v, "xb_fs", pos)
                b.inc(v.tensor_tensor(out=ysb[chp % 2][:, :],
                                      in0=tmp_buf[(pos - 1) % 2][:, :],
                                      in1=ysb[chp % 2][:, :], op=ALU.add),
                      "y_gadd", 1)
            if pos == NCH - 1:
                b.wait(v, "xb_fs", pos + 1)
                v.drain()
                b.inc(v.tensor_tensor(out=ysb[ch % 2][:, :],
                                      in0=tmp_buf[pos % 2][:, :],
                                      in1=ysb[ch % 2][:, :], op=ALU.add),
                      "y_gadd", 1)

    @blk.scalar
    def _(s):
        for pos, (ch, bi) in enumerate(YORD):
            par = pos % 2
            c0 = 2 * pos
            b.wait(s, "y_var", pos + 1)
            s.activation(out=yst["rstd"][:, c0:c0 + 2],
                         in_=yag[:, 4 * pos + 1:4 * pos + 4:2],
                         func=AF.Ln, bias=cS["eps_ln"][:, :])
            s.drain()
            s.activation(out=yst["rstd"][:, c0:c0 + 2],
                         in_=yst["rstd"][:, c0:c0 + 2], func=AF.Exp, scale=-1.0)
            b.wait(s, "xb_fv", pos + 1)
            s.drain()
            for q in range(2):
                cq = c0 + q
                s.activation(out=yst["vsum"][:, cq:cq + 1],
                             in_=yag[:, 4 * pos + 2 * q:4 * pos + 2 * q + 1],
                             func=AF.Identity, scale=swv_bc[:, 0:1],
                             bias=yst["s0"][:, cq:cq + 1])
            s.drain()
            for q in range(2):
                cq = c0 + q
                s.activation(out=yst["aco"][:, cq:cq + 1],
                             in_=yst["vsum"][:, cq:cq + 1], func=AF.Identity,
                             scale=yst["rstd"][:, cq:cq + 1])
            s.drain()
            for q in range(2):
                cq = c0 + q
                s.activation(out=yst["bco"][:, cq:cq + 1],
                             in_=yag[:, 4 * pos + 2 * q:4 * pos + 2 * q + 1],
                             func=AF.Identity, scale=yst["aco"][:, cq:cq + 1])
            for q in range(2):
                cq = c0 + q
                s.activation(out=yst["aco"][:, cq:cq + 1],
                             in_=yst["aco"][:, cq:cq + 1], func=AF.Identity,
                             scale=-1.0)
            s.drain()
            xb = xb_buf[par]
            if bi == 0:
                dst = ysb[ch % 2]
                if ch >= 2:
                    b.wait(s, f"yacc{ch % 2}", 16 * (ch // 2))
            else:
                dst = tmp_buf[par]
                if pos >= 2:
                    b.wait(s, "y_gadd", ADDCNT[pos - 2])
            s.activation(out=dst[:, 0:D], in_=xb[:, 0:D], func=AF.Identity,
                         scale=yst["aco"][:, c0:c0 + 1], bias=yst["bco"][:, c0:c0 + 1])
            ii = s.activation(out=dst[:, D:2 * D], in_=xb[:, D:2 * D],
                              func=AF.Identity, scale=yst["aco"][:, c0 + 1:c0 + 2],
                              bias=yst["bco"][:, c0 + 1:c0 + 2])
            b.inc(ii, "xb_fs", 1)

    @blk.gpsimd
    def _(g):
        for ch in range(NREG):
            b.wait(g, "y_gadd", ADDCNT[ch * B + B - 1])
            if ch >= 2:
                b.wait(g, f"yacc{ch % 2}", 16 * (ch // 2))
            dst = y_dram[ch * 2 * P:(ch + 1) * 2 * P, :].rearrange(
                "(q p) d -> p q d", p=P)
            b.inc(g.dma_start(
                out=dst, in_=ysb[ch % 2].rearrange("p (q d) -> p q d", q=2)),
                f"yacc{ch % 2}", 16)

    @blk.gpsimd
    def _(g):
        b.wait(g, "m_mm", 2 * NT)
        b.inc(g.dma_start(
            out=wbufA[:, :].rearrange("p (k n) -> p k n", k=ND),
            in_=wD["aWq"].rearrange("(k p) n -> p k n", p=P)), "ld_wq", 16)

    @blk.sync
    def _(s):
        b.wait(s, "yacc0")
        b.wait(s, "yacc1")
        for tt in range(NT):
            if tt >= 2:
                b.wait(s, "ytr_mm", tt - 1)
            b.inc(s.dma_start(out=ybk_buf[tt % 2],
                              in_=y_dram[tt * P:(tt + 1) * P, :]), f"ybk_ld{tt % 2}", 16)

    @blk.tensor
    def _(t):
        b.wait(t, "ld_wk")
        for tt in range(NT):
            b.wait(t, f"ybk_ld{tt % 2}", 16 * (tt // 2 + 1))
            if tt >= 1:
                b.wait(t, "m_ev", 2 * tt)
            for dt in range(ND):
                ii = t.matmul(psA[dt // 4][:, (dt % 4) * P:(dt % 4 + 1) * P],
                              lhsT=ybk_buf[tt % 2][:, dt * P:(dt + 1) * P],
                              rhs=cS["ident32"][:, :], start=True, stop=True)
            b.inc(ii, "ytr_mm", 1)
            b.wait(t, "ytr_ev", 2 * (tt + 1))
            for h in range(2):
                for dt in range(ND):
                    ii = t.matmul(psA[2 + h][:, 0:512],
                                  lhsT=trbuf[:, dt * P:(dt + 1) * P],
                                  rhs=wbufA[:, dt * D + h * 512: dt * D + (h + 1) * 512],
                                  start=(dt == 0), stop=(dt == ND - 1))
                b.inc(ii, "m_mm", 1)

    @blk.vector
    def _(v):
        v.drain()
        for tt in range(NT):
            b.wait(v, "ytr_mm", tt + 1)
            for h in range(2):
                b.inc(v.tensor_copy(trbuf[:, h * 512:(h + 1) * 512],
                                    psA[h][:, 0:512]), "ytr_ev", 1)
            b.wait(v, "m_mm", 2 * (tt + 1))
            for h in range(2):
                b.inc(v.tensor_copy(m_tok[:, tt * D + h * 512: tt * D + (h + 1) * 512],
                                    psA[2 + h][:, 0:512]), "m_ev", 1)

    @blk.tensor
    def _(t):
        b.wait(t, "m_ev")
        b.wait(t, "ld_wq")
        b.wait(t, "bc_done")
        for k in range(NDC):
            first = (k == 0)
            gl = ("gle0f" if first else "gle0", "gle1")
            gr = ("gro0f" if first else "gro0", "gro1")
            gc = ("gcar0f" if first else "gcar0", "gcar1")
            mk0 = m_tok[:, (2 * k) * D:(2 * k) * D + D]
            mk1 = m_tok[:, (2 * k + 1) * D:(2 * k + 1) * D + D]
            cr_in = carry[0:1, (k % 2) * D:(k % 2) * D + D]
            if k >= 1:
                b.wait(t, "q_ev", 4 * k)
                b.wait(t, "oa_ev", 4 * k)
            for nq in range(4):
                for dt in range(ND):
                    ii = t.matmul(psA[nq][:, 0:512],
                                  lhsT=wbufA[:, dt * D + k * P: dt * D + (k + 1) * P],
                                  rhs=xln_fm[:, dt * T + nq * 512: dt * T + (nq + 1) * 512],
                                  start=(dt == 0), stop=(dt == ND - 1))
            b.inc(ii, "q_mm", 1)
            b.wait(t, "car_ev", k)
            if k >= 2:
                pass
            for h in range(2):
                t.matmul(psR[0:1, h * 512:(h + 1) * 512], lhsT=cS[gc[0]][:, :],
                         rhs=mk0[:, h * 512:(h + 1) * 512], start=True, stop=False)
                t.matmul(psR[0:1, h * 512:(h + 1) * 512], lhsT=cS[gc[1]][:, :],
                         rhs=mk1[:, h * 512:(h + 1) * 512], start=False, stop=False)
                ii = t.matmul(psR[0:1, h * 512:(h + 1) * 512], lhsT=cS["g256"][:, :],
                              rhs=cr_in[0:1, h * 512:(h + 1) * 512],
                              start=False, stop=True)
            b.inc(ii, "car_mm", 1)
            for ci in range(4):
                side, h = divmod(ci, 2)
                gm = gl if side == 0 else gr
                ginj = "ginjL" if side == 0 else "ginjR"
                pt = psA[4 + ci % 2]
                if k >= 1 or ci >= 2:
                    b.wait(t, "lr_ev", 4 * k + ci - 1)
                t.matmul(pt[:, 0:512], lhsT=cS[gm[0]][:, :],
                         rhs=mk0[:, h * 512:(h + 1) * 512], start=True, stop=False)
                t.matmul(pt[:, 0:512], lhsT=cS[gm[1]][:, :],
                         rhs=mk1[:, h * 512:(h + 1) * 512], start=False, stop=False)
                ii = t.matmul(pt[:, 0:512], lhsT=cS[ginj][:, :],
                              rhs=cr_in[0:1, h * 512:(h + 1) * 512],
                              start=False, stop=True)
                b.inc(ii, "lr_mm", 1)
            b.wait(t, "lr_ev", 4 * k + 4)
            for g4 in range(4):
                if k > 0 or g4 > 0:
                    b.wait(t, "oa_ev", 4 * k + g4)
                for j in range(4):
                    tt4 = g4 * 4 + j
                    ii = t.matmul(psA[0][:, j * P:(j + 1) * P],
                                  lhsT=prod_scr[:, tt4 * P:(tt4 + 1) * P],
                                  rhs=cS["ident32"][:, :], start=True, stop=True)
                b.inc(ii, "oa_mm", 1)

    @blk.vector
    def _(v):
        v.drain()
        xv = x_tok[:, :].rearrange("p (tt d) -> p tt d", tt=NT)
        for k in range(NDC):
            b.wait(v, "q_mm", k + 1)
            for nq in range(4):
                b.inc(v.tensor_copy(q_scr[:, nq * 512:(nq + 1) * 512],
                                    psA[nq][:, 0:512]), "q_ev", 1)
            v.drain()
            b.wait(v, "car_mm", k + 1)
            b.wait(v, "lr_mm", 4 * k)
            b.inc(v.tensor_copy(carry[0:1, ((k + 1) % 2) * D:((k + 1) % 2) * D + D],
                                psR[0:1, 0:1024]), "car_ev", 1)
            for ci in range(4):
                side, h = divmod(ci, 2)
                b.wait(v, "lr_mm", 4 * k + ci + 1)
                off = side * 1024 + h * 512
                b.inc(v.tensor_tensor(out=prod_scr[:, off:off + 512],
                                      in0=psA[4 + ci % 2][:, 0:512],
                                      in1=q_scr[:, off:off + 512],
                                      op=ALU.mult), "lr_ev", 1)
            for g4 in range(4):
                b.wait(v, "oa_mm", 4 * k + g4 + 1)
                b.inc(v.tensor_tensor(
                    out=xv[:, 4 * g4:4 * g4 + 4, k * P:(k + 1) * P],
                    in0=psA[0][:, 0:512].rearrange("p (j t) -> p j t", j=4),
                    in1=xv[:, 4 * g4:4 * g4 + 4, k * P:(k + 1) * P],
                    op=ALU.add), "oa_ev", 1)

    m32 = m_tok.bitcast(F32)
    xs_d = [m32[:, 0:D], m32[:, 5 * D:6 * D]]
    u2_d = [m32[:, 2 * D:3 * D], m32[:, 6 * D:7 * D]]
    sq_d = [m32[:, D:2 * D], m32[:, D:2 * D]]
    stage_scr = [m32[:, 3 * D:4 * D], m32[:, 4 * D:5 * D]]
    b2_bc = wvs_bc[:, 0:D]
    router_bc = q_scr[:, 0:D]
    mstat = m32[:, 7 * D:8 * D]
    cmp_scr = m32[:, 0:2 * D]
    gbuf = [m32[:, 0:D], m32[:, D:2 * D],
            m32[:, 3 * D:4 * D], m32[:, 4 * D:5 * D]]
    score_bc = m32[:, 3 * D:5 * D]
    sel_tok = xln_fm[:, 0:8 * D]
    sel_fm = xln_fm[:, 8 * D:16 * D]
    w1buf = wbufA
    w2buf = tmp_buf_.bitcast(BF16)[:, 0:8 * D]
    h1q = xb_buf_.bitcast(BF16)[:, 0:8 * D]
    augb = [rot_[:, 0:D], rot_[:, D:2 * D]]
    msc_score = mstat[:, 0:NT]
    msc_rank = mstat[:, NT:2 * NT]
    msc_mw = mstat[:, 2 * NT:3 * NT]
    msc_s0 = mstat[:, 3 * NT:4 * NT]
    msc_sumr = mstat[:, 4 * NT:4 * NT + 1]
    idx_tok = b.sb("idx_tok", (P, NT), I32)
    fb1_s = b.sb("fb1_s", (P, FF // P), F32)
    fb2row = prod_scr[0:1, 0:D]
    rrow = wvs_row
    srow = q_scr[0:1, 0:T]

    def bn_tile_hook(v, tt):
        v.drain()
        v.bn_stats(bn12[:, 12 * tt:12 * tt + 6], x_tok[:, tt * D:tt * D + 512])
        v.bn_stats(bn12[:, 12 * tt + 6:12 * tt + 12],
                   x_tok[:, tt * D + 512:(tt + 1) * D])

    def emit_mod(inv, post_tile=None):
        V = f"v{inv}_"
        selv = sel_dram.ap()
        updv = upd_dram.ap()

        @blk.gpsimd
        def _(g):
            (b.wait(g, "oa_ev") if inv == 0 else
             (b.wait(g, "ab_m_xadd", b.n("ab_m_xadd")),
              b.wait(g, "ab_pkadd", b.n("ab_pkadd"))))
            b.inc(g.dma_start(out=rrow[0:1, 0:D], in_=wD["f_router"]), "md_ldr", 16)
            b.inc(g.dma_start(
                out=w1buf[:, :].rearrange("p (k n) -> p k n", k=ND),
                in_=wD["fW1"].rearrange("(k p) n -> p k n", p=P)[:, :, 0:D]),
                "md_ldw1", 16)
            b.inc(g.dma_start(
                out=w2buf[:, :].rearrange("p (k n) -> p k n", k=ND),
                in_=wD["fW2"].rearrange("(k p) n -> p k n", p=P)[:, 0:ND, :]),
                "md_ldw2", 16)
            if inv == 0:
                b.inc(g.dma_start(out=fb1_s[:, :], in_=wD["fb1"]), "ld_fb1", 16)
            b.inc(g.dma_start(out=fb2row, in_=wD["fb2"]), "md_ldfb2", 16)

        @blk.vector
        def _(v):
            if inv == 0:
                b.wait(v, "oa_ev")
                v.drain()
                for tt in range(NT):
                    v.bn_stats(bn12[:, 12 * tt:12 * tt + 6],
                               x_tok[:, tt * D:tt * D + 512])
                    v.bn_stats(bn12[:, 12 * tt + 6:12 * tt + 12],
                               x_tok[:, tt * D + 512:(tt + 1) * D])
            v.drain()
            for tt in range(NT):
                ii = v.bn_aggr(ag2[:, 2 * tt:2 * tt + 2], bn12[:, 12 * tt:12 * tt + 12])
            b.inc(ii, V + "var", 1)

        @blk.scalar
        def _(s):
            b.wait(s, V + "var")
            s.activation(out=st["rstd"][:, :], in_=ag2[:, 1::2], func=AF.Ln,
                         bias=cS["eps_ln"][:, :])
            s.drain()
            b.inc(s.activation(out=st["rstd"][:, :], in_=st["rstd"][:, :],
                               func=AF.Exp, scale=-0.5), V + "rstd", 1)

        @blk.tensor
        def _(t):
            (b.wait(t, "oa_ev") if inv == 0 else
             (b.wait(t, "ab_m_xadd", b.n("ab_m_xadd")),
              b.wait(t, "ab_pkadd", b.n("ab_pkadd"))))
            b.wait(t, "md_ldr", 16)
            t.matmul(psA[0][:, 0:512], lhsT=cS["ones32row"][:, :],
                     rhs=rrow[0:1, 0:512], start=True, stop=True)
            ii = t.matmul(psA[1][:, 0:512], lhsT=cS["ones32row"][:, :],
                          rhs=rrow[0:1, 512:1024], start=True, stop=True)
            b.inc(ii, V + "rbc_mm", 1)

        @blk.vector
        def _(v):
            b.wait(v, V + "rbc_mm")
            v.tensor_copy(router_bc[:, 0:512], psA[0][:, 0:512])
            v.tensor_copy(router_bc[:, 512:1024], psA[1][:, 0:512])
            b.wait(v, "md_ldr", 16)
            v.drain()
            b.inc(v.tensor_reduce(out=swv_tmp[0:1, 0:1], in_=rrow[0:1, :],
                                  axis=AX, op=ALU.add), V + "sumr0", 1)

        @blk.tensor
        def _(t):
            b.wait(t, V + "sumr0")
            ii = t.matmul(psA[2][:, 0:1], lhsT=cS["ones32row"][:, :],
                          rhs=swv_tmp[0:1, 0:1], start=True, stop=True)
            b.inc(ii, V + "srbc_mm", 1)
            b.wait(t, "md_ldfb2", 16)
            t.matmul(psA[3][:, 0:512], lhsT=cS["ones32row"][:, :],
                     rhs=fb2row[0:1, 0:512], start=True, stop=True)
            ii = t.matmul(psA[4][:, 0:512], lhsT=cS["ones32row"][:, :],
                          rhs=fb2row[0:1, 512:1024], start=True, stop=True)
            b.inc(ii, V + "b2bc_mm", 1)

        @blk.vector
        def _(v):
            b.wait(v, V + "srbc_mm")
            b.inc(v.tensor_copy(msc_sumr[:, 0:1], psA[2][:, 0:1]), V + "srbc", 1)
            b.wait(v, V + "b2bc_mm")
            v.tensor_copy(b2_bc[:, 0:512], psA[3][:, 0:512])
            v.tensor_copy(b2_bc[:, 512:1024], psA[4][:, 0:512])
            v.drain()
            for tt in range(NT):
                jo = (tt % 2) * D
                ii = v.scalar_tensor_tensor(out=junk[:, jo:jo + D],
                                            in0=x_tok[:, tt * D:(tt + 1) * D],
                                            scalar=1.0, in1=router_bc[:, 0:D],
                                            op0=ALU.mult, op1=ALU.mult,
                                            accum_out=msc_s0[:, tt:tt + 1])
                if tt % 2 == 1:
                    v.drain()
            v.drain()
            b.wait(v, V + "rstd")
            v.tensor_scalar(out=msc_score[:, :], in0=ag2[:, 0::2],
                            scalar1=msc_sumr[:, 0:1], scalar2=None, op0=ALU.mult)
            v.drain()
            v.tensor_tensor(out=msc_score[:, :], in0=msc_s0[:, :],
                            in1=msc_score[:, :], op=ALU.subtract)
            v.drain()
            b.inc(v.tensor_tensor(out=msc_score[:, :], in0=msc_score[:, :],
                                  in1=st["rstd"][:, :], op=ALU.mult), V + "score", 1)

        @blk.tensor
        def _(t):
            b.wait(t, V + "score")
            for tt in range(NT):
                ii = t.matmul(psA[2 + tt // 4][0:1, (tt % 4) * P:(tt % 4 + 1) * P],
                              lhsT=msc_score[:, tt:tt + 1], rhs=cS["ident32"][:, :],
                              start=True, stop=True)
            b.inc(ii, V + "srow_mm", 1)

        @blk.vector
        def _(v):
            b.wait(v, V + "srow_mm")
            for hh in range(4):
                ii = v.tensor_copy(srow[0:1, hh * 512:(hh + 1) * 512],
                                   psA[2 + hh][0:1, 0:512])
            b.inc(ii, V + "srow", 1)

        @blk.tensor
        def _(t):
            b.wait(t, V + "srow")
            for hh in range(4):
                ii = t.matmul(psA[hh][:, 0:512], lhsT=cS["ones32row"][:, :],
                              rhs=srow[0:1, hh * 512:(hh + 1) * 512],
                              start=True, stop=True)
            b.inc(ii, V + "sbc_mm", 1)

        @blk.vector
        def _(v):
            b.wait(v, V + "sbc_mm")
            for hh in range(4):
                ii = v.tensor_copy(score_bc[:, hh * 512:(hh + 1) * 512],
                                   psA[hh][:, 0:512])
            v.drain()
            for tt in range(NT):
                ii = v.tensor_scalar(out=cmp_scr[:, 0:T], in0=score_bc[:, 0:T],
                                     scalar1=msc_score[:, tt:tt + 1], scalar2=None,
                                     op0=ALU.is_gt, op1=ALU.add,
                                     accum_out=msc_rank[:, tt:tt + 1])
                v.drain()
            v.tensor_copy(idx_tok[:, :], msc_rank[:, :])
            b.inc(v.tensor_scalar(out=msc_mw[:, :], in0=msc_rank[:, :],
                                  scalar1=float(T // 2), scalar2=None,
                                  op0=ALU.is_lt), V + "rank", 1)

        @blk.scalar
        def _(s):
            b.wait(s, V + "rank")
            b.inc(s.activation(out=msc_s0[:, :], in_=msc_score[:, :],
                               func=AF.Sigmoid), V + "sigw", 1)

        @blk.vector
        def _(v):
            b.wait(v, V + "sigw")
            b.inc(v.tensor_tensor(out=msc_mw[:, :], in0=msc_mw[:, :],
                                  in1=msc_s0[:, :], op=ALU.mult), V + "mw", 1)

        @blk.vector
        def _(v):
            b.wait(v, V + "rstd")
            v.tensor_tensor(out=st["nmr"][:, :], in0=ag2[:, 0::2],
                            in1=st["rstd"][:, :], op=ALU.mult)
            v.drain()
            b.inc(v.tensor_scalar(out=st["nmr"][:, :], in0=st["nmr"][:, :],
                                  scalar1=-1.0, scalar2=None, op0=ALU.mult),
                  V + "nmr", 1)

        @blk.scalar
        def _(s):
            b.wait(s, V + "nmr")
            for tt in range(NT):
                if tt >= 2:
                    b.wait(s, f"md_scat{(tt - 2) % 2}", 16 * ((tt - 2) // 2 + 1))
                b.inc(s.activation(out=augb[tt % 2], in_=x_tok[:, tt * D:(tt + 1) * D],
                                   func=AF.Identity, scale=st["rstd"][:, tt:tt + 1],
                                   bias=st["nmr"][:, tt:tt + 1]), V + "aug", 1)

        @blk.gpsimd
        def _(g):
            b.wait(g, V + "rank")
            for tt in range(NT):
                b.wait(g, V + "aug", tt + 1)
                b.inc(g.indirect_dma_start(
                    out=selv, out_offset=bass.IndirectOffsetOnAxis(
                        idx_tok[:, tt:tt + 1], axis=0),
                    in_=augb[tt % 2], in_offset=None,
                    bounds_check=bc_reg, oob_is_err=False),
                    f"md_scat{tt % 2}", 16)

        @blk.sync
        def _(s):
            b.wait(s, "md_scat0")
            b.wait(s, "md_scat1")
            b.inc(s.dma_start(
                out=sel_tok[:, :].rearrange("p (c f) -> p c f", c=8),
                in_=sel_dram.ap().rearrange("(c p) f -> p c f", p=P)),
                "md_selld", 16)

        @blk.tensor
        def _(t):
            b.wait(t, "md_selld", 16)
            for c in range(8):
                if c >= 1:
                    b.wait(t, V + "self_ev", 2 * c)
                for dt in range(ND):
                    ii = t.matmul(psA[4 + dt // 4][:, (dt % 4) * P:(dt % 4 + 1) * P],
                                  lhsT=sel_tok[:, c * D + dt * P:c * D + (dt + 1) * P],
                                  rhs=cS["ident16"][:, :], start=True, stop=True)
                b.inc(ii, V + "self_mm", 1)

        @blk.vector
        def _(v):
            sfm = sel_fm.rearrange("p (dt r) -> p dt r", dt=ND)
            for c in range(8):
                b.wait(v, V + "self_mm", c + 1)
                for h in range(2):
                    b.inc(v.tensor_copy(
                        sfm[:, 4 * h:4 * h + 4, c * P:(c + 1) * P],
                        psA[4 + h][:, 0:512].rearrange("p (dt r) -> p dt r", dt=4)),
                        V + "self_ev", 1)

        NQ4 = FF // D

        @blk.gpsimd
        def _(g):
            for q in range(NQ4):
                if q + 1 < NQ4:
                    b.wait(g, V + "h1_mm", 8 * (q + 1))
                    b.inc(g.dma_start(
                        out=w1buf[:, :].rearrange("p (k n) -> p k n", k=ND),
                        in_=wD["fW1"].rearrange("(k p) n -> p k n", p=P)[:, :, (q + 1) * D:(q + 2) * D]),
                        "md_ldw1", 16)
                for rt in range(8):
                    gi = 8 * q + rt
                    par = rt % 2
                    b.wait(g, V + "h2_ev", 2 * (gi + 1))
                    if q >= 1:
                        uprev = (8 * (q - 1) + rt - par) // 2
                        b.wait(g, f"md_upda{par}", 16 * (uprev + 1))
                    if q > 0:
                        ii = g.dma_start(out=updv[rt * P:(rt + 1) * P, :],
                                         in_=stage_scr[par], accum_op=ALU.add)
                    else:
                        ii = g.dma_start(out=updv[rt * P:(rt + 1) * P, :],
                                         in_=stage_scr[par])
                    b.inc(ii, f"md_upda{par}", 16)
                if q + 1 < NQ4:
                    b.wait(g, V + "h2_mm", 8 * (q + 1))
                    b.inc(g.dma_start(
                        out=w2buf[:, :].rearrange("p (k n) -> p k n", k=ND),
                        in_=wD["fW2"].rearrange("(k p) n -> p k n", p=P)[:, (q + 1) * ND:(q + 2) * ND, :]),
                        "md_ldw2", 16)

        @blk.tensor
        def _(t):
            b.wait(t, V + "self_ev")
            if inv == 0:
                b.wait(t, "car_ev")
            for q in range(NQ4):
                b.wait(t, "md_ldw1", 16 * (q + 1))
                for ft in range(8):
                    gi = 8 * q + ft
                    if gi >= 2:
                        b.wait(t, V + "gx", gi - 1)
                    pr = [psA[0], psA[1]] if gi % 2 == 0 else [psA[2], psA[3]]
                    for h in range(2):
                        for dt in range(ND):
                            ii = t.matmul(pr[h][:, 0:512],
                                          lhsT=w1buf[:, dt * D + ft * P:dt * D + (ft + 1) * P],
                                          rhs=sel_fm[:, dt * D + h * 512:dt * D + (h + 1) * 512],
                                          start=(dt == 0), stop=(dt == ND - 1))
                    b.inc(ii, V + "h1_mm", 1)
                b.wait(t, "md_ldw2", 16 * (q + 1))
                b.wait(t, V + "gelu", 8 * (q + 1))
                for rt in range(8):
                    gi2 = 8 * q + rt
                    if gi2 >= 2:
                        b.wait(t, V + "h2_ev", 2 * (gi2 - 1))
                    ph2 = ([psA[4][:, 0:512], psA[5][:, 0:512]] if rt % 2 == 0
                           else [psR[:, 0:512], psR[:, 512:1024]])
                    for h in range(2):
                        for ff in range(8):
                            ii = t.matmul(ph2[h],
                                          lhsT=h1q[:, ff * D + rt * P:ff * D + (rt + 1) * P],
                                          rhs=w2buf[:, ff * D + h * 512:ff * D + (h + 1) * 512],
                                          start=(ff == 0), stop=(ff == 7))
                    b.inc(ii, V + "h2_mm", 1)

        @blk.scalar
        def _(s):
            b.wait(s, "ld_fb1", 16)
            for q in range(NQ4):
                for ft in range(8):
                    gi = 8 * q + ft
                    par = gi % 2
                    b.wait(s, V + "h1_mm", gi + 1)
                    if gi >= 2:
                        b.wait(s, V + "gelu", gi - 1)
                    pr = [psA[0], psA[1]] if par == 0 else [psA[2], psA[3]]
                    b1c = fb1_s[:, q * 8 + ft:q * 8 + ft + 1]
                    s.activation(out=xs_d[par][:, 0:512], in_=pr[0][:, 0:512],
                                 func=AF.Identity, bias=b1c)
                    s.activation(out=xs_d[par][:, 512:1024], in_=pr[1][:, 0:512],
                                 func=AF.Identity, bias=b1c)
                    s.activation(out=sq_d[par][:, 0:512], in_=pr[0][:, 0:512],
                                 func=AF.Square, bias=b1c)
                    ii = s.activation(out=sq_d[par][:, 512:1024], in_=pr[1][:, 0:512],
                                      func=AF.Square, bias=b1c)
                    b.inc(ii, V + "gx", 1)
                    b.wait(s, V + "gu2", gi + 1)
                    ii = s.activation(out=u2_d[par][:, :], in_=u2_d[par][:, :],
                                      func=AF.Sigmoid)
                    b.inc(ii, V + "gsg", 1)

        @blk.vector
        def _(v):
            for q in range(NQ4):
                for ft in range(8):
                    gi = 8 * q + ft
                    par = gi % 2
                    b.wait(v, V + "gx", gi + 1)
                    if gi >= 2:
                        b.wait(v, V + "gsg", gi - 1)
                    v.tensor_scalar(out=sq_d[par][:, :], in0=sq_d[par][:, :],
                                    scalar1=GELU_A * GELU_C, scalar2=GELU_A,
                                    op0=ALU.mult, op1=ALU.add)
                    v.drain()
                    b.inc(v.tensor_tensor(out=u2_d[par][:, :], in0=sq_d[par][:, :],
                                          in1=xs_d[par][:, :], op=ALU.mult), V + "gu2", 1)
                    b.wait(v, V + "gsg", gi + 1)
                    b.inc(v.tensor_tensor(out=h1q[:, ft * D:(ft + 1) * D],
                                          in0=u2_d[par][:, :], in1=xs_d[par][:, :],
                                          op=ALU.mult), V + "gelu", 1)
                for rt in range(8):
                    gi = 8 * q + rt
                    b.wait(v, V + "h2_mm", gi + 1)
                    par = rt % 2
                    u = (gi - par) // 2
                    if u >= 1:
                        b.wait(v, f"md_upda{par}", 16 * u)
                    ph2 = ([psA[4][:, 0:512], psA[5][:, 0:512]] if rt % 2 == 0
                           else [psR[:, 0:512], psR[:, 512:1024]])
                    if q == 0:
                        v.tensor_tensor(out=stage_scr[par][:, 0:512],
                                        in0=ph2[0], in1=b2_bc[:, 0:512],
                                        op=ALU.add)
                        ii = v.tensor_tensor(out=stage_scr[par][:, 512:1024],
                                             in0=ph2[1],
                                             in1=b2_bc[:, 512:1024], op=ALU.add)
                    else:
                        v.tensor_copy(stage_scr[par][:, 0:512], ph2[0])
                        ii = v.tensor_copy(stage_scr[par][:, 512:1024], ph2[1])
                    b.inc(ii, V + "h2_ev", 2)

        @blk.gpsimd
        def _(g):
            b.wait(g, "md_upda0")
            b.wait(g, "md_upda1")
            b.wait(g, V + "gz", 1)
            for tt in range(NT):
                if tt >= 4:
                    b.wait(g, V + "xadd", tt - 3)
                b.inc(g.indirect_dma_start(
                    out=gbuf[tt % 4], out_offset=None,
                    in_=updv, in_offset=bass.IndirectOffsetOnAxis(
                        idx_tok[:, tt:tt + 1], axis=0),
                    bounds_check=bc_reg, oob_is_err=False),
                    f"md_gat{tt % 4}", 16)

        @blk.vector
        def _(v):
            b.wait(v, V + "mw")
            b.wait(v, "md_upda0")
            b.wait(v, "md_upda1")
            v.drain()
            for s4 in range(4):
                ii = v.memset(gbuf[s4], 0.0)
            b.inc(ii, V + "gz", 1)
            v.drain()
            for tt in range(NT):
                b.wait(v, f"md_gat{tt % 4}", 16 * (tt // 4 + 1))
                v.drain()
                ii = v.scalar_tensor_tensor(
                    out=x_tok[:, tt * D:(tt + 1) * D],
                    in0=gbuf[tt % 4], scalar=msc_mw[:, tt:tt + 1],
                    in1=x_tok[:, tt * D:(tt + 1) * D],
                    op0=ALU.mult, op1=ALU.add)
                b.inc(ii, V + "xadd", 1)
                if post_tile is not None:
                    post_tile(v, tt)

    b.free("ld_c", "ld_wv", "ld_wk", "ld_wq", "ld_x", "wvs", "wvs_tr", "wvs_row",
           "swv0", "bc_mm", "bc_done", "own_var", "own_rstd", "own_nmr", "xln_em",
           "xln_tr", "xln_ev", "carry0", "y_var", "y_rstd", "y_ab", "y_gadd",
           "xb_fv", "xb_fs",
           "xb_ld0", "xb_ld1", "yacc0", "yacc1", "ybk_ld0", "ybk_ld1", "ytr_mm",
           "ytr_ev", "m_mm", "m_ev", "q_mm", "q_ev", "car_mm", "lr_mm",
           "lr_ev", "oa_mm")
    if N_SUB >= 2:
        emit_mod(0, post_tile=bn_tile_hook if N_SUB >= 3 else None)

    NCK = T // C2
    xfm16 = xln_fm
    S32 = m_tok.bitcast(F32)
    S16 = tmp_buf_.bitcast(BF16)
    xb16 = xb_buf_.bitcast(BF16)
    ktokP = [xfm16[:, 0:2048], xfm16[:, 2048:4096]]
    qP = [xfm16[:, 4096:6144], xfm16[:, 6144:8192]]
    vfmP = [xfm16[:, 8192:10240], xfm16[:, 10240:12288]]
    kfm_c = xfm16[:, 12288:14336]
    vtok_c = xfm16[:, 14336:16384]
    T2st = [xb16[:, 512:1024], xb16[:, 7168:7680]]
    A2st = [xb16[:, 1024:1536], xb16[:, 6656:7168]]
    Pnat = [xb16[:, 1536:2048], xb16[:, 3072:3584]]
    PnatT = [xb16[:, 2048:2560], xb16[:, 3584:4096]]
    Mst = [xb16[:, 2560:3072], xb16[:, 6144:6656]]
    kfmS = [xfm16[:, 0:2048], xfm16[:, 2048:4096]]
    ktokS = xfm16[:, 4096:6144]
    qS = xfm16[:, 6144:8192]
    vtokS = xfm16[:, 8192:10240]
    W_c = xfm16[:, 10240:12288]
    O_c = xfm16[:, 12288:14336]
    U_c = xfm16[:, 14336:16384]
    Mld = [xb16[:, 4096:4608], xb16[:, 4608:5120]]
    T2ld = [xb16[:, 5120:5632], xb16[:, 5632:6144]]
    av_scr = xb_buf_[:, 0:2048]
    btw16 = xb16[:, 8100:8108]
    bbc_scr = xb_buf_[:, 2048:4096]
    qkst = [m_tok[:, 0:2048], m_tok[:, 2048:4096]]
    bstat = yst["vsum"]
    ab_beta = bstat[:, 0:NT]
    ab_eb = bstat[:, 16:24]
    ab_asig = bstat[:, 24:32]
    ab_rstd = bstat[:, 32:48]
    ab_ms = bstat[:, 48:64]
    ab_ss = bstat[:, 64:96]
    ab_bb = bstat[:, 96:97]
    ab_q2 = bstat[:, 97:113]
    ab_nb = bstat[:, 113:129] if False else bstat[:, 0:0]
    ab_nb = yag[:, 32:48]
    ab_s1 = yag[:, 48:64]
    ab_e1 = yag[:, 64:80]
    n1w = yag[:, 0:8]
    n2w = yag[:, 8:16]
    alph = yag[:, 16:24]
    btw = yag[:, 24:32]

    def scale_rows_by(v, buf, gcol, nblk, width, sem):
        for dt in range(nblk):
            ii = v.tensor_scalar(out=buf[:, dt * width:(dt + 1) * width],
                                 in0=buf[:, dt * width:(dt + 1) * width],
                                 scalar1=gcol[:, dt:dt + 1], scalar2=None,
                                 op0=ALU.mult)
        v.drain()
        b.inc(v.tensor_copy(buf[0:1, 0:1], buf[0:1, 0:1]), sem, 1)

    def emit_attnb():
        V = "ab_"

        @blk.gpsimd
        def _(g):
            b.wait(g, "v0_xadd", b.n("v0_xadd"))
            b.inc(g.dma_start(out=n1w, in_=wD["b_n1"]), V + "vecs", 16)
            b.inc(g.dma_start(out=n2w, in_=wD["b_n2"]), V + "vecs", 16)
            b.inc(g.dma_start(out=alph, in_=wD["b_alpha"]), V + "vecs", 16)
            b.inc(g.dma_start(out=btw, in_=wD["b_betaw"]), V + "vecs", 16)
            b.inc(g.dma_start(out=swv_tmp[0:1, 0:1], in_=wD["b_betab"]), V + "vecs", 16)
            b.inc(g.dma_start(
                out=wbufA[:, :].rearrange("p (k n) -> p k n", k=ND),
                in_=wD["bWq"].rearrange("(k p) n -> p k n", p=P)), V + "wq_ld", 16)

        @blk.vector
        def _(v):
            b.wait(v, "v0_xadd", b.n("v0_xadd"))
            v.drain()
            for tt in range(NT):
                ii = v.bn_aggr(ag2[:, 2 * tt:2 * tt + 2], bn12[:, 12 * tt:12 * tt + 12])
            v.drain()
            b.inc(v.bn_aggr(yag[:, 254:256] if False else ag2[:, 0:2],
                            bn12[:, 0:12]) if False else
                  v.tensor_copy(ag2[0:1, 0:1], ag2[0:1, 0:1]), V + "ms", 1)
            b.wait(v, V + "vecs", 80)
            b.wait(v, V + "wq_ld", 16)
            scale_rows_by(v, wbufA, n1w, ND, D, V + "wq")

        @blk.scalar
        def _(s):
            b.wait(s, V + "ms")
            s.activation(out=ab_ms[:, :], in_=ag2[:, 1::2], func=AF.Ln,
                         bias=cS["eps_ln"][:, :])
            s.drain()
            s.activation(out=ab_s1[:, :], in_=ab_ms[:, :], func=AF.Exp, scale=-0.5)
            ii = s.activation(out=ab_e1[:, :], in_=ab_ms[:, :], func=AF.Exp, scale=-1.0)
            b.inc(ii, V + "s1", 1)
            b.wait(s, V + "q2v", 1)
            s.drain()
            s.activation(out=ab_q2[:, :], in_=ab_q2[:, :], func=AF.Ln,
                         bias=cS["eps_rms"][:, :])
            s.drain()
            b.inc(s.activation(out=ab_rstd[:, :], in_=ab_q2[:, :], func=AF.Exp,
                               scale=-0.5), V + "rstd", 1)
            b.wait(s, V + "vecs", 80)
            b.inc(s.activation(out=ab_asig[:, :], in_=alph, func=AF.Sigmoid),
                  V + "asig", 1)
            b.wait(s, V + "rstd")
            b.wait(s, V + "q2v", 1)
            s.drain()
            asc = yag[:, 80:96]
            bsc = yag[:, 96:112]
            for tt in range(NT):
                s.activation(out=asc[:, tt:tt + 1], in_=ab_s1[:, tt:tt + 1],
                             func=AF.Identity, scale=ab_rstd[:, tt:tt + 1])
                s.activation(out=bsc[:, tt:tt + 1], in_=ab_nb[:, tt:tt + 1],
                             func=AF.Identity, scale=ab_rstd[:, tt:tt + 1])
            s.drain()
            for tt in range(NT):
                if tt >= 2:
                    b.wait(s, V + "hev", 2 * (tt - 1))
                b.wait(s, "v0_xadd", tt + 1)
                b.inc(s.activation(out=xln_rot[tt % 2], in_=x_tok[:, tt * D:(tt + 1) * D],
                                   func=AF.Identity, scale=asc[:, tt:tt + 1],
                                   bias=bsc[:, tt:tt + 1]),
                      V + "hem", 1)

        @blk.gpsimd
        def _(g):
            xv_ = x_tok[:, :].rearrange("p (tt d) -> p tt d", tt=NT)
            yv_ = y_dram.ap().rearrange("(tt p) d -> p tt d", p=P)
            for j in range(4):
                b.wait(g, "v0_xadd", 4 * (j + 1))
                if j >= 1:
                    b.wait(g, V + "park", 16 * j)
                b.inc(g.dma_start(out=yv_[:, 4 * j:4 * j + 4, :],
                                  in_=xv_[:, 4 * j:4 * j + 4, :]),
                      V + "park", 16)

        @blk.vector
        def _(v):
            b.wait(v, V + "s1", 1)
            v.drain()
            v.tensor_tensor(out=ab_q2[:, :], in0=ag2[:, 1::2], in1=ab_e1[:, :],
                            op=ALU.mult)
            v.tensor_tensor(out=ab_nb[:, :], in0=ag2[:, 0::2], in1=ab_s1[:, :],
                            op=ALU.mult)
            v.drain()
            ii = v.tensor_scalar(out=ab_nb[:, :], in0=ab_nb[:, :], scalar1=-1.0,
                                 scalar2=None, op0=ALU.mult)
            b.inc(ii, V + "q2v", 1)
            v.drain()

        @blk.vector
        def _(v):
            b.wait(v, V + "asig")
            v.drain()
            b.inc(v.tensor_scalar(out=ab_eb[:, :], in0=ab_asig[:, :], scalar1=-1.0,
                                  scalar2=1.0, op0=ALU.mult, op1=ALU.add), V + "eb", 1)

        @blk.tensor
        def _(t):
            for tt in range(NT):
                b.wait(t, V + "hem", tt + 1)
                if tt >= 1:
                    b.wait(t, V + "hev", 2 * tt)
                for dt in range(ND):
                    ii = t.matmul(psA[4 + dt // 4][:, (dt % 4) * P:(dt % 4 + 1) * P],
                                  lhsT=xln_rot[tt % 2][:, dt * P:(dt + 1) * P],
                                  rhs=cS["ident16"][:, :], start=True, stop=True)
                b.inc(ii, V + "htr", 1)

        @blk.vector
        def _(v):
            hfm = xfm16[:, :].rearrange("p (dt t) -> p dt t", dt=ND)
            for tt in range(NT):
                b.wait(v, V + "htr", tt + 1)
                for h in range(2):
                    b.inc(v.tensor_copy(
                        hfm[:, 4 * h:4 * h + 4, tt * P:(tt + 1) * P],
                        psA[4 + h][:, 0:512].rearrange("p (dt t) -> p dt t", dt=4)),
                        V + "hev", 1)
                b.wait(v, V + "park", 16 * (tt // 4 + 1))
                if tt == 0:
                    v.drain()
                b.inc(v.tensor_scalar(out=x_tok[:, tt * D:(tt + 1) * D],
                                      in0=x_tok[:, tt * D:(tt + 1) * D],
                                      scalar1=ab_s1[:, tt:tt + 1],
                                      scalar2=ab_nb[:, tt:tt + 1],
                                      op0=ALU.mult, op1=ALU.add), V + "xl", 1)

        @blk.tensor
        def _(t):
            b.wait(t, V + "hev", 2 * NT)
            b.wait(t, V + "wq")
            for k in range(ND):
                if k >= 1:
                    b.wait(t, V + "qev", k)
                if k >= 2:
                    b.wait(t, V + f"qst{k % 2}", 16 * ((k - 2) // 2 + 1))
                for nq in range(4):
                    for dt in range(ND):
                        ii = t.matmul(psA[nq][:, 0:512],
                                      lhsT=wbufA[:, dt * D + k * P: dt * D + (k + 1) * P],
                                      rhs=xfm16[:, dt * T + nq * 512: dt * T + (nq + 1) * 512],
                                      start=(dt == 0), stop=(dt == ND - 1))
                b.inc(ii, V + "qmm", 1)

        @blk.vector
        def _(v):
            for k in range(ND):
                b.wait(v, V + "qmm", k + 1)
                for nq in range(4):
                    ii = v.tensor_copy(qkst[k % 2][:, nq * 512:(nq + 1) * 512],
                                       psA[nq][:, 0:512])
                b.inc(ii, V + "qev", 1)

        @blk.sync
        def _(s):
            for k in range(ND):
                b.wait(s, V + "qev", k + 1)
                b.inc(s.dma_start(out=q_dram[k * P:(k + 1) * P, :],
                                  in_=qkst[k % 2]), V + f"qst{k % 2}", 16)

        @blk.gpsimd
        def _(g):
            b.wait(g, V + "qmm", ND)
            b.inc(g.dma_start(
                out=wbufA[:, :].rearrange("p (k n) -> p k n", k=ND),
                in_=wD["bWk"].rearrange("(k p) n -> p k n", p=P)), V + "wk_ld", 16)

        @blk.vector
        def _(v):
            b.wait(v, V + "wk_ld", 16)
            b.wait(v, V + "qev", ND)
            scale_rows_by(v, wbufA, n1w, ND, D, V + "wk")

        @blk.tensor
        def _(t):
            b.wait(t, V + "wk")
            for tt in range(NT):
                if tt >= 1:
                    b.wait(t, V + "kev", tt)
                if tt >= 2:
                    b.wait(t, V + f"kst{tt % 2}", 16 * ((tt - 2) // 2 + 1))
                for h in range(2):
                    for dt in range(ND):
                        ii = t.matmul(psA[h][:, 0:512],
                                      lhsT=xfm16[:, dt * T + tt * P:dt * T + (tt + 1) * P],
                                      rhs=wbufA[:, dt * D + h * 512:dt * D + (h + 1) * 512],
                                      start=(dt == 0), stop=(dt == ND - 1))
                b.inc(ii, V + "kmm", 1)

        @blk.scalar
        def _(s):
            for tt in range(NT):
                b.wait(s, V + "kmm", tt + 1)
                s.activation(out=av_scr[:, 0:512], in_=psA[0][:, 0:512], func=AF.Square,
                             accum_out=ab_ss[:, 2 * tt:2 * tt + 1])
                ii = s.activation(out=av_scr[:, 512:1024], in_=psA[1][:, 0:512],
                                  func=AF.Square, accum_out=ab_ss[:, 2 * tt + 1:2 * tt + 2])
                b.inc(ii, V + "kss", 1)
                b.wait(s, V + "kssm", tt + 1)
                s.drain()
                s.activation(out=ab_ss[:, 2 * tt:2 * tt + 1],
                             in_=ab_ss[:, 2 * tt:2 * tt + 1], func=AF.Ln)
                s.drain()
                ii = s.activation(out=ab_ss[:, 2 * tt:2 * tt + 1],
                                  in_=ab_ss[:, 2 * tt:2 * tt + 1], func=AF.Exp,
                                  scale=-0.5)
                b.inc(ii, V + "krec", 1)

        @blk.vector
        def _(v):
            b.wait(v, V + "qst0")
            b.wait(v, V + "qst1")
            for tt in range(NT):
                b.wait(v, V + "kss", tt + 1)
                v.drain()
                v.tensor_tensor(out=ab_ss[:, 2 * tt:2 * tt + 1],
                                in0=ab_ss[:, 2 * tt:2 * tt + 1],
                                in1=ab_ss[:, 2 * tt + 1:2 * tt + 2], op=ALU.add)
                v.drain()
                b.inc(v.tensor_scalar(out=ab_ss[:, 2 * tt:2 * tt + 1],
                                      in0=ab_ss[:, 2 * tt:2 * tt + 1],
                                      scalar1=1e-24, scalar2=None, op0=ALU.max),
                      V + "kssm", 1)
                b.wait(v, V + "krec", tt + 1)
                v.drain()
                for h in range(2):
                    ii = v.tensor_scalar(
                        out=qkst[tt % 2][:, h * 512:(h + 1) * 512],
                        in0=psA[h][:, 0:512], scalar1=ab_ss[:, 2 * tt:2 * tt + 1],
                        scalar2=None, op0=ALU.mult)
                b.inc(ii, V + "kev", 1)

        @blk.sync
        def _(s):
            for tt in range(NT):
                b.wait(s, V + "kev", tt + 1)
                b.inc(s.dma_start(out=k_dram[tt * P:(tt + 1) * P, :],
                                  in_=qkst[tt % 2][:, 0:1024]), V + f"kst{tt % 2}", 16)

        @blk.vector
        def _(v):
            b.wait(v, V + "vecs", 80)
            v.drain()
            b.inc(v.tensor_tensor(out=btw16, in0=btw, in1=n1w, op=ALU.mult), V + "btwn", 1)

        @blk.tensor
        def _(t):
            b.wait(t, V + "kev", NT)
            b.wait(t, V + "btwn")
            for tt in range(NT):
                for dt in range(ND):
                    ii = t.matmul(psA[2][:, tt:tt + 1],
                                  lhsT=xfm16[:, dt * T + tt * P:dt * T + (tt + 1) * P],
                                  rhs=btw16[:, dt:dt + 1],
                                  start=(dt == 0), stop=(dt == ND - 1))
            b.inc(ii, V + "bmm", 1)
            ii = t.matmul(psA[3][:, 0:1], lhsT=cS["ones32row"][:, :],
                          rhs=swv_tmp[0:1, 0:1], start=True, stop=True)
            b.inc(ii, V + "bbmm", 1)

        @blk.vector
        def _(v):
            b.wait(v, V + "bbmm")
            v.tensor_copy(ab_bb[:, 0:1], psA[3][:, 0:1])
            v.drain()
            ii = v.tensor_scalar(out=ab_beta[:, :], in0=psA[2][:, 0:NT],
                                 scalar1=ab_bb[:, 0:1], scalar2=None, op0=ALU.add)
            b.inc(ii, V + "bev", 1)

        @blk.scalar
        def _(s):
            b.wait(s, V + "bev")
            s.drain()
            b.inc(s.activation(out=ab_beta[:, :], in_=ab_beta[:, :], func=AF.Sigmoid),
                  V + "bsig", 1)

        @blk.gpsimd
        def _(g):
            b.wait(g, V + "kmm", NT)
            b.inc(g.dma_start(
                out=wbufA[:, :].rearrange("p (k n) -> p k n", k=ND),
                in_=wD["bWv"].rearrange("(k p) n -> p k n", p=P)), V + "wv_ld", 16)

        @blk.vector
        def _(v):
            b.wait(v, V + "wv_ld", 16)
            b.wait(v, V + "bev")
            scale_rows_by(v, wbufA, n1w, ND, D, V + "wv")

        @blk.tensor
        def _(t):
            b.wait(t, V + "wv")
            b.wait(t, V + "bsig")
            for k in range(ND):
                if k >= 1:
                    b.wait(t, V + "vscan", k)
                if k >= 2:
                    b.wait(t, V + f"vst{k % 2}", 16 * ((k - 2) // 2 + 1))
                for nq in range(4):
                    for dt in range(ND):
                        ii = t.matmul(psA[nq][:, 0:512],
                                      lhsT=wbufA[:, dt * D + k * P: dt * D + (k + 1) * P],
                                      rhs=xfm16[:, dt * T + nq * 512: dt * T + (nq + 1) * 512],
                                      start=(dt == 0), stop=(dt == ND - 1))
                b.inc(ii, V + "vmm", 1)

        @blk.vector
        def _(v):
            b.wait(v, V + "eb")
            b.wait(v, V + "kst0")
            b.wait(v, V + "kst1")
            for k in range(ND):
                b.wait(v, V + "vmm", k + 1)
                v.drain()
                v.memset(bbc_scr[:, :], 1.0)
                v.drain()
                v.tensor_scalar(out=bbc_scr[:, :], in0=bbc_scr[:, :],
                                scalar1=ab_eb[:, k:k + 1], scalar2=None, op0=ALU.mult)
                for nq in range(4):
                    v.tensor_scalar(out=av_scr[:, nq * 512:(nq + 1) * 512],
                                    in0=psA[nq][:, 0:512],
                                    scalar1=ab_asig[:, k:k + 1], scalar2=None,
                                    op0=ALU.mult)
                v.drain()
                ii = v.tensor_tensor_scan(out=qkst[k % 2], data0=bbc_scr[:, :],
                                          data1=av_scr[:, :], initial=0.0,
                                          op0=ALU.mult, op1=ALU.add)
                b.inc(ii, V + "vscan", 1)

        @blk.sync
        def _(s):
            for k in range(ND):
                b.wait(s, V + "vscan", k + 1)
                b.inc(s.dma_start(out=vema_dram[k * P:(k + 1) * P, :],
                                  in_=qkst[k % 2]), V + f"vst{k % 2}", 16)

        @blk.gpsimd
        def _(g):
            b.wait(g, V + "vmm", ND)
            b.inc(g.dma_start(
                out=wbufA[:, :].rearrange("p (k n) -> p k n", k=ND),
                in_=wD["bWo"].rearrange("(k p) n -> p k n", p=P)), V + "wo_ld", 16)

        b.free("ab_hem", "ab_htr", "ab_hev", "ab_qmm", "ab_qev", "ab_kmm",
               "ab_kss", "ab_kssm", "ab_krec", "ab_kev", "ab_bmm", "ab_bbmm",
               "ab_bev", "ab_btwn", "ab_bsig", "ab_vmm", "ab_vscan", "ab_ms",
               "ab_rstd", "ab_asig", "ab_eb", "ab_wq", "ab_wk", "ab_wv")
        NLEV = 6

        selv_b = sel_dram.ap()
        kvscr = kv_scr.ap()

        @blk.sync
        def _(s):
            b.wait(s, V + "qst0", 16 * (ND // 2))
            b.wait(s, V + "qst1", 16 * (ND // 2))
            b.wait(s, V + "kst0", 16 * (NT // 2))
            b.wait(s, V + "kst1", 16 * (NT // 2))
            b.wait(s, V + "vst0", 16 * (ND // 2))
            b.wait(s, V + "vst1", 16 * (ND // 2))
            for k in range(NCK):
                if k >= 2:
                    b.wait(s, V + "dKtr", 2 * (k - 1))
                    b.wait(s, V + "dGmm", 2 * (k - 1))
                    b.wait(s, V + "dVtr", 2 * (k - 1))
                if k >= 1:
                    b.wait(s, V + "dldk", 16 * k)
                    b.wait(s, V + "dldq", 16 * k)
                    b.wait(s, V + "dldv", 16 * k)
                b.inc(s.dma_start(
                    out=ktokP[k % 2].rearrange("p (tm d) -> p tm d", tm=2),
                    in_=k_dram[k * C2:(k + 1) * C2, :].rearrange("(tm p) d -> p tm d", p=P)),
                    V + "dldk", 16)
                b.inc(s.dma_start(
                    out=qP[k % 2].rearrange("p (dt t) -> p dt t", dt=ND),
                    in_=q_dram[:, k * C2:(k + 1) * C2].rearrange("(dt p) t -> p dt t", p=P)),
                    V + "dldq", 16)
                b.inc(s.dma_start(
                    out=vfmP[k % 2].rearrange("p (dt t) -> p dt t", dt=ND),
                    in_=vema_dram[:, k * C2:(k + 1) * C2].rearrange("(dt p) t -> p dt t", p=P)),
                    V + "dldv", 16)

        @blk.gpsimd
        def _(g):
            for k in range(NCK):
                b.wait(g, V + "dKtr_ev", 2 * (k + 1))
                if k >= 1:
                    b.wait(g, "ab_okf", 16 * k)
                    b.wait(g, "ab_ovt", 16 * k)
                    b.wait(g, "ab_ot2", 16 * k)
                b.inc(g.dma_start(out=kvscr[k * P:(k + 1) * P, :],
                                  in_=kfm_c[:, :]), "ab_okf", 16)
                b.wait(g, V + "dVtr_ev", 2 * (k + 1))
                b.inc(g.dma_start(out=kvscr[T + k * P:T + (k + 1) * P, :],
                                  in_=vtok_c[:, :]), "ab_ovt", 16)
                b.wait(g, V + "dT2", k + 1)
                b.inc(g.dma_start(out=selv_b[k * P:(k + 1) * P, 512:1024],
                                  in_=T2st[k % 2][:, :]), "ab_ot2", 16)
                if k >= 2:
                    b.wait(g, V + "dMdone", k - 1)
                    b.wait(g, "ab_omt", 16 * (k - 2))
                    b.inc(g.dma_start(out=selv_b[(k - 2) * P:(k - 1) * P, 0:512],
                                      in_=Mst[k % 2][:, :]), "ab_omt", 16)
            for j in (NCK - 2, NCK - 1):
                b.wait(g, V + "dMdone", j + 1)
                b.wait(g, "ab_omt", 16 * j)
                b.inc(g.dma_start(out=selv_b[j * P:(j + 1) * P, 0:512],
                                  in_=Mst[j % 2][:, :]), "ab_omt", 16)

        chA = [psA[2][:, 0:512], psA[5][:, 0:512]]
        chB = [psA[3][:, 0:512], psR[:, 0:512]]
        chC = [psA[4][:, 0:512], psR[:, 512:1024]]

        @blk.tensor
        def _(t):
            def lev_piece(j, lev):
                p = j % 2
                Lp = j // 2
                sA = str(p)
                if lev == 0:
                    b.wait(t, V + "dMl" + sA, 7 * Lp + 1)
                else:
                    b.wait(t, V + "dPev" + sA, NLEV * Lp + lev)
                    b.wait(t, V + "dPTev" + sA, NLEV * Lp + lev)
                for a_ in range(2):
                    for m_ in range(2):
                        ii = t.matmul(chA[p][:, a_ * C2:(a_ + 1) * C2],
                                      lhsT=PnatT[p][:, m_ * C2 + a_ * P:m_ * C2 + (a_ + 1) * P],
                                      rhs=Pnat[p][:, m_ * C2:(m_ + 1) * C2],
                                      start=(m_ == 0), stop=(m_ == 1))
                b.inc(ii, V + "dPmmA" + sA, 1)
                for a_ in range(2):
                    for m_ in range(2):
                        ii = t.matmul(chB[p][:, a_ * C2:(a_ + 1) * C2],
                                      lhsT=Pnat[p][:, m_ * C2 + a_ * P:m_ * C2 + (a_ + 1) * P],
                                      rhs=PnatT[p][:, m_ * C2:(m_ + 1) * C2],
                                      start=(m_ == 0), stop=(m_ == 1))
                b.inc(ii, V + "dPmmB" + sA, 1)
                b.wait(t, V + "dPev" + sA, NLEV * Lp + lev + 1)
                b.wait(t, V + "dMuev" + sA, NLEV * Lp + lev)
                for jm in range(2):
                    for m_ in range(2):
                        ii = t.matmul(chC[p][:, jm * C2:(jm + 1) * C2],
                                      lhsT=Pnat[p][:, m_ * C2 + jm * P:m_ * C2 + (jm + 1) * P],
                                      rhs=Mst[p][:, m_ * C2:(m_ + 1) * C2],
                                      start=(m_ == 0), stop=(m_ == 1))
                b.inc(ii, V + "dMu_mm" + sA, 1)

            for k in range(NCK + 2):
                PRE = k < NCK
                jA, jB = k - 1, k - 2

                def piece(pi, k=k):
                    kp = k % 2
                    if pi == 0:
                        b.wait(t, V + "dldk", 16 * (k + 1))
                        if k >= 1:
                            b.wait(t, V + "dMl" + str((k - 1) % 2),
                                   7 * ((k - 1) // 2) + 1)
                        for dt in range(ND):
                            ii = t.matmul(psA[dt // 4][:, (dt % 4) * P:(dt % 4 + 1) * P],
                                          lhsT=ktokP[kp][:, dt * P:(dt + 1) * P],
                                          rhs=cS["ident16"][:, :], start=True, stop=True)
                        b.inc(ii, V + "dKtr", 1)
                    elif pi == 1:
                        b.wait(t, V + "dKtr_ev", 2 * k + 1)
                        for dt in range(ND):
                            ii = t.matmul(psA[dt // 4][:, (dt % 4) * P:(dt % 4 + 1) * P],
                                          lhsT=ktokP[kp][:, 1024 + dt * P:1024 + (dt + 1) * P],
                                          rhs=cS["ident16"][:, :], start=True, stop=True)
                        b.inc(ii, V + "dKtr", 1)
                    elif pi == 2:
                        b.wait(t, V + "dldv", 16 * (k + 1))
                        b.wait(t, V + "dKtr_ev", 2 * (k + 1))
                        for dt in range(ND):
                            ii = t.matmul(psA[dt // 4][:, (dt % 4) * P:(dt % 4 + 1) * P],
                                          lhsT=vfmP[kp][:, dt * C2:dt * C2 + P],
                                          rhs=cS["ident16"][:, :], start=True, stop=True)
                        b.inc(ii, V + "dVtr", 1)
                    elif pi == 3:
                        b.wait(t, V + "dVtr_ev", 2 * k + 1)
                        for dt in range(ND):
                            ii = t.matmul(psA[dt // 4][:, (dt % 4) * P:(dt % 4 + 1) * P],
                                          lhsT=vfmP[kp][:, dt * C2 + P:(dt + 1) * C2],
                                          rhs=cS["ident16"][:, :], start=True, stop=True)
                        b.inc(ii, V + "dVtr", 1)
                    elif pi == 4:
                        b.wait(t, V + "dVtr_ev", 2 * (k + 1))
                        for jm in range(2):
                            for dt in range(ND):
                                ii = t.matmul(psA[0][:, jm * C2:(jm + 1) * C2],
                                              lhsT=kfm_c[:, dt * C2 + jm * P:dt * C2 + (jm + 1) * P],
                                              rhs=kfm_c[:, dt * C2:(dt + 1) * C2],
                                              start=(dt == 0), stop=(dt == ND - 1))
                        b.inc(ii, V + "dGmm", 1)
                    elif pi == 5:
                        b.wait(t, V + "dldq", 16 * (k + 1))
                        for jm in range(2):
                            for dt in range(ND):
                                ii = t.matmul(psA[1][:, jm * C2:(jm + 1) * C2],
                                              lhsT=kfm_c[:, dt * C2 + jm * P:dt * C2 + (jm + 1) * P],
                                              rhs=qP[k % 2][:, dt * C2:(dt + 1) * C2],
                                              start=(dt == 0), stop=(dt == ND - 1))
                        b.inc(ii, V + "dGmm", 1)
                    else:
                        b.wait(t, V + "dA2", k + 1)
                        for a_ in range(2):
                            for b_ in range(2):
                                ii = t.matmul(psA[0][:, (2 * a_ + b_) * P:(2 * a_ + b_ + 1) * P],
                                              lhsT=A2st[k % 2][:, b_ * C2 + a_ * P:b_ * C2 + (a_ + 1) * P],
                                              rhs=cS["ident16"][:, :], start=True, stop=True)
                        b.inc(ii, V + "dAtr", 1)

                sched = [("p", 0), ("A", 0), ("p", 1), ("B", 3), ("p", 2),
                         ("A", 1), ("p", 3), ("B", 4), ("p", 4), ("A", 2),
                         ("p", 6), ("p", 5), ("B", 5)]
                for kind, idx in sched:
                    if kind == "p" and PRE:
                        piece(idx)
                    elif kind == "A" and 0 <= jA < NCK:
                        lev_piece(jA, idx)
                    elif kind == "B" and 0 <= jB < NCK:
                        lev_piece(jB, idx)

        @blk.scalar
        def _(s):
            kv = kfm_c.rearrange("p (dt tm t) -> p dt tm t", dt=ND, tm=2)
            vv = vtok_c.rearrange("p (tm dt t) -> p tm dt t", tm=2, dt=ND)
            for k in range(NCK):
                b.wait(s, V + "dKtr", 2 * k + 1)
                if k >= 1:
                    b.wait(s, V + "dGmm", 2 * k)
                    b.wait(s, "ab_okf", 16 * k)
                s.activation(out=kv[:, 0:4, 0, :],
                             in_=psA[0][:, 0:512].rearrange("p (dt t) -> p dt t", dt=4),
                             func=AF.Identity)
                ii = s.activation(out=kv[:, 4:8, 0, :],
                                  in_=psA[1][:, 0:512].rearrange("p (dt t) -> p dt t", dt=4),
                                  func=AF.Identity)
                b.inc(ii, V + "dKtr_ev", 1)
                b.wait(s, V + "dKtr", 2 * (k + 1))
                s.activation(out=kv[:, 0:4, 1, :],
                             in_=psA[0][:, 0:512].rearrange("p (dt t) -> p dt t", dt=4),
                             func=AF.Identity)
                ii = s.activation(out=kv[:, 4:8, 1, :],
                                  in_=psA[1][:, 0:512].rearrange("p (dt t) -> p dt t", dt=4),
                                  func=AF.Identity)
                b.inc(ii, V + "dKtr_ev", 1)
                b.wait(s, V + "dVtr", 2 * k + 1)
                if k >= 1:
                    b.wait(s, "ab_ovt", 16 * k)
                s.activation(out=vv[:, 0, 0:4, :],
                             in_=psA[0][:, 0:512].rearrange("p (dt t) -> p dt t", dt=4),
                             func=AF.Identity)
                ii = s.activation(out=vv[:, 0, 4:8, :],
                                  in_=psA[1][:, 0:512].rearrange("p (dt t) -> p dt t", dt=4),
                                  func=AF.Identity)
                b.inc(ii, V + "dVtr_ev", 1)
                b.wait(s, V + "dVtr", 2 * (k + 1))
                s.activation(out=vv[:, 1, 0:4, :],
                             in_=psA[0][:, 0:512].rearrange("p (dt t) -> p dt t", dt=4),
                             func=AF.Identity)
                ii = s.activation(out=vv[:, 1, 4:8, :],
                                  in_=psA[1][:, 0:512].rearrange("p (dt t) -> p dt t", dt=4),
                                  func=AF.Identity)
                b.inc(ii, V + "dVtr_ev", 1)

        @blk.vector
        def _(v):
            kv = kfm_c.rearrange("p (dt tm t) -> p dt tm t", dt=ND, tm=2)
            vv = vtok_c.rearrange("p (tm dt t) -> p tm dt t", tm=2, dt=ND)

            def vlev(j, lev):
                p = j % 2
                Lp = j // 2
                sA = str(p)
                b.wait(v, V + "dPmmA" + sA, NLEV * Lp + lev + 1)
                b.wait(v, V + "dPmmB" + sA, NLEV * Lp + lev + 1)
                v.drain()
                b.inc(v.tensor_copy(Pnat[p][:, :], chA[p]), V + "dPev" + sA, 1)
                b.inc(v.tensor_copy(PnatT[p][:, :], chB[p]), V + "dPTev" + sA, 1)
                b.wait(v, V + "dMu_mm" + sA, NLEV * Lp + lev + 1)
                v.drain()
                v.tensor_tensor(out=Mst[p][:, 0:C2], in0=chC[p][:, 0:C2],
                                in1=Mst[p][:, 0:C2], op=ALU.add)
                ii = v.tensor_tensor(out=Mst[p][:, C2:2 * C2], in0=chC[p][:, C2:2 * C2],
                                     in1=Mst[p][:, C2:2 * C2], op=ALU.add)
                b.inc(ii, V + "dMuev" + sA, 1)
                v.drain()
                b.inc(v.tensor_copy(Mst[p][0:1, 0:1], Mst[p][0:1, 0:1]),
                      V + "dMl" + sA, 1)
                v.drain()
                if lev == NLEV - 1:
                    b.inc(v.tensor_copy(Mst[p][0:1, 0:1], Mst[p][0:1, 0:1]),
                          V + "dMdone", 1)

            for k in range(NCK + 2):
                PRE = k < NCK
                jA, jB = k - 1, k - 2

                def vpiece(pi, k=k):
                    kp = k % 2
                    if pi in (0, 1, 2, 3):
                        pass
                    elif pi == 4:
                        b.wait(v, V + "dGmm", 2 * k + 1)
                        if k >= 1:
                            b.wait(v, V + "dAtr", k)
                        v.drain()
                        b0 = ab_beta[:, 2 * k:2 * k + 1]
                        b1 = ab_beta[:, 2 * k + 1:2 * k + 2]
                        A2c = A2st[kp]
                        v.scalar_tensor_tensor(out=A2c[:, 0:P], in0=psA[0][:, 0:P],
                                               scalar=b0, in1=cS["mask_jlt"][:, :],
                                               op0=ALU.mult, op1=ALU.mult)
                        v.tensor_scalar(out=A2c[:, P:C2], in0=psA[0][:, P:C2],
                                        scalar1=b0, scalar2=None, op0=ALU.mult)
                        v.memset(A2c[:, C2:C2 + P], 0.0)
                        v.scalar_tensor_tensor(out=A2c[:, C2 + P:2 * C2],
                                               in0=psA[0][:, C2 + P:2 * C2],
                                               scalar=b1, in1=cS["mask_jlt"][:, :],
                                               op0=ALU.mult, op1=ALU.mult)
                        v.drain()
                        b.inc(v.tensor_copy(A2c[0:1, 0:1], A2c[0:1, 0:1]),
                              V + "dA2", 1)
                    elif pi == 5:
                        b.wait(v, V + "dGmm", 2 * (k + 1))
                        if k >= 1:
                            b.wait(v, "ab_ot2", 16 * k)
                        v.drain()
                        T2c = T2st[kp]
                        v.tensor_tensor(out=T2c[:, 0:P], in0=psA[1][:, 0:P],
                                        in1=cS["mask_jle"][:, :], op=ALU.mult)
                        v.tensor_copy(T2c[:, P:C2], psA[1][:, P:C2])
                        v.memset(T2c[:, C2:C2 + P], 0.0)
                        v.tensor_tensor(out=T2c[:, C2 + P:2 * C2],
                                        in0=psA[1][:, C2 + P:2 * C2],
                                        in1=cS["mask_jle"][:, :], op=ALU.mult)
                        v.drain()
                        b.inc(v.tensor_copy(T2c[0:1, 0:1], T2c[0:1, 0:1]),
                              V + "dT2", 1)
                    else:
                        b.wait(v, V + "dAtr", k + 1)
                        if k >= 2:
                            b.wait(v, "ab_omt", 16 * (k - 1))
                        v.drain()
                        A2c = A2st[kp]
                        v.tensor_copy(Pnat[kp][:, :], psA[0][:, 0:512])
                        v.tensor_copy(PnatT[kp][:, :], A2c[:, :])
                        v.tensor_tensor(out=Mst[kp][:, 0:P], in0=cS["ident16"][:, :],
                                        in1=A2c[:, 0:P], op=ALU.subtract)
                        v.tensor_scalar(out=Mst[kp][:, P:C2], in0=A2c[:, P:C2],
                                        scalar1=-1.0, scalar2=None, op0=ALU.mult)
                        v.memset(Mst[kp][:, C2:C2 + P], 0.0)
                        v.tensor_tensor(out=Mst[kp][:, C2 + P:2 * C2],
                                        in0=cS["ident16"][:, :],
                                        in1=A2c[:, C2 + P:2 * C2], op=ALU.subtract)
                        v.drain()
                        b.inc(v.tensor_copy(Mst[kp][0:1, 0:1], Mst[kp][0:1, 0:1]),
                              V + "dMl" + str(kp), 1)

                sched = [("p", 0), ("A", 0), ("p", 1), ("B", 3), ("p", 2),
                         ("A", 1), ("p", 3), ("B", 4), ("p", 4), ("A", 2),
                         ("p", 5), ("B", 5), ("p", 6)]
                for kind, idx in sched:
                    if kind == "p" and PRE:
                        vpiece(idx)
                    elif kind == "A" and 0 <= jA < NCK:
                        vlev(jA, idx)
                    elif kind == "B" and 0 <= jB < NCK:
                        vlev(jB, idx)

        @blk.sync
        def _(s):
            b.wait(s, V + "dKtr", 2 * NCK)
            b.wait(s, V + "dVtr", 2 * NCK)
            b.wait(s, V + "dGmm", 2 * NCK)
            b.wait(s, V + "dAtr", NCK)
            b.wait(s, "ab_okf", 16 * NCK)
            b.wait(s, "ab_ovt", 16 * NCK)
            b.wait(s, "ab_omt", 16 * NCK)
            b.wait(s, "ab_ot2", 16 * NCK)
            for k in range(NCK):
                if k >= 3:
                    b.wait(s, V + "dKS_mm", k - 2)
                if k >= 2:
                    b.wait(s, V + "dU_mm", 4 * (k - 1))
                    b.wait(s, V + "dO_mm", 2 * (k - 1))
                if k >= 1:
                    b.wait(s, V + "dS_mm", 16 * k)
                    b.wait(s, V + "dO_mm", 2 * k)
                    b.wait(s, V + "dWev", 2 * k)
                    for nm in ("ab_ikf", "ab_imt", "ab_it2", "ab_ikt",
                               "ab_iq", "ab_ivt"):
                        b.wait(s, nm, 16 * k)
                b.inc(s.dma_start(out=kfmS[k % 2],
                                  in_=kvscr[k * P:(k + 1) * P, :]), "ab_ikf", 16)
                b.inc(s.dma_start(out=Mld[k % 2],
                                  in_=selv_b[k * P:(k + 1) * P, 0:512]), "ab_imt", 16)
                b.inc(s.dma_start(out=T2ld[k % 2],
                                  in_=selv_b[k * P:(k + 1) * P, 512:1024]), "ab_it2", 16)
                b.inc(s.dma_start(
                    out=ktokS.rearrange("p (tm d) -> p tm d", tm=2),
                    in_=k_dram[k * C2:(k + 1) * C2, :].rearrange("(tm p) d -> p tm d", p=P)),
                    "ab_ikt", 16)
                b.inc(s.dma_start(
                    out=qS.rearrange("p (dt t) -> p dt t", dt=ND),
                    in_=q_dram[:, k * C2:(k + 1) * C2].rearrange("(dt p) t -> p dt t", p=P)),
                    "ab_iq", 16)
                b.inc(s.dma_start(out=vtokS,
                                  in_=kvscr[T + k * P:T + (k + 1) * P, :]), "ab_ivt", 16)

        b.free("ab_dKtr", "ab_dKtr_ev", "ab_dVtr", "ab_dVtr_ev", "ab_dGmm",
               "ab_dA2", "ab_dT2", "ab_dAtr", "ab_dMdone", "ab_dMl0", "ab_dMl1",
               "ab_dPmmA0", "ab_dPmmA1", "ab_dPmmB0", "ab_dPmmB1",
               "ab_dMu_mm0", "ab_dMu_mm1", "ab_dPev0", "ab_dPev1",
               "ab_dPTev0", "ab_dPTev1", "ab_dMuev0", "ab_dMuev1")

        @blk.tensor
        def _(t):
            b.wait(t, V + "wo_ld", 16)
            for k in range(NCK):
                first = (k == 0)
                b.wait(t, "ab_ikf", 16 * (k + 1))
                b.wait(t, "ab_imt", 16 * (k + 1))
                if not first:
                    b.wait(t, V + "dS16", 16 * k)
                    b.wait(t, V + "dOev", 2 * k)
                    b.wait(t, V + "dWo_ev", 4 * k)
                    for tm in range(2):
                        for nh in range(2):
                            for dkt in range(ND):
                                ii = t.matmul(psA[2 * tm + nh][:, 0:512],
                                              lhsT=kfmS[k % 2][:, dkt * C2 + tm * P:dkt * C2 + (tm + 1) * P],
                                              rhs=S16[:, dkt * 1024 + nh * 512:dkt * 1024 + (nh + 1) * 512],
                                              start=(dkt == 0), stop=(dkt == ND - 1))
                    b.inc(ii, V + "dKS_mm", 1)
                b.wait(t, V + "dWev", 2 * (k + 1))
                if not first:
                    b.wait(t, V + "dSev", 16 * k)
                for im in range(2):
                    for nh in range(2):
                        for jm in range(2):
                            ii = t.matmul(psA[2 + 2 * im + nh][:, 0:512],
                                          lhsT=Mld[k % 2][:, jm * C2 + im * P:jm * C2 + (im + 1) * P],
                                          rhs=W_c[:, jm * 1024 + nh * 512:jm * 1024 + (nh + 1) * 512],
                                          start=(jm == 0), stop=(jm == 1))
                        b.inc(ii, V + "dU_mm", 1)
                b.wait(t, V + "dUev", 4 * (k + 1))
                b.wait(t, "ab_ikt", 16 * (k + 1))
                for grp in range(4):
                    b.wait(t, V + "dSev", 16 * k + 4 * grp)
                    for gg in range(4):
                        dkt = grp * 2 + gg // 2
                        nh = gg % 2
                        for tm in range(2):
                            ii = t.matmul(psA[2 + gg][:, 0:512],
                                          lhsT=ktokS[:, tm * 1024 + dkt * P:tm * 1024 + (dkt + 1) * P],
                                          rhs=U_c[:, tm * 1024 + nh * 512:tm * 1024 + (nh + 1) * 512],
                                          start=(tm == 0), stop=(tm == 1))
                    b.inc(ii, V + "dS_mm", 4)
                b.wait(t, "ab_iq", 16 * (k + 1))
                b.wait(t, "ab_it2", 16 * (k + 1))
                b.wait(t, V + "dSev", 16 * (k + 1))
                for half in range(2):
                    if k > 0 or half > 0:
                        b.wait(t, V + "dOev", 2 * k + half)
                    for dvq in range(4):
                        dv = 4 * half + dvq
                        if not first:
                            for dkt in range(ND):
                                t.matmul(psA[dvq][:, 0:C2],
                                         lhsT=S16[:, dkt * 1024 + dv * P:dkt * 1024 + (dv + 1) * P],
                                         rhs=qS[:, dkt * C2:(dkt + 1) * C2],
                                         start=(dkt == 0), stop=False)
                        for jm in range(2):
                            ii = t.matmul(psA[dvq][:, 0:C2],
                                          lhsT=U_c[:, jm * 1024 + dv * P:jm * 1024 + (dv + 1) * P],
                                          rhs=T2ld[k % 2][:, jm * C2:(jm + 1) * C2],
                                          start=(first and jm == 0), stop=(jm == 1))
                    b.inc(ii, V + "dO_mm", 1)
                b.wait(t, V + "dOev", 2 * (k + 1))
                for tm in range(2):
                    for nh in range(2):
                        for dvt in range(ND):
                            ii = t.matmul(psR[:, nh * 512:(nh + 1) * 512] if tm == 0
                                          else psA[nh][:, 0:512],
                                          lhsT=O_c[:, dvt * C2 + tm * P:dvt * C2 + (tm + 1) * P],
                                          rhs=wbufA[:, dvt * D + nh * 512:dvt * D + (nh + 1) * 512],
                                          start=(dvt == 0), stop=(dvt == ND - 1))
                b.inc(ii, V + "dWo_mm", 1)

        @blk.vector
        def _(v):
            for k in range(NCK):
                first = (k == 0)
                b.wait(v, "ab_ivt", 16 * (k + 1))
                if first:
                    v.drain()
                    ii = v.tensor_copy(W_c[:, :], vtokS[:, :])
                    b.inc(ii, V + "dWev", 2)
                else:
                    b.wait(v, V + "dKS_mm", k)
                    v.drain()
                    for tm in range(2):
                        for nh in range(2):
                            ii = v.tensor_tensor(
                                out=W_c[:, tm * 1024 + nh * 512:tm * 1024 + (nh + 1) * 512],
                                in0=vtokS[:, tm * 1024 + nh * 512:tm * 1024 + (nh + 1) * 512],
                                in1=psA[2 * tm + nh][:, 0:512], op=ALU.subtract)
                    b.inc(ii, V + "dWev", 2)
                for im in range(2):
                    for nh in range(2):
                        b.wait(v, V + "dU_mm", 4 * k + 2 * im + nh + 1)
                        b.inc(v.tensor_scalar(
                            out=U_c[:, im * 1024 + nh * 512:im * 1024 + (nh + 1) * 512],
                            in0=psA[2 + 2 * im + nh][:, 0:512],
                            scalar1=ab_beta[:, 2 * k + im:2 * k + im + 1],
                            scalar2=None, op0=ALU.mult), V + "dUev", 1)
                for grp in range(4):
                    b.wait(v, V + "dS_mm", 16 * k + 4 * (grp + 1))
                    v.drain()
                    for gg in range(4):
                        dkt = grp * 2 + gg // 2
                        nh = gg % 2
                        off = dkt * 1024 + nh * 512
                        if first:
                            ii = v.tensor_copy(S32[:, off:off + 512], psA[2 + gg][:, 0:512])
                        else:
                            ii = v.tensor_tensor(out=S32[:, off:off + 512],
                                                 in0=psA[2 + gg][:, 0:512],
                                                 in1=S32[:, off:off + 512], op=ALU.add)
                        b.inc(ii, V + "dSev", 1)
                for half in range(2):
                    b.wait(v, V + "dO_mm", 2 * k + half + 1)
                    v.drain()
                    for dvq in range(4):
                        dv = 4 * half + dvq
                        ii = v.tensor_copy(O_c[:, dv * C2:(dv + 1) * C2],
                                           psA[dvq][:, 0:C2])
                    b.inc(ii, V + "dOev", 1)
                b.wait(v, V + "dWo_mm", k + 1)
                v.drain()
                for tm in range(2):
                    tt = 2 * k + tm
                    for nh in range(2):
                        src = psR[:, nh * 512:(nh + 1) * 512] if tm == 0 else psA[nh][:, 0:512]
                        b.inc(v.tensor_tensor(
                            out=x_tok[:, tt * D + nh * 512:tt * D + (nh + 1) * 512],
                            in0=src,
                            in1=x_tok[:, tt * D + nh * 512:tt * D + (nh + 1) * 512],
                            op=ALU.add), V + "dWo_ev", 1)

        @blk.scalar
        def _(s):
            for k in range(NCK):
                for nh in range(2):
                    first = True
                    for grp in range(4):
                        for gh in range(2):
                            gg = 2 * gh + nh
                            i16 = 4 * grp + gg
                            b.wait(s, V + "dSev", 16 * k + i16 + 1)
                            if first:
                                b.wait(s, V + "dO_mm", 2 * k + 1 + nh)
                                first = False
                            dkt = grp * 2 + gg // 2
                            off = dkt * 1024 + nh * 512
                            ii = s.activation(out=S16[:, off:off + 512],
                                              in_=S32[:, off:off + 512],
                                              func=AF.Identity)
                            b.inc(ii, V + "dS16", 1)

        emit_mlp()

    wg_buf = wbufA
    wu_buf = tmp_buf_.bitcast(BF16)[:, 0:8192]
    wd_buf = m_tok[:, 0:8192]
    gate_t = xb16[:, 0:4096]
    prod_t = xb16[:, 4096:8192]

    def emit_mlp():
        V = "ab_"
        NTB = NT // 4
        b.free("ab_dKtr", "ab_dKtr_ev", "ab_dVtr", "ab_dVtr_ev", "ab_dGmm",
               "ab_dA2", "ab_dAtr", "ab_dPtr", "ab_dPtr_ev", "ab_dMu_mm",
               "ab_dMu_ev", "ab_dP2_mm", "ab_dP2_ev", "ab_dMl", "ab_dMdone",
               "ab_dKS_mm", "ab_dU_mm", "ab_dUev", "ab_dOev")

        @blk.vector
        def _(v):
            b.wait(v, V + "dWo_ev", 4 * NCK)
            v.drain()
            for tt in range(NT):
                v.bn_stats(bn12[:, 12 * tt:12 * tt + 6], x_tok[:, tt * D:tt * D + 512])
                v.bn_stats(bn12[:, 12 * tt + 6:12 * tt + 12],
                           x_tok[:, tt * D + 512:(tt + 1) * D])
            v.drain()
            for tt in range(NT):
                ii = v.bn_aggr(ag2[:, 2 * tt:2 * tt + 2], bn12[:, 12 * tt:12 * tt + 12])
            v.drain()
            v.tensor_tensor(out=ab_ms[:, :], in0=ag2[:, 0::2], in1=ag2[:, 0::2],
                            op=ALU.mult)
            v.drain()
            b.inc(v.tensor_tensor(out=ab_ms[:, :], in0=ab_ms[:, :], in1=ag2[:, 1::2],
                                  op=ALU.add), V + "m_ms", 1)

        @blk.scalar
        def _(s):
            b.wait(s, V + "m_ms")
            s.activation(out=ab_rstd[:, :], in_=ab_ms[:, :], func=AF.Ln,
                         bias=cS["eps_rms"][:, :])
            s.drain()
            b.inc(s.activation(out=ab_rstd[:, :], in_=ab_rstd[:, :], func=AF.Exp,
                               scale=-0.5), V + "m_rstd", 1)
            s.drain()
            for tt in range(NT):
                if tt >= 2:
                    b.wait(s, V + "m_hev", 2 * (tt - 1))
                b.inc(s.activation(out=xln_rot[tt % 2], in_=x_tok[:, tt * D:(tt + 1) * D],
                                   func=AF.Identity, scale=ab_rstd[:, tt:tt + 1]),
                      V + "m_hem", 1)

        @blk.tensor
        def _(t):
            b.wait(t, V + "dS16", 16 * NCK)
            for tt in range(NT):
                b.wait(t, V + "m_hem", tt + 1)
                if tt >= 1:
                    b.wait(t, V + "m_hev", 2 * tt)
                for dt in range(ND):
                    ii = t.matmul(psA[4 + dt // 4][:, (dt % 4) * P:(dt % 4 + 1) * P],
                                  lhsT=xln_rot[tt % 2][:, dt * P:(dt + 1) * P],
                                  rhs=cS["ident16"][:, :], start=True, stop=True)
                b.inc(ii, V + "m_htr", 1)

        @blk.vector
        def _(v):
            hfm = xfm16[:, :].rearrange("p (dt t) -> p dt t", dt=ND)
            for tt in range(NT):
                b.wait(v, V + "m_htr", tt + 1)
                for h in range(2):
                    b.inc(v.tensor_copy(
                        hfm[:, 4 * h:4 * h + 4, tt * P:(tt + 1) * P],
                        psA[4 + h][:, 0:512].rearrange("p (dt t) -> p dt t", dt=4)),
                        V + "m_hev", 1)

        @blk.gpsimd
        def _(g):
            for fh in range(2):
                if fh == 0:
                    b.wait(g, V + "dWo_mm", NCK)
                else:
                    b.wait(g, V + "m_gmm", 8 * NTB)
                b.inc(g.dma_start(
                    out=wg_buf[:, :].rearrange("p (k n) -> p k n", k=ND),
                    in_=wD["b_gate"].rearrange("(k p) n -> p k n", p=P)[:, :, fh * D:(fh + 1) * D]),
                    V + f"wg{fh}", 16)
                if fh == 0:
                    b.wait(g, V + "dS16", 16 * NCK)
                    b.wait(g, V + "dO_mm", 2 * NCK)
                else:
                    b.wait(g, V + "m_umm", 8 * NTB)
                b.inc(g.dma_start(
                    out=wu_buf[:, :].rearrange("p (k n) -> p k n", k=ND),
                    in_=wD["b_up"].rearrange("(k p) n -> p k n", p=P)[:, :, fh * D:(fh + 1) * D]),
                    V + f"wu{fh}", 16)
                if fh == 0:
                    b.wait(g, V + "dSev", 16 * NCK)
                    b.wait(g, V + "dS16", 16 * NCK)
                else:
                    b.wait(g, V + "m_dmm", 4 * NTB)
                b.inc(g.dma_start(
                    out=wd_buf[:, :].rearrange("p (k n) -> p k n", k=ND),
                    in_=wD["b_down"].rearrange("(k p) n -> p k n", p=P)[:, fh * ND:(fh + 1) * ND, :]),
                    V + f"wd{fh}", 16)

        @blk.tensor
        def _(t):
            for fh in range(2):
                b.wait(t, V + f"wgs{fh}", 1)
                b.wait(t, V + f"wus{fh}", 1)
                b.wait(t, V + f"wd{fh}", 16)
                for tb in range(NTB):
                    gidx = fh * NTB + tb
                    for ft in range(ND):
                        fi = gidx * ND + ft
                        if fi >= 2:
                            b.wait(t, V + "m_sil", fi - 1)
                        for dt in range(ND):
                            ii = t.matmul(psA[ft % 2][:, 0:512],
                                          lhsT=wg_buf[:, dt * D + ft * P:dt * D + (ft + 1) * P],
                                          rhs=xfm16[:, dt * T + tb * 512:dt * T + (tb + 1) * 512],
                                          start=(dt == 0), stop=(dt == ND - 1))
                        b.inc(ii, V + "m_gmm", 1)
                    for ft in range(ND):
                        fi = gidx * ND + ft
                        if fi >= 2:
                            b.wait(t, V + "m_pev", fi - 1)
                        for dt in range(ND):
                            ii = t.matmul(psA[2 + ft % 2][:, 0:512],
                                          lhsT=wu_buf[:, dt * D + ft * P:dt * D + (ft + 1) * P],
                                          rhs=xfm16[:, dt * T + tb * 512:dt * T + (tb + 1) * 512],
                                          start=(dt == 0), stop=(dt == ND - 1))
                        b.inc(ii, V + "m_umm", 1)
                    b.wait(t, V + "m_pev", gidx * ND + ND)
                    for ttl in range(4):
                        di = 4 * gidx + ttl
                        if di >= 2:
                            b.wait(t, V + "m_xadd", 2 * (di - 1))
                        for nh in range(2):
                            dst = (psA[4 + ttl % 2][:, 0:512] if nh == 0
                                   else psR[:, (ttl % 2) * 512:(ttl % 2) * 512 + 512])
                            for ff in range(ND):
                                ii = t.matmul(dst,
                                              lhsT=prod_t[:, ff * 512 + ttl * P:ff * 512 + (ttl + 1) * P],
                                              rhs=wd_buf[:, ff * D + nh * 512:ff * D + (nh + 1) * 512],
                                              start=(ff == 0), stop=(ff == ND - 1))
                        b.inc(ii, V + "m_dmm", 1)

        @blk.scalar
        def _(s):
            for fh in range(2):
                for tb in range(NTB):
                    gidx = fh * NTB + tb
                    for ft in range(ND):
                        fi = gidx * ND + ft
                        b.wait(s, V + "m_gmm", fi + 1)
                        s.drain()
                        ii = s.activation(out=gate_t[:, ft * 512:(ft + 1) * 512],
                                          in_=psA[ft % 2][:, 0:512], func=AF.Silu)
                        b.inc(ii, V + "m_sil", 1)

        @blk.vector
        def _(v):
            for fh in range(2):
                b.wait(v, V + f"wg{fh}", 16)
                if fh == 0:
                    b.wait(v, V + "m_hev", 2 * NT)
                v.drain()
                scale_rows_by(v, wg_buf, n2w, ND, D, V + f"wgs{fh}")
                b.wait(v, V + f"wu{fh}", 16)
                scale_rows_by(v, wu_buf, n2w, ND, D, V + f"wus{fh}")
                for tb in range(NTB):
                    gidx = fh * NTB + tb
                    for ft in range(ND):
                        fi = gidx * ND + ft
                        b.wait(v, V + "m_umm", fi + 1)
                        b.wait(v, V + "m_sil", fi + 1)
                        v.drain()
                        ii = v.tensor_tensor(out=prod_t[:, ft * 512:(ft + 1) * 512],
                                             in0=psA[2 + ft % 2][:, 0:512],
                                             in1=gate_t[:, ft * 512:(ft + 1) * 512],
                                             op=ALU.mult)
                        b.inc(ii, V + "m_pev", 1)
                    for ttl in range(4):
                        b.wait(v, V + "m_dmm", 4 * gidx + ttl + 1)
                        v.drain()
                        tt = tb * 4 + ttl
                        for nh in range(2):
                            src = (psA[4 + ttl % 2][:, 0:512] if nh == 0
                                   else psR[:, (ttl % 2) * 512:(ttl % 2) * 512 + 512])
                            b.inc(v.tensor_tensor(
                                out=x_tok[:, tt * D + nh * 512:tt * D + (nh + 1) * 512],
                                in0=src,
                                in1=x_tok[:, tt * D + nh * 512:tt * D + (nh + 1) * 512],
                                op=ALU.add), V + "m_xadd", 1)

        pk_st = [xb_buf_[:, 0:D], xb_buf_[:, D:2 * D]]

        @blk.sync
        def _(s):
            b.wait(s, V + "m_xadd")
            b.wait(s, V + "m_dmm")
            for tt in range(NT):
                if tt >= 2:
                    b.wait(s, V + "pkadd", tt - 1)
                b.inc(s.dma_start(out=pk_st[tt % 2],
                                  in_=y_dram[tt * P:(tt + 1) * P, :]),
                      V + f"pk{tt % 2}", 16)

        @blk.vector
        def _(v):
            v.drain()
            for tt in range(NT):
                b.wait(v, V + f"pk{tt % 2}", 16 * (tt // 2 + 1))
                v.drain()
                b.inc(v.tensor_tensor(out=x_tok[:, tt * D:(tt + 1) * D],
                                      in0=pk_st[tt % 2],
                                      in1=x_tok[:, tt * D:(tt + 1) * D],
                                      op=ALU.add), V + "pkadd", 1)
                if N_SUB >= 4:
                    bn_tile_hook(v, tt)

    S32_dump = m_tok.bitcast(F32)
    bstat_dump = yst["vsum"]
    if N_SUB >= 3:
        b.free_prefix("v0_", keep=("v0_xadd",))
        emit_attnb()
    if N_SUB >= 4:
        b.free_prefix("ab_", keep=("ab_pkadd", "ab_m_xadd"))
        b.rebase("md_ldr", "md_ldfb2", "md_ldw1", "md_ldw2", "md_scat0",
                 "md_scat1", "md_selld", "md_upda0", "md_upda1", "md_gat0",
                 "md_gat1", "md_gat2", "md_gat3")
        emit_mod(1)

    @blk.gpsimd
    def _(g):
        if DBG_ATTNB and N_SUB == 3:
            b.wait(g, "ab_pkadd", b.n("ab_pkadd"))
            b.inc(g.dma_start(out=dbg["q"], in_=q_dram.ap()), "st_out", 16)
            b.inc(g.dma_start(out=dbg["k"], in_=k_dram.ap()), "st_out", 16)
            b.inc(g.dma_start(out=dbg["v"], in_=vema_dram.ap()), "st_out", 16)
            b.inc(g.dma_start(out=dbg["beta"], in_=bstat_dump[:, :]), "st_out", 16)
        ov_ = out.rearrange("(tt p) d -> p tt d", p=P)
        xv_ = x_tok[:, :].rearrange("p (tt d) -> p tt d", tt=NT)
        if N_SUB >= 4:
            for j in range(4):
                b.wait(g, "v1_xadd", 4 * (j + 1))
                b.inc(g.dma_start(out=ov_[:, 4 * j:4 * j + 4, :],
                                  in_=xv_[:, 4 * j:4 * j + 4, :]), f"st_out", 16)
        else:
            if N_SUB == 3:
                b.wait(g, "ab_m_xadd", b.n("ab_m_xadd"))
                b.wait(g, "ab_pkadd", b.n("ab_pkadd"))
            elif N_SUB == 2:
                b.wait(g, "v0_xadd", b.n("v0_xadd"))
            else:
                b.wait(g, "oa_ev")
            b.inc(g.dma_start(out=ov_, in_=xv_), "st_out", 16)
        b.wait(g, "st_out")


_CACHE = {}


def kernel(L0_attn_a_x, L0_attn_a_Wq, L0_attn_a_Wk, L0_attn_a_Wv,
           L0_attn_b_norm1_w, L0_attn_b_Wq, L0_attn_b_Wk, L0_attn_b_Wv,
           L0_attn_b_beta_w, L0_attn_b_beta_b, L0_attn_b_Wo, L0_attn_b_alpha_logit,
           L0_attn_b_norm2_w, L0_attn_b_w_gate, L0_attn_b_w_up, L0_attn_b_w_down,
           L0_ffn_w_router, L0_ffn_W1, L0_ffn_b1, L0_ffn_W2, L0_ffn_b2):
    x = np.ascontiguousarray(np.asarray(L0_attn_a_x, dtype=np.float32))
    consts = build_consts()
    if "nc" not in _CACHE:
        _CACHE["nc"] = build_nc()
    nc = _CACHE["nc"]

    common = {
        "x_all": x,
        "aWq": np.ascontiguousarray(L0_attn_a_Wq, dtype=np.float32),
        "aWk": np.ascontiguousarray(L0_attn_a_Wk, dtype=np.float32),
        "aWv": np.ascontiguousarray(L0_attn_a_Wv, dtype=np.float32),
        "f_router": np.ascontiguousarray(L0_ffn_w_router, dtype=np.float32).reshape(1, D),
        "fW1": np.ascontiguousarray(L0_ffn_W1, dtype=np.float32),
        "fb1": np.ascontiguousarray(np.asarray(L0_ffn_b1, dtype=np.float32).reshape(FF // P, P).T),
        "fW2": np.ascontiguousarray(L0_ffn_W2, dtype=np.float32),
        "fb2": np.ascontiguousarray(L0_ffn_b2, dtype=np.float32).reshape(1, D),
        "b_n1": np.ascontiguousarray(np.asarray(L0_attn_b_norm1_w, dtype=np.float32).reshape(D // P, P).T),
        "bWq": np.ascontiguousarray(L0_attn_b_Wq, dtype=np.float32),
        "bWk": np.ascontiguousarray(L0_attn_b_Wk, dtype=np.float32),
        "bWv": np.ascontiguousarray(L0_attn_b_Wv, dtype=np.float32),
        "b_betaw": np.ascontiguousarray(np.asarray(L0_attn_b_beta_w, dtype=np.float32).reshape(D // P, P).T),
        "b_betab": np.ascontiguousarray(L0_attn_b_beta_b, dtype=np.float32).reshape(1, 1),
        "bWo": np.ascontiguousarray(L0_attn_b_Wo, dtype=np.float32),
        "b_alpha": np.ascontiguousarray(np.asarray(L0_attn_b_alpha_logit, dtype=np.float32).reshape(D // P, P).T),
        "b_n2": np.ascontiguousarray(np.asarray(L0_attn_b_norm2_w, dtype=np.float32).reshape(D // P, P).T),
        "b_gate": np.ascontiguousarray(L0_attn_b_w_gate, dtype=np.float32),
        "b_up": np.ascontiguousarray(L0_attn_b_w_up, dtype=np.float32),
        "b_down": np.ascontiguousarray(L0_attn_b_w_down, dtype=np.float32),
    }
    for nm, arr in consts.items():
        common["c_" + nm] = np.ascontiguousarray(arr)

    in_maps = []
    for c in range(B):
        m = dict(common)
        m["x_own"] = np.ascontiguousarray(x[c])
        in_maps.append(m)

    res = run_bass_kernel_spmd(nc, in_maps, core_ids=list(range(B)))
    return np.stack([np.asarray(res.results[c]["out"]) for c in range(B)], axis=0)


# version 18 (final)
# speedup vs baseline: 1.0376x; 1.0376x over previous
import numpy as np
import ml_dtypes
from contextlib import ExitStack

import concourse.bass as bass
import concourse.mybir as mybir
from concourse.bass_utils import run_bass_kernel_spmd

F32 = mybir.dt.float32
BF16 = mybir.dt.bfloat16
I32 = mybir.dt.int32
AF = mybir.ActivationFunctionType
ALU = mybir.AluOpType
AX = mybir.AxisListType.X
BF = ml_dtypes.bfloat16

P = 128
B = 8
T = 2048
D = 1024
HID = 2048
FF = 4096
GAMMA = 0.96875
C2 = 256
N_SUB = 4
DBG_ATTNB = False

NT = T // P
ND = D // P
NDC = T // C2
NCH = B * (NT // 2)

GELU_A = float(2.0 * np.sqrt(2.0 / np.pi))
GELU_C = 0.044715


def configure(T_=2048, n_sub=4):
    global T, N_SUB, NT, NDC, NCH
    T = T_
    N_SUB = n_sub
    NT = T // P
    NDC = T // C2
    NCH = B * (NT // 2)


def build_consts():
    g = np.float64(GAMMA)
    c = {}
    c["ident16"] = np.eye(P, dtype=np.float32).astype(BF)
    c["ident32"] = np.eye(P, dtype=np.float32)
    c["ones32row"] = np.ones((1, P), dtype=np.float32)
    u = np.arange(P)[:, None].astype(np.float64)
    i = np.arange(P)[None, :].astype(np.float64)

    def tbl(off, ksub):
        e = off + 2 * i - (u + P * ksub)
        return np.power(g, np.maximum(e, 0.0)) * (e >= 0)

    c["gle0"] = tbl(0.0, 0).astype(BF)
    c["gle1"] = tbl(0.0, 1).astype(BF)
    c["gro0"] = tbl(1.0, 0).astype(BF)
    c["gro1"] = tbl(1.0, 1).astype(BF)
    gle0f = tbl(0.0, 0)
    gro0f = tbl(1.0, 0)
    gle0f[0, :] = 0.0
    gro0f[0, :] = 0.0
    gle0f[:, 0] = 0.0
    gle0f[1, 0] = 1.0
    c["gle0f"] = gle0f.astype(BF)
    c["gro0f"] = gro0f.astype(BF)
    e0 = (C2 - 1) - np.arange(P).astype(np.float64)
    e1 = (P - 1) - np.arange(P).astype(np.float64)
    c["gcar0"] = np.power(g, e0)[:, None].astype(BF)
    c["gcar1"] = np.power(g, e1)[:, None].astype(BF)
    g0f = np.power(g, e0)
    g0f[0] = 0.0
    c["gcar0f"] = g0f[:, None].astype(BF)
    c["ginjL"] = np.power(g, 2 * np.arange(P) + 1)[None, :].astype(BF)
    c["ginjR"] = np.power(g, 2 * np.arange(P) + 2)[None, :].astype(BF)
    c["g256"] = np.array([[g ** C2]], dtype=np.float64).astype(BF)
    jj = np.arange(P)[:, None]
    ii = np.arange(P)[None, :]
    c["mask_jle"] = (jj <= ii).astype(np.float32).astype(BF)
    c["mask_jlt"] = (jj < ii).astype(np.float32).astype(BF)
    c["eps_ln"] = np.full((P, 1), 1e-5, dtype=np.float32)
    c["eps_rms"] = np.full((P, 1), 1e-6, dtype=np.float32)
    return c


DMA_SEM_NAMES = {
    "ld_x", "ld_c", "ld_wv", "ld_wk", "ld_wq", "xb_ld0", "xb_ld1", "yacc0",
    "yacc1", "ybk_ld0", "ybk_ld1", "ld_fb1", "md_ldr", "md_ldfb2", "md_ldw1",
    "md_ldw2", "md_scat0", "md_scat1", "md_selld", "md_upda0", "md_upda1",
    "md_gat0", "md_gat1", "md_gat2", "md_gat3", "ab_vecs", "ab_wq_ld", "ab_wk_ld", "ab_wv_ld",
    "ab_wo_ld", "ab_qst0", "ab_qst1", "ab_kst0", "ab_kst1", "ab_vst0",
    "ab_vst1", "ab_dldk", "ab_dldq", "ab_dldv", "ab_wg0", "ab_wg1", "ab_wu0",
    "ab_wu1", "ab_wd0", "ab_wd1", "ab_pk0", "ab_pk1", "ab_park", "st_out",
    "ab_okf", "ab_ovt", "ab_omt", "ab_ot2", "ab_ikf", "ab_ikt", "ab_iq",
    "ab_ivt", "ab_imt", "ab_it2",
}


class Bld:
    def __init__(self, nc, es):
        self.nc = nc
        self.es = es
        self._s = {}
        self._pool = []

    def sem(self, name):
        if name not in self._s:
            if self._pool and name not in DMA_SEM_NAMES:
                h, cnt = self._pool.pop()
                self._s[name] = [h, cnt, cnt]
            else:
                self._s[name] = [self.es.enter_context(self.nc.semaphore(name)), 0, 0]
        return self._s[name][0]

    def free(self, *names):
        for nm in list(names):
            if nm in self._s and nm not in DMA_SEM_NAMES:
                e = self._s.pop(nm)
                self._pool.append((e[0], e[1]))

    def free_prefix(self, pfx, keep=()):
        for nm in [n for n in self._s if n.startswith(pfx) and n not in keep
                   and n not in DMA_SEM_NAMES]:
            e = self._s.pop(nm)
            self._pool.append((e[0], e[1]))

    def rebase(self, *names):
        for nm in names:
            if nm in self._s:
                self._s[nm][2] = self._s[nm][1]

    def inc(self, instr, name, amt=1):
        h = self.sem(name)
        self._s[name][1] += amt
        instr.then_inc(h, amt)
        return instr

    def n(self, name):
        self.sem(name)
        e = self._s[name]
        return e[1] - e[2]

    def wait(self, eng, name, val=None):
        h = self.sem(name)
        e = self._s[name]
        eng.wait_ge(h, e[1] if val is None else e[2] + val)

    def sb(self, name, shape, dtype):
        return self.es.enter_context(self.nc.sbuf_tensor(name, list(shape), dtype))

    def ps(self, name, shape, dtype=F32):
        return self.es.enter_context(self.nc.psum_tensor(name, list(shape), dtype))


def build_nc():
    consts = build_consts()
    nc = bass.Bass("TRN2", target_bir_lowering=False)

    x_own = nc.dram_tensor("x_own", [T, D], F32, kind="ExternalInput").ap()
    x_all = nc.dram_tensor("x_all", [B, T, D], F32, kind="ExternalInput").ap()
    wD = {}
    for nm, shape in [
        ("aWq", [D, D]), ("aWk", [D, D]), ("aWv", [D, D]),
        ("f_router", [1, D]), ("fW1", [D, FF]), ("fb1", [P, FF // P]),
        ("fW2", [FF, D]), ("fb2", [1, D]),
        ("b_n1", [P, D // P]), ("bWq", [D, D]), ("bWk", [D, D]), ("bWv", [D, D]),
        ("b_betaw", [P, D // P]), ("b_betab", [1, 1]), ("bWo", [D, D]),
        ("b_alpha", [P, D // P]), ("b_n2", [P, D // P]),
        ("b_gate", [D, HID]), ("b_up", [D, HID]), ("b_down", [HID, D]),
    ]:
        wD[nm] = nc.dram_tensor(nm, shape, F32, kind="ExternalInput").ap()
    cD = {}
    for nm, arr in consts.items():
        dt = BF16 if arr.dtype == BF else F32
        cD[nm] = nc.dram_tensor("c_" + nm, list(arr.shape), dt, kind="ExternalInput").ap()
    out = nc.dram_tensor("out", [T, D], F32, kind="ExternalOutput").ap()
    dbg = {}
    if DBG_ATTNB:
        dbg["q"] = nc.dram_tensor("dbg_q", [D, T], BF16, kind="ExternalOutput").ap()
        dbg["k"] = nc.dram_tensor("dbg_k", [T, D], BF16, kind="ExternalOutput").ap()
        dbg["v"] = nc.dram_tensor("dbg_v", [D, T], BF16, kind="ExternalOutput").ap()
        dbg["S"] = nc.dram_tensor("dbg_S", [P, 8 * D], F32, kind="ExternalOutput").ap()
        dbg["beta"] = nc.dram_tensor("dbg_beta", [P, 128], F32, kind="ExternalOutput").ap()
    sel_dram = nc.dram_tensor("sel_dram", [T // 2, D], BF16)
    upd_dram = nc.dram_tensor("upd_dram", [T // 2, D], F32)
    y_dram = nc.dram_tensor("y_dram", [T, D], F32)
    q_dram = nc.dram_tensor("q_dram", [D, T], BF16)
    k_dram = nc.dram_tensor("k_dram", [T, D], BF16)
    vema_dram = nc.dram_tensor("vema_dram", [D, T], BF16)
    kv_scr = nc.dram_tensor("kv_scr", [2 * T, 2 * D], BF16)

    es = ExitStack()
    with es:
        b = Bld(nc, es)
        emit(b, consts, cD, wD, x_own, x_all, out, sel_dram, upd_dram, y_dram,
             q_dram, k_dram, vema_dram, kv_scr, dbg)
    return nc


def emit(b, consts, cD, wD, x_own, x_all, out, sel_dram, upd_dram, y_dram,
         q_dram, k_dram, vema_dram, kv_scr=None, dbg=None):
    nc = b.nc
    blk = b.es.enter_context(nc.Block())

    bc_reg = nc.gpsimd.alloc_register("bc_reg")

    x_tok = b.sb("x_tok", (P, NT * D), F32)
    m_tok = b.sb("m_tok", (P, NT * D), BF16)
    xln_fm = b.sb("xln_fm", (P, ND * T), BF16)
    wbufA = b.sb("wbufA", (P, ND * D), BF16)
    xb_buf_ = b.sb("xb_buf", (P, 2 * 2 * D), F32)
    xb_buf = [xb_buf_[:, 0:2 * D], xb_buf_[:, 2 * D:4 * D]]
    q_scr = xb_buf_[:, 0:T]
    prod_scr = xb_buf_[:, T:2 * T]
    tmp_buf_ = b.sb("tmp_buf", (P, 2 * 2 * D), F32)
    tmp_buf = [tmp_buf_[:, 0:2 * D], tmp_buf_[:, 2 * D:4 * D]]
    ybk_buf = [tmp_buf_[:, 0:D], tmp_buf_[:, D:2 * D]]
    rot_ = b.sb("rot", (P, 2 * D), BF16)
    xln_rot = [rot_[:, 0:D], rot_[:, D:2 * D]]
    trbuf = [rot_[:, 0:8 * P], rot_[:, 8 * P:16 * P]]

    cS = {}
    for nm, arr in consts.items():
        dt = BF16 if arr.dtype == BF else F32
        cS[nm] = b.sb("cs_" + nm, arr.shape, dt)

    st = {nm: b.sb("st_" + nm, (P, NT), F32) for nm in ["rstd", "nmr"]}
    bn12 = b.sb("st_bn12", (P, 12 * NT), F32)
    ag2 = b.sb("st_ag2", (P, 2 * NT), F32)
    yst = {nm: b.sb("yst_" + nm, (P, 2 * NCH), F32)
           for nm in ["rstd", "s0", "vsum", "aco", "bco"]}
    ybn = b.sb("yst_bn", (P, 48), F32)
    yag = b.sb("yst_ag", (P, 4 * NCH), F32)

    wvs_col = b.sb("wvs_col", (P, ND), F32)
    wvs_row = b.sb("wvs_row", (1, D), F32)
    wvs_bc = b.sb("wvs_bc", (P, D), F32)
    swv_tmp = b.sb("swv_tmp", (1, 1), F32)
    swv_bc = b.sb("swv_bc", (P, 1), F32)
    carry = b.sb("carry", (1, 2 * D), BF16)
    junk = m_tok.bitcast(F32)[:, 0:2 * D]

    psA = [b.ps(f"psA{i}", (P, 512), F32) for i in range(6)]
    psR = b.ps("psR", (P, 1024), F32)

    @blk.sync
    def _(s):
        b.inc(s.dma_start(
            out=x_tok[:, :].rearrange("p (tt d) -> p tt d", tt=NT),
            in_=x_own.rearrange("(tt p) d -> p tt d", p=P)), "ld_x", 16)

    @blk.gpsimd
    def _(g):
        g.reg_mov(bc_reg, T // 2 - 1)
        for nm in consts:
            b.inc(g.dma_start(out=cS[nm][:, :], in_=cD[nm]), "ld_c", 16)
        b.inc(g.dma_start(
            out=wbufA[:, :].rearrange("p (k n) -> p k n", k=ND),
            in_=wD["aWv"].rearrange("(k p) n -> p k n", p=P)), "ld_wv", 16)

    @blk.vector
    def _(v):
        b.wait(v, "ld_wv")
        for dt in range(ND):
            ii = v.tensor_reduce(out=wvs_col[:, dt:dt + 1],
                                 in_=wbufA[:, dt * D:(dt + 1) * D], axis=AX, op=ALU.add)
        b.inc(ii, "wvs", 1)
        v.memset(carry[0:1, :], 0.0)

    @blk.gpsimd
    def _(g):
        b.wait(g, "wvs")
        b.inc(g.dma_start(
            out=wbufA[:, :].rearrange("p (k n) -> p k n", k=ND),
            in_=wD["aWk"].rearrange("(k p) n -> p k n", p=P)), "ld_wk", 16)

    @blk.tensor
    def _(t):
        b.wait(t, "wvs")
        b.wait(t, "ld_c")
        for dt in range(ND):
            ii = t.matmul(psA[dt // 4][0:1, (dt % 4) * P:(dt % 4 + 1) * P],
                          lhsT=wvs_col[:, dt:dt + 1], rhs=cS["ident32"][:, :],
                          start=True, stop=True)
        b.inc(ii, "wvs_tr", 1)

    @blk.vector
    def _(v):
        b.wait(v, "wvs_tr")
        v.tensor_copy(wvs_row[0:1, 0:512], psA[0][0:1, 0:512])
        v.tensor_copy(wvs_row[0:1, 512:1024], psA[1][0:1, 0:512])
        v.drain()
        b.inc(v.tensor_reduce(out=swv_tmp[0:1, 0:1], in_=wvs_row[0:1, :],
                              axis=AX, op=ALU.add), "swv0", 1)

    @blk.tensor
    def _(t):
        b.wait(t, "swv0")
        t.matmul(psA[2][:, 0:512], lhsT=cS["ones32row"][:, :],
                 rhs=wvs_row[0:1, 0:512], start=True, stop=True)
        t.matmul(psA[3][:, 0:512], lhsT=cS["ones32row"][:, :],
                 rhs=wvs_row[0:1, 512:1024], start=True, stop=True)
        ii = t.matmul(psA[4][:, 0:1], lhsT=cS["ones32row"][:, :],
                      rhs=swv_tmp[0:1, 0:1], start=True, stop=True)
        b.inc(ii, "bc_mm", 1)

    @blk.vector
    def _(v):
        b.wait(v, "bc_mm")
        v.tensor_scalar(out=wvs_bc[:, 0:512], in0=psA[2][:, 0:512],
                        scalar1=-1.0, scalar2=None, op0=ALU.mult)
        v.tensor_scalar(out=wvs_bc[:, 512:1024], in0=psA[3][:, 0:512],
                        scalar1=-1.0, scalar2=None, op0=ALU.mult)
        b.inc(v.tensor_copy(swv_bc[:, :], psA[4][:, 0:1]), "bc_done", 1)

    @blk.vector
    def _(v):
        b.wait(v, "ld_x")
        for tt in range(NT):
            v.bn_stats(bn12[:, 12 * tt:12 * tt + 6], x_tok[:, tt * D:tt * D + 512])
            v.bn_stats(bn12[:, 12 * tt + 6:12 * tt + 12],
                       x_tok[:, tt * D + 512:(tt + 1) * D])
        v.drain()
        for tt in range(NT):
            ii = v.bn_aggr(ag2[:, 2 * tt:2 * tt + 2], bn12[:, 12 * tt:12 * tt + 12])
        b.inc(ii, "own_var", 1)

    @blk.scalar
    def _(s):
        b.wait(s, "own_var")
        b.wait(s, "ld_c")
        s.activation(out=st["rstd"][:, :], in_=ag2[:, 1::2], func=AF.Ln,
                     bias=cS["eps_ln"][:, :])
        s.drain()
        b.inc(s.activation(out=st["rstd"][:, :], in_=st["rstd"][:, :], func=AF.Exp,
                           scale=-0.5), "own_rstd", 1)

    @blk.vector
    def _(v):
        b.wait(v, "own_rstd")
        v.tensor_tensor(out=st["nmr"][:, :], in0=ag2[:, 0::2],
                        in1=st["rstd"][:, :], op=ALU.mult)
        v.drain()
        b.inc(v.tensor_scalar(out=st["nmr"][:, :], in0=st["nmr"][:, :], scalar1=-1.0,
                              scalar2=None, op0=ALU.mult), "own_nmr", 1)

    @blk.scalar
    def _(s):
        b.wait(s, "own_nmr")
        for tt in range(NT):
            if tt >= 2:
                b.wait(s, "xln_ev", 2 * (tt - 1))
            b.inc(s.activation(out=xln_rot[tt % 2],
                               in_=x_tok[:, tt * D:(tt + 1) * D], func=AF.Identity,
                               scale=st["rstd"][:, tt:tt + 1],
                               bias=st["nmr"][:, tt:tt + 1]), "xln_em", 1)

    @blk.tensor
    def _(t):
        b.wait(t, "bc_done")
        for tt in range(NT):
            b.wait(t, "xln_em", tt + 1)
            if tt >= 1:
                b.wait(t, "xln_ev", 2 * tt)
            for dt in range(ND):
                ii = t.matmul(psA[4 + dt // 4][:, (dt % 4) * P:(dt % 4 + 1) * P],
                              lhsT=xln_rot[tt % 2][:, dt * P:(dt + 1) * P],
                              rhs=cS["ident16"][:, :], start=True, stop=True)
            b.inc(ii, "xln_tr", 1)

    @blk.vector
    def _(v):
        xfm = xln_fm[:, :].rearrange("p (dt t) -> p dt t", dt=ND)
        for tt in range(NT):
            b.wait(v, "xln_tr", tt + 1)
            for h in range(2):
                b.inc(v.tensor_copy(
                    xfm[:, 4 * h:4 * h + 4, tt * P:(tt + 1) * P],
                    psA[4 + h][:, 0:512].rearrange("p (dt t) -> p dt t", dt=4)),
                    "xln_ev", 1)

    xa4 = x_all.rearrange("b (cc q p) d -> b p cc q d", p=P, q=2)
    NREG = NT // 2
    YORD = [(ch_, bi_) for ch_ in range(NREG) for bi_ in range(B)]
    assert len(YORD) == NCH
    ADDCNT = []
    _c = 0
    for pos, (ch_, bi_) in enumerate(YORD):
        if bi_ >= 1:
            _c += 1
        ADDCNT.append(_c)
    ysb = [m_tok.bitcast(F32)[:, 2 * D:4 * D], m_tok.bitcast(F32)[:, 4 * D:6 * D]]

    @blk.sync
    def _(s):
        for pos, (ch, bi) in enumerate(YORD):
            if pos >= 2:
                b.wait(s, "xb_fv", pos - 1)
                b.wait(s, "xb_fs", pos - 1)
            b.inc(s.dma_start(
                out=xb_buf[pos % 2].rearrange("p (q d) -> p q d", q=2),
                in_=xa4[bi, :, ch, :, :]), f"xb_ld{pos % 2}", 16)

    @blk.vector
    def _(v):
        for pos, (ch, bi) in enumerate(YORD):
            par = pos % 2
            b.wait(v, f"xb_ld{pos % 2}", 16 * (pos // 2 + 1))
            v.drain()
            xb = xb_buf[par]
            c0 = 2 * pos
            sl = slice(c0, c0 + 2)
            yb0 = 24 * (pos % 2)
            for q in range(2):
                v.bn_stats(ybn[:, yb0 + 12 * q:yb0 + 12 * q + 6],
                           xb[:, q * D:q * D + 512])
                v.bn_stats(ybn[:, yb0 + 12 * q + 6:yb0 + 12 * q + 12],
                           xb[:, q * D + 512:(q + 1) * D])
            v.drain()
            for q in range(2):
                ii = v.bn_aggr(yag[:, 4 * pos + 2 * q:4 * pos + 2 * q + 2],
                               ybn[:, yb0 + 12 * q:yb0 + 12 * q + 12])
            b.inc(ii, "y_var", 1)
            v.scalar_tensor_tensor(out=junk[:, 0:D], in0=xb[:, 0:D], scalar=1.0,
                                   in1=wvs_bc[:, :], op0=ALU.mult, op1=ALU.mult,
                                   accum_out=yst["s0"][:, c0:c0 + 1])
            ii = v.scalar_tensor_tensor(out=junk[:, D:2 * D], in0=xb[:, D:2 * D],
                                        scalar=1.0, in1=wvs_bc[:, :],
                                        op0=ALU.mult, op1=ALU.mult,
                                        accum_out=yst["s0"][:, c0 + 1:c0 + 2])
            b.inc(ii, "xb_fv", 1)
            if pos >= 1 and YORD[pos - 1][1] >= 1:
                chp = YORD[pos - 1][0]
                b.wait(v, "xb_fs", pos)
                b.inc(v.tensor_tensor(out=ysb[chp % 2][:, :],
                                      in0=tmp_buf[(pos - 1) % 2][:, :],
                                      in1=ysb[chp % 2][:, :], op=ALU.add),
                      "y_gadd", 1)
            if pos == NCH - 1:
                b.wait(v, "xb_fs", pos + 1)
                v.drain()
                b.inc(v.tensor_tensor(out=ysb[ch % 2][:, :],
                                      in0=tmp_buf[pos % 2][:, :],
                                      in1=ysb[ch % 2][:, :], op=ALU.add),
                      "y_gadd", 1)

    @blk.scalar
    def _(s):
        for pos, (ch, bi) in enumerate(YORD):
            par = pos % 2
            c0 = 2 * pos
            b.wait(s, "y_var", pos + 1)
            s.activation(out=yst["rstd"][:, c0:c0 + 2],
                         in_=yag[:, 4 * pos + 1:4 * pos + 4:2],
                         func=AF.Ln, bias=cS["eps_ln"][:, :])
            s.drain()
            s.activation(out=yst["rstd"][:, c0:c0 + 2],
                         in_=yst["rstd"][:, c0:c0 + 2], func=AF.Exp, scale=-1.0)
            b.wait(s, "xb_fv", pos + 1)
            s.drain()
            for q in range(2):
                cq = c0 + q
                s.activation(out=yst["vsum"][:, cq:cq + 1],
                             in_=yag[:, 4 * pos + 2 * q:4 * pos + 2 * q + 1],
                             func=AF.Identity, scale=swv_bc[:, 0:1],
                             bias=yst["s0"][:, cq:cq + 1])
            s.drain()
            for q in range(2):
                cq = c0 + q
                s.activation(out=yst["aco"][:, cq:cq + 1],
                             in_=yst["vsum"][:, cq:cq + 1], func=AF.Identity,
                             scale=yst["rstd"][:, cq:cq + 1])
            s.drain()
            for q in range(2):
                cq = c0 + q
                s.activation(out=yst["bco"][:, cq:cq + 1],
                             in_=yag[:, 4 * pos + 2 * q:4 * pos + 2 * q + 1],
                             func=AF.Identity, scale=yst["aco"][:, cq:cq + 1])
            for q in range(2):
                cq = c0 + q
                s.activation(out=yst["aco"][:, cq:cq + 1],
                             in_=yst["aco"][:, cq:cq + 1], func=AF.Identity,
                             scale=-1.0)
            s.drain()
            xb = xb_buf[par]
            if bi == 0:
                dst = ysb[ch % 2]
                if ch >= 2:
                    b.wait(s, f"yacc{ch % 2}", 16 * (ch // 2))
            else:
                dst = tmp_buf[par]
                if pos >= 2:
                    b.wait(s, "y_gadd", ADDCNT[pos - 2])
            s.activation(out=dst[:, 0:D], in_=xb[:, 0:D], func=AF.Identity,
                         scale=yst["aco"][:, c0:c0 + 1], bias=yst["bco"][:, c0:c0 + 1])
            ii = s.activation(out=dst[:, D:2 * D], in_=xb[:, D:2 * D],
                              func=AF.Identity, scale=yst["aco"][:, c0 + 1:c0 + 2],
                              bias=yst["bco"][:, c0 + 1:c0 + 2])
            b.inc(ii, "xb_fs", 1)

    @blk.gpsimd
    def _(g):
        for ch in range(NREG):
            b.wait(g, "y_gadd", ADDCNT[ch * B + B - 1])
            if ch >= 2:
                b.wait(g, f"yacc{ch % 2}", 16 * (ch // 2))
            dst = y_dram[ch * 2 * P:(ch + 1) * 2 * P, :].rearrange(
                "(q p) d -> p q d", p=P)
            b.inc(g.dma_start(
                out=dst, in_=ysb[ch % 2].rearrange("p (q d) -> p q d", q=2)),
                f"yacc{ch % 2}", 16)

    @blk.gpsimd
    def _(g):
        b.wait(g, "m_mm", 2 * NT)
        b.inc(g.dma_start(
            out=wbufA[:, :].rearrange("p (k n) -> p k n", k=ND),
            in_=wD["aWq"].rearrange("(k p) n -> p k n", p=P)), "ld_wq", 16)

    @blk.sync
    def _(s):
        b.wait(s, "yacc0")
        b.wait(s, "yacc1")
        for tt in range(NT):
            if tt >= 2:
                b.wait(s, "ytr_mm", tt - 1)
            b.inc(s.dma_start(out=ybk_buf[tt % 2],
                              in_=y_dram[tt * P:(tt + 1) * P, :]), f"ybk_ld{tt % 2}", 16)

    @blk.tensor
    def _(t):
        def ytr_mms(tt):
            b.wait(t, f"ybk_ld{tt % 2}", 16 * (tt // 2 + 1))
            if tt >= 2:
                b.wait(t, "ytr_ev", 2 * (tt - 1))
            pb = 0 if tt % 2 == 0 else 4
            for dt in range(ND):
                ii = t.matmul(psA[pb + dt // 4][:, (dt % 4) * P:(dt % 4 + 1) * P],
                              lhsT=ybk_buf[tt % 2][:, dt * P:(dt + 1) * P],
                              rhs=cS["ident32"][:, :], start=True, stop=True)
            b.inc(ii, "ytr_mm", 1)

        def m_mms(tt):
            b.wait(t, "ytr_ev", 2 * (tt + 1))
            if tt >= 1:
                b.wait(t, "m_ev", 2 * tt)
            for h in range(2):
                for dt in range(ND):
                    ii = t.matmul(psA[2 + h][:, 0:512],
                                  lhsT=trbuf[tt % 2][:, dt * P:(dt + 1) * P],
                                  rhs=wbufA[:, dt * D + h * 512: dt * D + (h + 1) * 512],
                                  start=(dt == 0), stop=(dt == ND - 1))
                b.inc(ii, "m_mm", 1)

        b.wait(t, "ld_wk")
        for tt in range(NT):
            ytr_mms(tt)
            if tt >= 1:
                m_mms(tt - 1)
        m_mms(NT - 1)

    @blk.vector
    def _(v):
        v.drain()
        for tt in range(NT):
            b.wait(v, "ytr_mm", tt + 1)
            if tt >= 2:
                b.wait(v, "m_mm", 2 * (tt - 1))
            pb = 0 if tt % 2 == 0 else 4
            for h in range(2):
                b.inc(v.tensor_copy(trbuf[tt % 2][:, h * 512:(h + 1) * 512],
                                    psA[pb + h][:, 0:512]), "ytr_ev", 1)
            if tt >= 1:
                b.wait(v, "m_mm", 2 * tt)
                for h in range(2):
                    b.inc(v.tensor_copy(
                        m_tok[:, (tt - 1) * D + h * 512:(tt - 1) * D + (h + 1) * 512],
                        psA[2 + h][:, 0:512]), "m_ev", 1)
        b.wait(v, "m_mm", 2 * NT)
        for h in range(2):
            b.inc(v.tensor_copy(
                m_tok[:, (NT - 1) * D + h * 512:(NT - 1) * D + (h + 1) * 512],
                psA[2 + h][:, 0:512]), "m_ev", 1)

    @blk.tensor
    def _(t):
        b.wait(t, "m_ev")
        b.wait(t, "ld_wq")
        b.wait(t, "bc_done")
        for k in range(NDC):
            first = (k == 0)
            gl = ("gle0f" if first else "gle0", "gle1")
            gr = ("gro0f" if first else "gro0", "gro1")
            gc = ("gcar0f" if first else "gcar0", "gcar1")
            mk0 = m_tok[:, (2 * k) * D:(2 * k) * D + D]
            mk1 = m_tok[:, (2 * k + 1) * D:(2 * k + 1) * D + D]
            cr_in = carry[0:1, (k % 2) * D:(k % 2) * D + D]
            if k >= 1:
                b.wait(t, "q_ev", 4 * k)
                b.wait(t, "oa_ev", 4 * k)
            for nq in range(4):
                for dt in range(ND):
                    ii = t.matmul(psA[nq][:, 0:512],
                                  lhsT=wbufA[:, dt * D + k * P: dt * D + (k + 1) * P],
                                  rhs=xln_fm[:, dt * T + nq * 512: dt * T + (nq + 1) * 512],
                                  start=(dt == 0), stop=(dt == ND - 1))
            b.inc(ii, "q_mm", 1)
            b.wait(t, "car_ev", k)
            if k >= 2:
                pass
            for h in range(2):
                t.matmul(psR[0:1, h * 512:(h + 1) * 512], lhsT=cS[gc[0]][:, :],
                         rhs=mk0[:, h * 512:(h + 1) * 512], start=True, stop=False)
                t.matmul(psR[0:1, h * 512:(h + 1) * 512], lhsT=cS[gc[1]][:, :],
                         rhs=mk1[:, h * 512:(h + 1) * 512], start=False, stop=False)
                ii = t.matmul(psR[0:1, h * 512:(h + 1) * 512], lhsT=cS["g256"][:, :],
                              rhs=cr_in[0:1, h * 512:(h + 1) * 512],
                              start=False, stop=True)
            b.inc(ii, "car_mm", 1)
            for ci in range(4):
                side, h = divmod(ci, 2)
                gm = gl if side == 0 else gr
                ginj = "ginjL" if side == 0 else "ginjR"
                pt = psA[4 + ci % 2]
                if k >= 1 or ci >= 2:
                    b.wait(t, "lr_ev", 4 * k + ci - 1)
                t.matmul(pt[:, 0:512], lhsT=cS[gm[0]][:, :],
                         rhs=mk0[:, h * 512:(h + 1) * 512], start=True, stop=False)
                t.matmul(pt[:, 0:512], lhsT=cS[gm[1]][:, :],
                         rhs=mk1[:, h * 512:(h + 1) * 512], start=False, stop=False)
                ii = t.matmul(pt[:, 0:512], lhsT=cS[ginj][:, :],
                              rhs=cr_in[0:1, h * 512:(h + 1) * 512],
                              start=False, stop=True)
                b.inc(ii, "lr_mm", 1)
            b.wait(t, "lr_ev", 4 * k + 4)
            for g4 in range(4):
                if k > 0 or g4 > 0:
                    b.wait(t, "oa_ev", 4 * k + g4)
                for j in range(4):
                    tt4 = g4 * 4 + j
                    ii = t.matmul(psA[0][:, j * P:(j + 1) * P],
                                  lhsT=prod_scr[:, tt4 * P:(tt4 + 1) * P],
                                  rhs=cS["ident32"][:, :], start=True, stop=True)
                b.inc(ii, "oa_mm", 1)

    @blk.vector
    def _(v):
        v.drain()
        xv = x_tok[:, :].rearrange("p (tt d) -> p tt d", tt=NT)
        for k in range(NDC):
            b.wait(v, "q_mm", k + 1)
            for nq in range(4):
                b.inc(v.tensor_copy(q_scr[:, nq * 512:(nq + 1) * 512],
                                    psA[nq][:, 0:512]), "q_ev", 1)
            v.drain()
            b.wait(v, "car_mm", k + 1)
            b.wait(v, "lr_mm", 4 * k)
            b.inc(v.tensor_copy(carry[0:1, ((k + 1) % 2) * D:((k + 1) % 2) * D + D],
                                psR[0:1, 0:1024]), "car_ev", 1)
            for ci in range(4):
                side, h = divmod(ci, 2)
                b.wait(v, "lr_mm", 4 * k + ci + 1)
                off = side * 1024 + h * 512
                b.inc(v.tensor_tensor(out=prod_scr[:, off:off + 512],
                                      in0=psA[4 + ci % 2][:, 0:512],
                                      in1=q_scr[:, off:off + 512],
                                      op=ALU.mult), "lr_ev", 1)
            for g4 in range(4):
                b.wait(v, "oa_mm", 4 * k + g4 + 1)
                b.inc(v.tensor_tensor(
                    out=xv[:, 4 * g4:4 * g4 + 4, k * P:(k + 1) * P],
                    in0=psA[0][:, 0:512].rearrange("p (j t) -> p j t", j=4),
                    in1=xv[:, 4 * g4:4 * g4 + 4, k * P:(k + 1) * P],
                    op=ALU.add), "oa_ev", 1)

    m32 = m_tok.bitcast(F32)
    xs_d = [m32[:, 0:D], m32[:, 5 * D:6 * D]]
    u2_d = [m32[:, 2 * D:3 * D], m32[:, 6 * D:7 * D]]
    sq_d = [m32[:, D:2 * D], m32[:, D:2 * D]]
    stage_scr = [m32[:, 3 * D:4 * D], m32[:, 4 * D:5 * D]]
    b2_bc = wvs_bc[:, 0:D]
    router_bc = q_scr[:, 0:D]
    mstat = m32[:, 7 * D:8 * D]
    cmp_scr = m32[:, 0:2 * D]
    gbuf = [m32[:, 0:D], m32[:, D:2 * D],
            m32[:, 3 * D:4 * D], m32[:, 4 * D:5 * D]]
    score_bc = m32[:, 3 * D:5 * D]
    sel_tok = xln_fm[:, 0:8 * D]
    sel_fm = xln_fm[:, 8 * D:16 * D]
    w1buf = wbufA
    w2buf = tmp_buf_.bitcast(BF16)[:, 0:8 * D]
    h1q = xb_buf_.bitcast(BF16)[:, 0:8 * D]
    augb = [rot_[:, 0:D], rot_[:, D:2 * D]]
    msc_score = mstat[:, 0:NT]
    msc_rank = mstat[:, NT:2 * NT]
    msc_mw = mstat[:, 2 * NT:3 * NT]
    msc_s0 = mstat[:, 3 * NT:4 * NT]
    msc_sumr = mstat[:, 4 * NT:4 * NT + 1]
    idx_tok = b.sb("idx_tok", (P, NT), I32)
    fb1_s = b.sb("fb1_s", (P, FF // P), F32)
    fb2row = prod_scr[0:1, 0:D]
    rrow = wvs_row
    srow = q_scr[0:1, 0:T]

    def bn_tile_hook(v, tt):
        v.drain()
        v.bn_stats(bn12[:, 12 * tt:12 * tt + 6], x_tok[:, tt * D:tt * D + 512])
        v.bn_stats(bn12[:, 12 * tt + 6:12 * tt + 12],
                   x_tok[:, tt * D + 512:(tt + 1) * D])

    def emit_mod(inv, post_tile=None):
        V = f"v{inv}_"
        selv = sel_dram.ap()
        updv = upd_dram.ap()

        @blk.gpsimd
        def _(g):
            (b.wait(g, "oa_ev") if inv == 0 else
             (b.wait(g, "ab_m_xadd", b.n("ab_m_xadd")),
              b.wait(g, "ab_pkadd", b.n("ab_pkadd"))))
            b.inc(g.dma_start(out=rrow[0:1, 0:D], in_=wD["f_router"]), "md_ldr", 16)
            b.inc(g.dma_start(
                out=w1buf[:, :].rearrange("p (k n) -> p k n", k=ND),
                in_=wD["fW1"].rearrange("(k p) n -> p k n", p=P)[:, :, 0:D]),
                "md_ldw1", 16)
            b.inc(g.dma_start(
                out=w2buf[:, :].rearrange("p (k n) -> p k n", k=ND),
                in_=wD["fW2"].rearrange("(k p) n -> p k n", p=P)[:, 0:ND, :]),
                "md_ldw2", 16)
            if inv == 0:
                b.inc(g.dma_start(out=fb1_s[:, :], in_=wD["fb1"]), "ld_fb1", 16)
            b.inc(g.dma_start(out=fb2row, in_=wD["fb2"]), "md_ldfb2", 16)

        @blk.vector
        def _(v):
            if inv == 0:
                b.wait(v, "oa_ev")
                v.drain()
                for tt in range(NT):
                    v.bn_stats(bn12[:, 12 * tt:12 * tt + 6],
                               x_tok[:, tt * D:tt * D + 512])
                    v.bn_stats(bn12[:, 12 * tt + 6:12 * tt + 12],
                               x_tok[:, tt * D + 512:(tt + 1) * D])
            v.drain()
            for tt in range(NT):
                ii = v.bn_aggr(ag2[:, 2 * tt:2 * tt + 2], bn12[:, 12 * tt:12 * tt + 12])
            b.inc(ii, V + "var", 1)

        @blk.scalar
        def _(s):
            b.wait(s, V + "var")
            s.activation(out=st["rstd"][:, :], in_=ag2[:, 1::2], func=AF.Ln,
                         bias=cS["eps_ln"][:, :])
            s.drain()
            b.inc(s.activation(out=st["rstd"][:, :], in_=st["rstd"][:, :],
                               func=AF.Exp, scale=-0.5), V + "rstd", 1)

        @blk.tensor
        def _(t):
            (b.wait(t, "oa_ev") if inv == 0 else
             (b.wait(t, "ab_m_xadd", b.n("ab_m_xadd")),
              b.wait(t, "ab_pkadd", b.n("ab_pkadd"))))
            b.wait(t, "md_ldr", 16)
            t.matmul(psA[0][:, 0:512], lhsT=cS["ones32row"][:, :],
                     rhs=rrow[0:1, 0:512], start=True, stop=True)
            ii = t.matmul(psA[1][:, 0:512], lhsT=cS["ones32row"][:, :],
                          rhs=rrow[0:1, 512:1024], start=True, stop=True)
            b.inc(ii, V + "rbc_mm", 1)

        @blk.vector
        def _(v):
            b.wait(v, V + "rbc_mm")
            v.tensor_copy(router_bc[:, 0:512], psA[0][:, 0:512])
            v.tensor_copy(router_bc[:, 512:1024], psA[1][:, 0:512])
            b.wait(v, "md_ldr", 16)
            v.drain()
            b.inc(v.tensor_reduce(out=swv_tmp[0:1, 0:1], in_=rrow[0:1, :],
                                  axis=AX, op=ALU.add), V + "sumr0", 1)

        @blk.tensor
        def _(t):
            b.wait(t, V + "sumr0")
            ii = t.matmul(psA[2][:, 0:1], lhsT=cS["ones32row"][:, :],
                          rhs=swv_tmp[0:1, 0:1], start=True, stop=True)
            b.inc(ii, V + "srbc_mm", 1)
            b.wait(t, "md_ldfb2", 16)
            t.matmul(psA[3][:, 0:512], lhsT=cS["ones32row"][:, :],
                     rhs=fb2row[0:1, 0:512], start=True, stop=True)
            ii = t.matmul(psA[4][:, 0:512], lhsT=cS["ones32row"][:, :],
                          rhs=fb2row[0:1, 512:1024], start=True, stop=True)
            b.inc(ii, V + "b2bc_mm", 1)

        @blk.vector
        def _(v):
            b.wait(v, V + "srbc_mm")
            b.inc(v.tensor_copy(msc_sumr[:, 0:1], psA[2][:, 0:1]), V + "srbc", 1)
            b.wait(v, V + "b2bc_mm")
            v.tensor_copy(b2_bc[:, 0:512], psA[3][:, 0:512])
            v.tensor_copy(b2_bc[:, 512:1024], psA[4][:, 0:512])
            v.drain()
            for tt in range(NT):
                jo = (tt % 2) * D
                ii = v.scalar_tensor_tensor(out=junk[:, jo:jo + D],
                                            in0=x_tok[:, tt * D:(tt + 1) * D],
                                            scalar=1.0, in1=router_bc[:, 0:D],
                                            op0=ALU.mult, op1=ALU.mult,
                                            accum_out=msc_s0[:, tt:tt + 1])
                if tt % 2 == 1:
                    v.drain()
            v.drain()
            b.wait(v, V + "rstd")
            v.tensor_scalar(out=msc_score[:, :], in0=ag2[:, 0::2],
                            scalar1=msc_sumr[:, 0:1], scalar2=None, op0=ALU.mult)
            v.drain()
            v.tensor_tensor(out=msc_score[:, :], in0=msc_s0[:, :],
                            in1=msc_score[:, :], op=ALU.subtract)
            v.drain()
            b.inc(v.tensor_tensor(out=msc_score[:, :], in0=msc_score[:, :],
                                  in1=st["rstd"][:, :], op=ALU.mult), V + "score", 1)

        @blk.tensor
        def _(t):
            b.wait(t, V + "score")
            for tt in range(NT):
                ii = t.matmul(psA[2 + tt // 4][0:1, (tt % 4) * P:(tt % 4 + 1) * P],
                              lhsT=msc_score[:, tt:tt + 1], rhs=cS["ident32"][:, :],
                              start=True, stop=True)
            b.inc(ii, V + "srow_mm", 1)

        @blk.vector
        def _(v):
            b.wait(v, V + "srow_mm")
            for hh in range(4):
                ii = v.tensor_copy(srow[0:1, hh * 512:(hh + 1) * 512],
                                   psA[2 + hh][0:1, 0:512])
            b.inc(ii, V + "srow", 1)

        @blk.tensor
        def _(t):
            b.wait(t, V + "srow")
            for hh in range(4):
                ii = t.matmul(psA[hh][:, 0:512], lhsT=cS["ones32row"][:, :],
                              rhs=srow[0:1, hh * 512:(hh + 1) * 512],
                              start=True, stop=True)
            b.inc(ii, V + "sbc_mm", 1)

        @blk.vector
        def _(v):
            b.wait(v, V + "sbc_mm")
            for hh in range(4):
                ii = v.tensor_copy(score_bc[:, hh * 512:(hh + 1) * 512],
                                   psA[hh][:, 0:512])
            v.drain()
            for tt in range(NT):
                ii = v.tensor_scalar(out=cmp_scr[:, 0:T], in0=score_bc[:, 0:T],
                                     scalar1=msc_score[:, tt:tt + 1], scalar2=None,
                                     op0=ALU.is_gt, op1=ALU.add,
                                     accum_out=msc_rank[:, tt:tt + 1])
                v.drain()
            v.tensor_copy(idx_tok[:, :], msc_rank[:, :])
            b.inc(v.tensor_scalar(out=msc_mw[:, :], in0=msc_rank[:, :],
                                  scalar1=float(T // 2), scalar2=None,
                                  op0=ALU.is_lt), V + "rank", 1)

        @blk.scalar
        def _(s):
            b.wait(s, V + "rank")
            b.inc(s.activation(out=msc_s0[:, :], in_=msc_score[:, :],
                               func=AF.Sigmoid), V + "sigw", 1)

        @blk.vector
        def _(v):
            b.wait(v, V + "sigw")
            b.inc(v.tensor_tensor(out=msc_mw[:, :], in0=msc_mw[:, :],
                                  in1=msc_s0[:, :], op=ALU.mult), V + "mw", 1)

        @blk.vector
        def _(v):
            b.wait(v, V + "rstd")
            v.tensor_tensor(out=st["nmr"][:, :], in0=ag2[:, 0::2],
                            in1=st["rstd"][:, :], op=ALU.mult)
            v.drain()
            b.inc(v.tensor_scalar(out=st["nmr"][:, :], in0=st["nmr"][:, :],
                                  scalar1=-1.0, scalar2=None, op0=ALU.mult),
                  V + "nmr", 1)

        @blk.scalar
        def _(s):
            b.wait(s, V + "nmr")
            for tt in range(NT):
                if tt >= 2:
                    b.wait(s, f"md_scat{(tt - 2) % 2}", 16 * ((tt - 2) // 2 + 1))
                b.inc(s.activation(out=augb[tt % 2], in_=x_tok[:, tt * D:(tt + 1) * D],
                                   func=AF.Identity, scale=st["rstd"][:, tt:tt + 1],
                                   bias=st["nmr"][:, tt:tt + 1]), V + "aug", 1)

        @blk.gpsimd
        def _(g):
            b.wait(g, V + "rank")
            for tt in range(NT):
                b.wait(g, V + "aug", tt + 1)
                b.inc(g.indirect_dma_start(
                    out=selv, out_offset=bass.IndirectOffsetOnAxis(
                        idx_tok[:, tt:tt + 1], axis=0),
                    in_=augb[tt % 2], in_offset=None,
                    bounds_check=bc_reg, oob_is_err=False),
                    f"md_scat{tt % 2}", 16)

        @blk.sync
        def _(s):
            b.wait(s, "md_scat0")
            b.wait(s, "md_scat1")
            b.inc(s.dma_start(
                out=sel_tok[:, :].rearrange("p (c f) -> p c f", c=8),
                in_=sel_dram.ap().rearrange("(c p) f -> p c f", p=P)),
                "md_selld", 16)

        @blk.tensor
        def _(t):
            b.wait(t, "md_selld", 16)
            for c in range(8):
                if c >= 1:
                    b.wait(t, V + "self_ev", 2 * c)
                for dt in range(ND):
                    ii = t.matmul(psA[4 + dt // 4][:, (dt % 4) * P:(dt % 4 + 1) * P],
                                  lhsT=sel_tok[:, c * D + dt * P:c * D + (dt + 1) * P],
                                  rhs=cS["ident16"][:, :], start=True, stop=True)
                b.inc(ii, V + "self_mm", 1)

        @blk.vector
        def _(v):
            sfm = sel_fm.rearrange("p (dt r) -> p dt r", dt=ND)
            for c in range(8):
                b.wait(v, V + "self_mm", c + 1)
                for h in range(2):
                    b.inc(v.tensor_copy(
                        sfm[:, 4 * h:4 * h + 4, c * P:(c + 1) * P],
                        psA[4 + h][:, 0:512].rearrange("p (dt r) -> p dt r", dt=4)),
                        V + "self_ev", 1)

        NQ4 = FF // D

        @blk.gpsimd
        def _(g):
            for q in range(NQ4):
                if q + 1 < NQ4:
                    b.wait(g, V + "h1_mm", 8 * (q + 1))
                    b.inc(g.dma_start(
                        out=w1buf[:, :].rearrange("p (k n) -> p k n", k=ND),
                        in_=wD["fW1"].rearrange("(k p) n -> p k n", p=P)[:, :, (q + 1) * D:(q + 2) * D]),
                        "md_ldw1", 16)
                for rt in range(8):
                    gi = 8 * q + rt
                    par = rt % 2
                    b.wait(g, V + "h2_ev", 2 * (gi + 1))
                    if q >= 1:
                        uprev = (8 * (q - 1) + rt - par) // 2
                        b.wait(g, f"md_upda{par}", 16 * (uprev + 1))
                    if q > 0:
                        ii = g.dma_start(out=updv[rt * P:(rt + 1) * P, :],
                                         in_=stage_scr[par], accum_op=ALU.add)
                    else:
                        ii = g.dma_start(out=updv[rt * P:(rt + 1) * P, :],
                                         in_=stage_scr[par])
                    b.inc(ii, f"md_upda{par}", 16)
                if q + 1 < NQ4:
                    b.wait(g, V + "h2_mm", 8 * (q + 1))
                    b.inc(g.dma_start(
                        out=w2buf[:, :].rearrange("p (k n) -> p k n", k=ND),
                        in_=wD["fW2"].rearrange("(k p) n -> p k n", p=P)[:, (q + 1) * ND:(q + 2) * ND, :]),
                        "md_ldw2", 16)

        @blk.tensor
        def _(t):
            b.wait(t, V + "self_ev")
            if inv == 0:
                b.wait(t, "car_ev")
            for q in range(NQ4):
                b.wait(t, "md_ldw1", 16 * (q + 1))
                for ft in range(8):
                    gi = 8 * q + ft
                    if gi >= 2:
                        b.wait(t, V + "gx", gi - 1)
                    pr = [psA[0], psA[1]] if gi % 2 == 0 else [psA[2], psA[3]]
                    for h in range(2):
                        for dt in range(ND):
                            ii = t.matmul(pr[h][:, 0:512],
                                          lhsT=w1buf[:, dt * D + ft * P:dt * D + (ft + 1) * P],
                                          rhs=sel_fm[:, dt * D + h * 512:dt * D + (h + 1) * 512],
                                          start=(dt == 0), stop=(dt == ND - 1))
                    b.inc(ii, V + "h1_mm", 1)
                b.wait(t, "md_ldw2", 16 * (q + 1))
                b.wait(t, V + "gelu", 8 * (q + 1))
                for rt in range(8):
                    gi2 = 8 * q + rt
                    if gi2 >= 2:
                        b.wait(t, V + "h2_ev", 2 * (gi2 - 1))
                    ph2 = ([psA[4][:, 0:512], psA[5][:, 0:512]] if rt % 2 == 0
                           else [psR[:, 0:512], psR[:, 512:1024]])
                    for h in range(2):
                        for ff in range(8):
                            ii = t.matmul(ph2[h],
                                          lhsT=h1q[:, ff * D + rt * P:ff * D + (rt + 1) * P],
                                          rhs=w2buf[:, ff * D + h * 512:ff * D + (h + 1) * 512],
                                          start=(ff == 0), stop=(ff == 7))
                    b.inc(ii, V + "h2_mm", 1)

        @blk.scalar
        def _(s):
            b.wait(s, "ld_fb1", 16)
            for q in range(NQ4):
                for ft in range(8):
                    gi = 8 * q + ft
                    par = gi % 2
                    b.wait(s, V + "h1_mm", gi + 1)
                    if gi >= 2:
                        b.wait(s, V + "gelu", gi - 1)
                    pr = [psA[0], psA[1]] if par == 0 else [psA[2], psA[3]]
                    b1c = fb1_s[:, q * 8 + ft:q * 8 + ft + 1]
                    s.activation(out=xs_d[par][:, 0:512], in_=pr[0][:, 0:512],
                                 func=AF.Identity, bias=b1c)
                    s.activation(out=xs_d[par][:, 512:1024], in_=pr[1][:, 0:512],
                                 func=AF.Identity, bias=b1c)
                    s.activation(out=sq_d[par][:, 0:512], in_=pr[0][:, 0:512],
                                 func=AF.Square, bias=b1c)
                    ii = s.activation(out=sq_d[par][:, 512:1024], in_=pr[1][:, 0:512],
                                      func=AF.Square, bias=b1c)
                    b.inc(ii, V + "gx", 1)
                    b.wait(s, V + "gu2", gi + 1)
                    ii = s.activation(out=u2_d[par][:, :], in_=u2_d[par][:, :],
                                      func=AF.Sigmoid)
                    b.inc(ii, V + "gsg", 1)

        @blk.vector
        def _(v):
            for q in range(NQ4):
                for ft in range(8):
                    gi = 8 * q + ft
                    par = gi % 2
                    b.wait(v, V + "gx", gi + 1)
                    if gi >= 2:
                        b.wait(v, V + "gsg", gi - 1)
                    v.tensor_scalar(out=sq_d[par][:, :], in0=sq_d[par][:, :],
                                    scalar1=GELU_A * GELU_C, scalar2=GELU_A,
                                    op0=ALU.mult, op1=ALU.add)
                    v.drain()
                    b.inc(v.tensor_tensor(out=u2_d[par][:, :], in0=sq_d[par][:, :],
                                          in1=xs_d[par][:, :], op=ALU.mult), V + "gu2", 1)
                    b.wait(v, V + "gsg", gi + 1)
                    b.inc(v.tensor_tensor(out=h1q[:, ft * D:(ft + 1) * D],
                                          in0=u2_d[par][:, :], in1=xs_d[par][:, :],
                                          op=ALU.mult), V + "gelu", 1)
                for rt in range(8):
                    gi = 8 * q + rt
                    b.wait(v, V + "h2_mm", gi + 1)
                    par = rt % 2
                    u = (gi - par) // 2
                    if u >= 1:
                        b.wait(v, f"md_upda{par}", 16 * u)
                    ph2 = ([psA[4][:, 0:512], psA[5][:, 0:512]] if rt % 2 == 0
                           else [psR[:, 0:512], psR[:, 512:1024]])
                    if q == 0:
                        v.tensor_tensor(out=stage_scr[par][:, 0:512],
                                        in0=ph2[0], in1=b2_bc[:, 0:512],
                                        op=ALU.add)
                        ii = v.tensor_tensor(out=stage_scr[par][:, 512:1024],
                                             in0=ph2[1],
                                             in1=b2_bc[:, 512:1024], op=ALU.add)
                    else:
                        v.tensor_copy(stage_scr[par][:, 0:512], ph2[0])
                        ii = v.tensor_copy(stage_scr[par][:, 512:1024], ph2[1])
                    b.inc(ii, V + "h2_ev", 2)

        @blk.gpsimd
        def _(g):
            b.wait(g, "md_upda0")
            b.wait(g, "md_upda1")
            b.wait(g, V + "gz", 1)
            for tt in range(NT):
                if tt >= 4:
                    b.wait(g, V + "xadd", tt - 3)
                b.inc(g.indirect_dma_start(
                    out=gbuf[tt % 4], out_offset=None,
                    in_=updv, in_offset=bass.IndirectOffsetOnAxis(
                        idx_tok[:, tt:tt + 1], axis=0),
                    bounds_check=bc_reg, oob_is_err=False),
                    f"md_gat{tt % 4}", 16)

        @blk.vector
        def _(v):
            b.wait(v, V + "mw")
            b.wait(v, "md_upda0")
            b.wait(v, "md_upda1")
            v.drain()
            for s4 in range(4):
                ii = v.memset(gbuf[s4], 0.0)
            b.inc(ii, V + "gz", 1)
            v.drain()
            for tt in range(NT):
                b.wait(v, f"md_gat{tt % 4}", 16 * (tt // 4 + 1))
                v.drain()
                ii = v.scalar_tensor_tensor(
                    out=x_tok[:, tt * D:(tt + 1) * D],
                    in0=gbuf[tt % 4], scalar=msc_mw[:, tt:tt + 1],
                    in1=x_tok[:, tt * D:(tt + 1) * D],
                    op0=ALU.mult, op1=ALU.add)
                b.inc(ii, V + "xadd", 1)
                if post_tile is not None:
                    post_tile(v, tt)

    b.free("ld_c", "ld_wv", "ld_wk", "ld_wq", "ld_x", "wvs", "wvs_tr", "wvs_row",
           "swv0", "bc_mm", "bc_done", "own_var", "own_rstd", "own_nmr", "xln_em",
           "xln_tr", "xln_ev", "carry0", "y_var", "y_rstd", "y_ab", "y_gadd",
           "xb_fv", "xb_fs",
           "xb_ld0", "xb_ld1", "yacc0", "yacc1", "ybk_ld0", "ybk_ld1", "ytr_mm",
           "ytr_ev", "m_mm", "m_ev", "q_mm", "q_ev", "car_mm", "lr_mm",
           "lr_ev", "oa_mm")
    if N_SUB >= 2:
        emit_mod(0, post_tile=bn_tile_hook if N_SUB >= 3 else None)

    NCK = T // C2
    xfm16 = xln_fm
    S32 = m_tok.bitcast(F32)
    S16 = tmp_buf_.bitcast(BF16)
    xb16 = xb_buf_.bitcast(BF16)
    ktokP = [xfm16[:, 0:2048], xfm16[:, 2048:4096]]
    qP = [xfm16[:, 4096:6144], xfm16[:, 6144:8192]]
    vfmP = [xfm16[:, 8192:10240], xfm16[:, 10240:12288]]
    kfm_c = xfm16[:, 12288:14336]
    vtok_c = xfm16[:, 14336:16384]
    T2st = [xb16[:, 512:1024], xb16[:, 7168:7680]]
    A2st = [xb16[:, 1024:1536], xb16[:, 6656:7168]]
    Pnat = [xb16[:, 1536:2048], xb16[:, 3072:3584]]
    PnatT = [xb16[:, 2048:2560], xb16[:, 3584:4096]]
    Mst = [xb16[:, 2560:3072], xb16[:, 6144:6656]]
    kfmS = [xfm16[:, 0:2048], xfm16[:, 2048:4096]]
    ktokS = xfm16[:, 4096:6144]
    qS = xfm16[:, 6144:8192]
    vtokS = xfm16[:, 8192:10240]
    W_c = xfm16[:, 10240:12288]
    O_c = xfm16[:, 12288:14336]
    U_c = xfm16[:, 14336:16384]
    Mld = [xb16[:, 4096:4608], xb16[:, 4608:5120]]
    T2ld = [xb16[:, 5120:5632], xb16[:, 5632:6144]]
    av_scr = xb_buf_[:, 0:2048]
    btw16 = xb16[:, 8100:8108]
    bbc_scr = xb_buf_[:, 2048:4096]
    qkst = [m_tok[:, 0:2048], m_tok[:, 2048:4096]]
    bstat = yst["vsum"]
    ab_beta = bstat[:, 0:NT]
    ab_eb = bstat[:, 16:24]
    ab_asig = bstat[:, 24:32]
    ab_rstd = bstat[:, 32:48]
    ab_ms = bstat[:, 48:64]
    ab_ss = bstat[:, 64:96]
    ab_bb = bstat[:, 96:97]
    ab_q2 = bstat[:, 97:113]
    ab_nb = bstat[:, 113:129] if False else bstat[:, 0:0]
    ab_nb = yag[:, 32:48]
    ab_s1 = yag[:, 48:64]
    ab_e1 = yag[:, 64:80]
    n1w = yag[:, 0:8]
    n2w = yag[:, 8:16]
    alph = yag[:, 16:24]
    btw = yag[:, 24:32]

    def scale_rows_by(v, buf, gcol, nblk, width, sem):
        for dt in range(nblk):
            ii = v.tensor_scalar(out=buf[:, dt * width:(dt + 1) * width],
                                 in0=buf[:, dt * width:(dt + 1) * width],
                                 scalar1=gcol[:, dt:dt + 1], scalar2=None,
                                 op0=ALU.mult)
        v.drain()
        b.inc(v.tensor_copy(buf[0:1, 0:1], buf[0:1, 0:1]), sem, 1)

    def emit_attnb():
        V = "ab_"

        @blk.gpsimd
        def _(g):
            b.wait(g, "v0_xadd", b.n("v0_xadd"))
            b.inc(g.dma_start(out=n1w, in_=wD["b_n1"]), V + "vecs", 16)
            b.inc(g.dma_start(out=n2w, in_=wD["b_n2"]), V + "vecs", 16)
            b.inc(g.dma_start(out=alph, in_=wD["b_alpha"]), V + "vecs", 16)
            b.inc(g.dma_start(out=btw, in_=wD["b_betaw"]), V + "vecs", 16)
            b.inc(g.dma_start(out=swv_tmp[0:1, 0:1], in_=wD["b_betab"]), V + "vecs", 16)
            b.inc(g.dma_start(
                out=wbufA[:, :].rearrange("p (k n) -> p k n", k=ND),
                in_=wD["bWq"].rearrange("(k p) n -> p k n", p=P)), V + "wq_ld", 16)

        @blk.vector
        def _(v):
            b.wait(v, "v0_xadd", b.n("v0_xadd"))
            v.drain()
            for tt in range(NT):
                ii = v.bn_aggr(ag2[:, 2 * tt:2 * tt + 2], bn12[:, 12 * tt:12 * tt + 12])
            v.drain()
            b.inc(v.bn_aggr(yag[:, 254:256] if False else ag2[:, 0:2],
                            bn12[:, 0:12]) if False else
                  v.tensor_copy(ag2[0:1, 0:1], ag2[0:1, 0:1]), V + "ms", 1)
            b.wait(v, V + "vecs", 80)
            b.wait(v, V + "wq_ld", 16)
            scale_rows_by(v, wbufA, n1w, ND, D, V + "wq")

        @blk.scalar
        def _(s):
            b.wait(s, V + "ms")
            s.activation(out=ab_ms[:, :], in_=ag2[:, 1::2], func=AF.Ln,
                         bias=cS["eps_ln"][:, :])
            s.drain()
            s.activation(out=ab_s1[:, :], in_=ab_ms[:, :], func=AF.Exp, scale=-0.5)
            ii = s.activation(out=ab_e1[:, :], in_=ab_ms[:, :], func=AF.Exp, scale=-1.0)
            b.inc(ii, V + "s1", 1)
            b.wait(s, V + "q2v", 1)
            s.drain()
            s.activation(out=ab_q2[:, :], in_=ab_q2[:, :], func=AF.Ln,
                         bias=cS["eps_rms"][:, :])
            s.drain()
            b.inc(s.activation(out=ab_rstd[:, :], in_=ab_q2[:, :], func=AF.Exp,
                               scale=-0.5), V + "rstd", 1)
            b.wait(s, V + "vecs", 80)
            b.inc(s.activation(out=ab_asig[:, :], in_=alph, func=AF.Sigmoid),
                  V + "asig", 1)
            b.wait(s, V + "rstd")
            b.wait(s, V + "q2v", 1)
            s.drain()
            asc = yag[:, 80:96]
            bsc = yag[:, 96:112]
            for tt in range(NT):
                s.activation(out=asc[:, tt:tt + 1], in_=ab_s1[:, tt:tt + 1],
                             func=AF.Identity, scale=ab_rstd[:, tt:tt + 1])
                s.activation(out=bsc[:, tt:tt + 1], in_=ab_nb[:, tt:tt + 1],
                             func=AF.Identity, scale=ab_rstd[:, tt:tt + 1])
            s.drain()
            for tt in range(NT):
                if tt >= 2:
                    b.wait(s, V + "hev", 2 * (tt - 1))
                b.wait(s, "v0_xadd", tt + 1)
                b.inc(s.activation(out=xln_rot[tt % 2], in_=x_tok[:, tt * D:(tt + 1) * D],
                                   func=AF.Identity, scale=asc[:, tt:tt + 1],
                                   bias=bsc[:, tt:tt + 1]),
                      V + "hem", 1)

        @blk.gpsimd
        def _(g):
            xv_ = x_tok[:, :].rearrange("p (tt d) -> p tt d", tt=NT)
            yv_ = y_dram.ap().rearrange("(tt p) d -> p tt d", p=P)
            for j in range(4):
                b.wait(g, "v0_xadd", 4 * (j + 1))
                if j >= 1:
                    b.wait(g, V + "park", 16 * j)
                b.inc(g.dma_start(out=yv_[:, 4 * j:4 * j + 4, :],
                                  in_=xv_[:, 4 * j:4 * j + 4, :]),
                      V + "park", 16)

        @blk.vector
        def _(v):
            b.wait(v, V + "s1", 1)
            v.drain()
            v.tensor_tensor(out=ab_q2[:, :], in0=ag2[:, 1::2], in1=ab_e1[:, :],
                            op=ALU.mult)
            v.tensor_tensor(out=ab_nb[:, :], in0=ag2[:, 0::2], in1=ab_s1[:, :],
                            op=ALU.mult)
            v.drain()
            ii = v.tensor_scalar(out=ab_nb[:, :], in0=ab_nb[:, :], scalar1=-1.0,
                                 scalar2=None, op0=ALU.mult)
            b.inc(ii, V + "q2v", 1)
            v.drain()

        @blk.vector
        def _(v):
            b.wait(v, V + "asig")
            v.drain()
            b.inc(v.tensor_scalar(out=ab_eb[:, :], in0=ab_asig[:, :], scalar1=-1.0,
                                  scalar2=1.0, op0=ALU.mult, op1=ALU.add), V + "eb", 1)

        @blk.tensor
        def _(t):
            for tt in range(NT):
                b.wait(t, V + "hem", tt + 1)
                if tt >= 1:
                    b.wait(t, V + "hev", 2 * tt)
                for dt in range(ND):
                    ii = t.matmul(psA[4 + dt // 4][:, (dt % 4) * P:(dt % 4 + 1) * P],
                                  lhsT=xln_rot[tt % 2][:, dt * P:(dt + 1) * P],
                                  rhs=cS["ident16"][:, :], start=True, stop=True)
                b.inc(ii, V + "htr", 1)

        @blk.vector
        def _(v):
            hfm = xfm16[:, :].rearrange("p (dt t) -> p dt t", dt=ND)
            for tt in range(NT):
                b.wait(v, V + "htr", tt + 1)
                for h in range(2):
                    b.inc(v.tensor_copy(
                        hfm[:, 4 * h:4 * h + 4, tt * P:(tt + 1) * P],
                        psA[4 + h][:, 0:512].rearrange("p (dt t) -> p dt t", dt=4)),
                        V + "hev", 1)
                b.wait(v, V + "park", 16 * (tt // 4 + 1))
                if tt == 0:
                    v.drain()
                b.inc(v.tensor_scalar(out=x_tok[:, tt * D:(tt + 1) * D],
                                      in0=x_tok[:, tt * D:(tt + 1) * D],
                                      scalar1=ab_s1[:, tt:tt + 1],
                                      scalar2=ab_nb[:, tt:tt + 1],
                                      op0=ALU.mult, op1=ALU.add), V + "xl", 1)

        @blk.tensor
        def _(t):
            b.wait(t, V + "hev", 2 * NT)
            b.wait(t, V + "wq")
            for k in range(ND):
                if k >= 1:
                    b.wait(t, V + "qev", k)
                if k >= 2:
                    b.wait(t, V + f"qst{k % 2}", 16 * ((k - 2) // 2 + 1))
                for nq in range(4):
                    for dt in range(ND):
                        ii = t.matmul(psA[nq][:, 0:512],
                                      lhsT=wbufA[:, dt * D + k * P: dt * D + (k + 1) * P],
                                      rhs=xfm16[:, dt * T + nq * 512: dt * T + (nq + 1) * 512],
                                      start=(dt == 0), stop=(dt == ND - 1))
                b.inc(ii, V + "qmm", 1)

        @blk.vector
        def _(v):
            for k in range(ND):
                b.wait(v, V + "qmm", k + 1)
                for nq in range(4):
                    ii = v.tensor_copy(qkst[k % 2][:, nq * 512:(nq + 1) * 512],
                                       psA[nq][:, 0:512])
                b.inc(ii, V + "qev", 1)

        @blk.sync
        def _(s):
            for k in range(ND):
                b.wait(s, V + "qev", k + 1)
                b.inc(s.dma_start(out=q_dram[k * P:(k + 1) * P, :],
                                  in_=qkst[k % 2]), V + f"qst{k % 2}", 16)

        @blk.gpsimd
        def _(g):
            b.wait(g, V + "qmm", ND)
            b.inc(g.dma_start(
                out=wbufA[:, :].rearrange("p (k n) -> p k n", k=ND),
                in_=wD["bWk"].rearrange("(k p) n -> p k n", p=P)), V + "wk_ld", 16)

        @blk.vector
        def _(v):
            b.wait(v, V + "wk_ld", 16)
            b.wait(v, V + "qev", ND)
            scale_rows_by(v, wbufA, n1w, ND, D, V + "wk")

        @blk.tensor
        def _(t):
            b.wait(t, V + "wk")
            for tt in range(NT):
                if tt >= 1:
                    b.wait(t, V + "kev", tt)
                if tt >= 2:
                    b.wait(t, V + f"kst{tt % 2}", 16 * ((tt - 2) // 2 + 1))
                for h in range(2):
                    for dt in range(ND):
                        ii = t.matmul(psA[h][:, 0:512],
                                      lhsT=xfm16[:, dt * T + tt * P:dt * T + (tt + 1) * P],
                                      rhs=wbufA[:, dt * D + h * 512:dt * D + (h + 1) * 512],
                                      start=(dt == 0), stop=(dt == ND - 1))
                b.inc(ii, V + "kmm", 1)

        @blk.scalar
        def _(s):
            for tt in range(NT):
                b.wait(s, V + "kmm", tt + 1)
                s.activation(out=av_scr[:, 0:512], in_=psA[0][:, 0:512], func=AF.Square,
                             accum_out=ab_ss[:, 2 * tt:2 * tt + 1])
                ii = s.activation(out=av_scr[:, 512:1024], in_=psA[1][:, 0:512],
                                  func=AF.Square, accum_out=ab_ss[:, 2 * tt + 1:2 * tt + 2])
                b.inc(ii, V + "kss", 1)
                b.wait(s, V + "kssm", tt + 1)
                s.drain()
                s.activation(out=ab_ss[:, 2 * tt:2 * tt + 1],
                             in_=ab_ss[:, 2 * tt:2 * tt + 1], func=AF.Ln)
                s.drain()
                ii = s.activation(out=ab_ss[:, 2 * tt:2 * tt + 1],
                                  in_=ab_ss[:, 2 * tt:2 * tt + 1], func=AF.Exp,
                                  scale=-0.5)
                b.inc(ii, V + "krec", 1)

        @blk.vector
        def _(v):
            b.wait(v, V + "qst0")
            b.wait(v, V + "qst1")
            for tt in range(NT):
                b.wait(v, V + "kss", tt + 1)
                v.drain()
                v.tensor_tensor(out=ab_ss[:, 2 * tt:2 * tt + 1],
                                in0=ab_ss[:, 2 * tt:2 * tt + 1],
                                in1=ab_ss[:, 2 * tt + 1:2 * tt + 2], op=ALU.add)
                v.drain()
                b.inc(v.tensor_scalar(out=ab_ss[:, 2 * tt:2 * tt + 1],
                                      in0=ab_ss[:, 2 * tt:2 * tt + 1],
                                      scalar1=1e-24, scalar2=None, op0=ALU.max),
                      V + "kssm", 1)
                b.wait(v, V + "krec", tt + 1)
                v.drain()
                for h in range(2):
                    ii = v.tensor_scalar(
                        out=qkst[tt % 2][:, h * 512:(h + 1) * 512],
                        in0=psA[h][:, 0:512], scalar1=ab_ss[:, 2 * tt:2 * tt + 1],
                        scalar2=None, op0=ALU.mult)
                b.inc(ii, V + "kev", 1)

        @blk.sync
        def _(s):
            for tt in range(NT):
                b.wait(s, V + "kev", tt + 1)
                b.inc(s.dma_start(out=k_dram[tt * P:(tt + 1) * P, :],
                                  in_=qkst[tt % 2][:, 0:1024]), V + f"kst{tt % 2}", 16)

        @blk.vector
        def _(v):
            b.wait(v, V + "vecs", 80)
            v.drain()
            b.inc(v.tensor_tensor(out=btw16, in0=btw, in1=n1w, op=ALU.mult), V + "btwn", 1)

        @blk.tensor
        def _(t):
            b.wait(t, V + "kev", NT)
            b.wait(t, V + "btwn")
            for tt in range(NT):
                for dt in range(ND):
                    ii = t.matmul(psA[2][:, tt:tt + 1],
                                  lhsT=xfm16[:, dt * T + tt * P:dt * T + (tt + 1) * P],
                                  rhs=btw16[:, dt:dt + 1],
                                  start=(dt == 0), stop=(dt == ND - 1))
            b.inc(ii, V + "bmm", 1)
            ii = t.matmul(psA[3][:, 0:1], lhsT=cS["ones32row"][:, :],
                          rhs=swv_tmp[0:1, 0:1], start=True, stop=True)
            b.inc(ii, V + "bbmm", 1)

        @blk.vector
        def _(v):
            b.wait(v, V + "bbmm")
            v.tensor_copy(ab_bb[:, 0:1], psA[3][:, 0:1])
            v.drain()
            ii = v.tensor_scalar(out=ab_beta[:, :], in0=psA[2][:, 0:NT],
                                 scalar1=ab_bb[:, 0:1], scalar2=None, op0=ALU.add)
            b.inc(ii, V + "bev", 1)

        @blk.scalar
        def _(s):
            b.wait(s, V + "bev")
            s.drain()
            b.inc(s.activation(out=ab_beta[:, :], in_=ab_beta[:, :], func=AF.Sigmoid),
                  V + "bsig", 1)

        @blk.gpsimd
        def _(g):
            b.wait(g, V + "kmm", NT)
            b.inc(g.dma_start(
                out=wbufA[:, :].rearrange("p (k n) -> p k n", k=ND),
                in_=wD["bWv"].rearrange("(k p) n -> p k n", p=P)), V + "wv_ld", 16)

        @blk.vector
        def _(v):
            b.wait(v, V + "wv_ld", 16)
            b.wait(v, V + "bev")
            scale_rows_by(v, wbufA, n1w, ND, D, V + "wv")

        @blk.tensor
        def _(t):
            b.wait(t, V + "wv")
            b.wait(t, V + "bsig")
            for k in range(ND):
                if k >= 1:
                    b.wait(t, V + "vscan", k)
                if k >= 2:
                    b.wait(t, V + f"vst{k % 2}", 16 * ((k - 2) // 2 + 1))
                for nq in range(4):
                    for dt in range(ND):
                        ii = t.matmul(psA[nq][:, 0:512],
                                      lhsT=wbufA[:, dt * D + k * P: dt * D + (k + 1) * P],
                                      rhs=xfm16[:, dt * T + nq * 512: dt * T + (nq + 1) * 512],
                                      start=(dt == 0), stop=(dt == ND - 1))
                b.inc(ii, V + "vmm", 1)

        @blk.vector
        def _(v):
            b.wait(v, V + "eb")
            b.wait(v, V + "kst0")
            b.wait(v, V + "kst1")
            for k in range(ND):
                b.wait(v, V + "vmm", k + 1)
                v.drain()
                v.memset(bbc_scr[:, :], 1.0)
                v.drain()
                v.tensor_scalar(out=bbc_scr[:, :], in0=bbc_scr[:, :],
                                scalar1=ab_eb[:, k:k + 1], scalar2=None, op0=ALU.mult)
                for nq in range(4):
                    v.tensor_scalar(out=av_scr[:, nq * 512:(nq + 1) * 512],
                                    in0=psA[nq][:, 0:512],
                                    scalar1=ab_asig[:, k:k + 1], scalar2=None,
                                    op0=ALU.mult)
                v.drain()
                ii = v.tensor_tensor_scan(out=qkst[k % 2], data0=bbc_scr[:, :],
                                          data1=av_scr[:, :], initial=0.0,
                                          op0=ALU.mult, op1=ALU.add)
                b.inc(ii, V + "vscan", 1)

        @blk.sync
        def _(s):
            for k in range(ND):
                b.wait(s, V + "vscan", k + 1)
                b.inc(s.dma_start(out=vema_dram[k * P:(k + 1) * P, :],
                                  in_=qkst[k % 2]), V + f"vst{k % 2}", 16)

        @blk.gpsimd
        def _(g):
            b.wait(g, V + "vmm", ND)
            b.inc(g.dma_start(
                out=wbufA[:, :].rearrange("p (k n) -> p k n", k=ND),
                in_=wD["bWo"].rearrange("(k p) n -> p k n", p=P)), V + "wo_ld", 16)

        b.free("ab_hem", "ab_htr", "ab_hev", "ab_qmm", "ab_qev", "ab_kmm",
               "ab_kss", "ab_kssm", "ab_krec", "ab_kev", "ab_bmm", "ab_bbmm",
               "ab_bev", "ab_btwn", "ab_bsig", "ab_vmm", "ab_vscan", "ab_ms",
               "ab_rstd", "ab_asig", "ab_eb", "ab_wq", "ab_wk", "ab_wv")
        NLEV = 6

        selv_b = sel_dram.ap()
        kvscr = kv_scr.ap()

        @blk.sync
        def _(s):
            b.wait(s, V + "qst0", 16 * (ND // 2))
            b.wait(s, V + "qst1", 16 * (ND // 2))
            b.wait(s, V + "kst0", 16 * (NT // 2))
            b.wait(s, V + "kst1", 16 * (NT // 2))
            b.wait(s, V + "vst0", 16 * (ND // 2))
            b.wait(s, V + "vst1", 16 * (ND // 2))
            for k in range(NCK):
                if k >= 2:
                    b.wait(s, V + "dKtr", 2 * (k - 1))
                    b.wait(s, V + "dGmm", 2 * (k - 1))
                    b.wait(s, V + "dVtr", 2 * (k - 1))
                if k >= 1:
                    b.wait(s, V + "dldk", 16 * k)
                    b.wait(s, V + "dldq", 16 * k)
                    b.wait(s, V + "dldv", 16 * k)
                b.inc(s.dma_start(
                    out=ktokP[k % 2].rearrange("p (tm d) -> p tm d", tm=2),
                    in_=k_dram[k * C2:(k + 1) * C2, :].rearrange("(tm p) d -> p tm d", p=P)),
                    V + "dldk", 16)
                b.inc(s.dma_start(
                    out=qP[k % 2].rearrange("p (dt t) -> p dt t", dt=ND),
                    in_=q_dram[:, k * C2:(k + 1) * C2].rearrange("(dt p) t -> p dt t", p=P)),
                    V + "dldq", 16)
                b.inc(s.dma_start(
                    out=vfmP[k % 2].rearrange("p (dt t) -> p dt t", dt=ND),
                    in_=vema_dram[:, k * C2:(k + 1) * C2].rearrange("(dt p) t -> p dt t", p=P)),
                    V + "dldv", 16)

        @blk.gpsimd
        def _(g):
            for k in range(NCK):
                b.wait(g, V + "dKtr_ev", 2 * (k + 1))
                if k >= 1:
                    b.wait(g, "ab_okf", 16 * k)
                    b.wait(g, "ab_ovt", 16 * k)
                    b.wait(g, "ab_ot2", 16 * k)
                b.inc(g.dma_start(out=kvscr[k * P:(k + 1) * P, :],
                                  in_=kfm_c[:, :]), "ab_okf", 16)
                b.wait(g, V + "dVtr_ev", 2 * (k + 1))
                b.inc(g.dma_start(out=kvscr[T + k * P:T + (k + 1) * P, :],
                                  in_=vtok_c[:, :]), "ab_ovt", 16)
                b.wait(g, V + "dT2", k + 1)
                b.inc(g.dma_start(out=selv_b[k * P:(k + 1) * P, 512:1024],
                                  in_=T2st[k % 2][:, :]), "ab_ot2", 16)
                if k >= 2:
                    b.wait(g, V + "dMdone", k - 1)
                    b.wait(g, "ab_omt", 16 * (k - 2))
                    b.inc(g.dma_start(out=selv_b[(k - 2) * P:(k - 1) * P, 0:512],
                                      in_=Mst[k % 2][:, :]), "ab_omt", 16)
            for j in (NCK - 2, NCK - 1):
                b.wait(g, V + "dMdone", j + 1)
                b.wait(g, "ab_omt", 16 * j)
                b.inc(g.dma_start(out=selv_b[j * P:(j + 1) * P, 0:512],
                                  in_=Mst[j % 2][:, :]), "ab_omt", 16)

        chA = [psA[2][:, 0:512], psA[5][:, 0:512]]
        chB = [psA[3][:, 0:512], psR[:, 0:512]]
        chC = [psA[4][:, 0:512], psR[:, 512:1024]]

        @blk.tensor
        def _(t):
            def lev_piece(j, lev):
                p = j % 2
                Lp = j // 2
                sA = str(p)
                if lev == 0:
                    b.wait(t, V + "dMl" + sA, 7 * Lp + 1)
                else:
                    b.wait(t, V + "dPev" + sA, NLEV * Lp + lev)
                    b.wait(t, V + "dPTev" + sA, NLEV * Lp + lev)
                for a_ in range(2):
                    for m_ in range(2):
                        ii = t.matmul(chA[p][:, a_ * C2:(a_ + 1) * C2],
                                      lhsT=PnatT[p][:, m_ * C2 + a_ * P:m_ * C2 + (a_ + 1) * P],
                                      rhs=Pnat[p][:, m_ * C2:(m_ + 1) * C2],
                                      start=(m_ == 0), stop=(m_ == 1))
                b.inc(ii, V + "dPmmA" + sA, 1)
                for a_ in range(2):
                    for m_ in range(2):
                        ii = t.matmul(chB[p][:, a_ * C2:(a_ + 1) * C2],
                                      lhsT=Pnat[p][:, m_ * C2 + a_ * P:m_ * C2 + (a_ + 1) * P],
                                      rhs=PnatT[p][:, m_ * C2:(m_ + 1) * C2],
                                      start=(m_ == 0), stop=(m_ == 1))
                b.inc(ii, V + "dPmmB" + sA, 1)
                b.wait(t, V + "dPev" + sA, NLEV * Lp + lev + 1)
                b.wait(t, V + "dMuev" + sA, NLEV * Lp + lev)
                for jm in range(2):
                    for m_ in range(2):
                        ii = t.matmul(chC[p][:, jm * C2:(jm + 1) * C2],
                                      lhsT=Pnat[p][:, m_ * C2 + jm * P:m_ * C2 + (jm + 1) * P],
                                      rhs=Mst[p][:, m_ * C2:(m_ + 1) * C2],
                                      start=(m_ == 0), stop=(m_ == 1))
                b.inc(ii, V + "dMu_mm" + sA, 1)

            for k in range(NCK + 2):
                PRE = k < NCK
                jA, jB = k - 1, k - 2

                def piece(pi, k=k):
                    kp = k % 2
                    if pi == 0:
                        b.wait(t, V + "dldk", 16 * (k + 1))
                        if k >= 1:
                            b.wait(t, V + "dMl" + str((k - 1) % 2),
                                   7 * ((k - 1) // 2) + 1)
                        for dt in range(ND):
                            ii = t.matmul(psA[dt // 4][:, (dt % 4) * P:(dt % 4 + 1) * P],
                                          lhsT=ktokP[kp][:, dt * P:(dt + 1) * P],
                                          rhs=cS["ident16"][:, :], start=True, stop=True)
                        b.inc(ii, V + "dKtr", 1)
                    elif pi == 1:
                        b.wait(t, V + "dKtr_ev", 2 * k + 1)
                        for dt in range(ND):
                            ii = t.matmul(psA[dt // 4][:, (dt % 4) * P:(dt % 4 + 1) * P],
                                          lhsT=ktokP[kp][:, 1024 + dt * P:1024 + (dt + 1) * P],
                                          rhs=cS["ident16"][:, :], start=True, stop=True)
                        b.inc(ii, V + "dKtr", 1)
                    elif pi == 2:
                        b.wait(t, V + "dldv", 16 * (k + 1))
                        b.wait(t, V + "dKtr_ev", 2 * (k + 1))
                        for dt in range(ND):
                            ii = t.matmul(psA[dt // 4][:, (dt % 4) * P:(dt % 4 + 1) * P],
                                          lhsT=vfmP[kp][:, dt * C2:dt * C2 + P],
                                          rhs=cS["ident16"][:, :], start=True, stop=True)
                        b.inc(ii, V + "dVtr", 1)
                    elif pi == 3:
                        b.wait(t, V + "dVtr_ev", 2 * k + 1)
                        for dt in range(ND):
                            ii = t.matmul(psA[dt // 4][:, (dt % 4) * P:(dt % 4 + 1) * P],
                                          lhsT=vfmP[kp][:, dt * C2 + P:(dt + 1) * C2],
                                          rhs=cS["ident16"][:, :], start=True, stop=True)
                        b.inc(ii, V + "dVtr", 1)
                    elif pi == 4:
                        b.wait(t, V + "dVtr_ev", 2 * (k + 1))
                        for jm in range(2):
                            for dt in range(ND):
                                ii = t.matmul(psA[0][:, jm * C2:(jm + 1) * C2],
                                              lhsT=kfm_c[:, dt * C2 + jm * P:dt * C2 + (jm + 1) * P],
                                              rhs=kfm_c[:, dt * C2:(dt + 1) * C2],
                                              start=(dt == 0), stop=(dt == ND - 1))
                        b.inc(ii, V + "dGmm", 1)
                    elif pi == 5:
                        b.wait(t, V + "dldq", 16 * (k + 1))
                        for jm in range(2):
                            for dt in range(ND):
                                ii = t.matmul(psA[1][:, jm * C2:(jm + 1) * C2],
                                              lhsT=kfm_c[:, dt * C2 + jm * P:dt * C2 + (jm + 1) * P],
                                              rhs=qP[k % 2][:, dt * C2:(dt + 1) * C2],
                                              start=(dt == 0), stop=(dt == ND - 1))
                        b.inc(ii, V + "dGmm", 1)
                    else:
                        b.wait(t, V + "dA2", k + 1)
                        for a_ in range(2):
                            for b_ in range(2):
                                ii = t.matmul(psA[0][:, (2 * a_ + b_) * P:(2 * a_ + b_ + 1) * P],
                                              lhsT=A2st[k % 2][:, b_ * C2 + a_ * P:b_ * C2 + (a_ + 1) * P],
                                              rhs=cS["ident16"][:, :], start=True, stop=True)
                        b.inc(ii, V + "dAtr", 1)

                sched = [("p", 0), ("A", 0), ("p", 1), ("B", 3), ("p", 2),
                         ("A", 1), ("p", 3), ("B", 4), ("p", 4), ("A", 2),
                         ("p", 6), ("p", 5), ("B", 5)]
                for kind, idx in sched:
                    if kind == "p" and PRE:
                        piece(idx)
                    elif kind == "A" and 0 <= jA < NCK:
                        lev_piece(jA, idx)
                    elif kind == "B" and 0 <= jB < NCK:
                        lev_piece(jB, idx)

        @blk.scalar
        def _(s):
            kv = kfm_c.rearrange("p (dt tm t) -> p dt tm t", dt=ND, tm=2)
            vv = vtok_c.rearrange("p (tm dt t) -> p tm dt t", tm=2, dt=ND)
            for k in range(NCK):
                b.wait(s, V + "dKtr", 2 * k + 1)
                if k >= 1:
                    b.wait(s, V + "dGmm", 2 * k)
                    b.wait(s, "ab_okf", 16 * k)
                s.activation(out=kv[:, 0:4, 0, :],
                             in_=psA[0][:, 0:512].rearrange("p (dt t) -> p dt t", dt=4),
                             func=AF.Identity)
                ii = s.activation(out=kv[:, 4:8, 0, :],
                                  in_=psA[1][:, 0:512].rearrange("p (dt t) -> p dt t", dt=4),
                                  func=AF.Identity)
                b.inc(ii, V + "dKtr_ev", 1)
                b.wait(s, V + "dKtr", 2 * (k + 1))
                s.activation(out=kv[:, 0:4, 1, :],
                             in_=psA[0][:, 0:512].rearrange("p (dt t) -> p dt t", dt=4),
                             func=AF.Identity)
                ii = s.activation(out=kv[:, 4:8, 1, :],
                                  in_=psA[1][:, 0:512].rearrange("p (dt t) -> p dt t", dt=4),
                                  func=AF.Identity)
                b.inc(ii, V + "dKtr_ev", 1)
                b.wait(s, V + "dVtr", 2 * k + 1)
                if k >= 1:
                    b.wait(s, "ab_ovt", 16 * k)
                s.activation(out=vv[:, 0, 0:4, :],
                             in_=psA[0][:, 0:512].rearrange("p (dt t) -> p dt t", dt=4),
                             func=AF.Identity)
                ii = s.activation(out=vv[:, 0, 4:8, :],
                                  in_=psA[1][:, 0:512].rearrange("p (dt t) -> p dt t", dt=4),
                                  func=AF.Identity)
                b.inc(ii, V + "dVtr_ev", 1)
                b.wait(s, V + "dVtr", 2 * (k + 1))
                s.activation(out=vv[:, 1, 0:4, :],
                             in_=psA[0][:, 0:512].rearrange("p (dt t) -> p dt t", dt=4),
                             func=AF.Identity)
                ii = s.activation(out=vv[:, 1, 4:8, :],
                                  in_=psA[1][:, 0:512].rearrange("p (dt t) -> p dt t", dt=4),
                                  func=AF.Identity)
                b.inc(ii, V + "dVtr_ev", 1)

        @blk.vector
        def _(v):
            kv = kfm_c.rearrange("p (dt tm t) -> p dt tm t", dt=ND, tm=2)
            vv = vtok_c.rearrange("p (tm dt t) -> p tm dt t", tm=2, dt=ND)

            def vlev(j, lev):
                p = j % 2
                Lp = j // 2
                sA = str(p)
                b.wait(v, V + "dPmmA" + sA, NLEV * Lp + lev + 1)
                b.wait(v, V + "dPmmB" + sA, NLEV * Lp + lev + 1)
                v.drain()
                b.inc(v.tensor_copy(Pnat[p][:, :], chA[p]), V + "dPev" + sA, 1)
                b.inc(v.tensor_copy(PnatT[p][:, :], chB[p]), V + "dPTev" + sA, 1)
                b.wait(v, V + "dMu_mm" + sA, NLEV * Lp + lev + 1)
                v.drain()
                v.tensor_tensor(out=Mst[p][:, 0:C2], in0=chC[p][:, 0:C2],
                                in1=Mst[p][:, 0:C2], op=ALU.add)
                ii = v.tensor_tensor(out=Mst[p][:, C2:2 * C2], in0=chC[p][:, C2:2 * C2],
                                     in1=Mst[p][:, C2:2 * C2], op=ALU.add)
                b.inc(ii, V + "dMuev" + sA, 1)
                v.drain()
                b.inc(v.tensor_copy(Mst[p][0:1, 0:1], Mst[p][0:1, 0:1]),
                      V + "dMl" + sA, 1)
                v.drain()
                if lev == NLEV - 1:
                    b.inc(v.tensor_copy(Mst[p][0:1, 0:1], Mst[p][0:1, 0:1]),
                          V + "dMdone", 1)

            for k in range(NCK + 2):
                PRE = k < NCK
                jA, jB = k - 1, k - 2

                def vpiece(pi, k=k):
                    kp = k % 2
                    if pi in (0, 1, 2, 3):
                        pass
                    elif pi == 4:
                        b.wait(v, V + "dGmm", 2 * k + 1)
                        if k >= 1:
                            b.wait(v, V + "dAtr", k)
                        v.drain()
                        b0 = ab_beta[:, 2 * k:2 * k + 1]
                        b1 = ab_beta[:, 2 * k + 1:2 * k + 2]
                        A2c = A2st[kp]
                        v.scalar_tensor_tensor(out=A2c[:, 0:P], in0=psA[0][:, 0:P],
                                               scalar=b0, in1=cS["mask_jlt"][:, :],
                                               op0=ALU.mult, op1=ALU.mult)
                        v.tensor_scalar(out=A2c[:, P:C2], in0=psA[0][:, P:C2],
                                        scalar1=b0, scalar2=None, op0=ALU.mult)
                        v.memset(A2c[:, C2:C2 + P], 0.0)
                        v.scalar_tensor_tensor(out=A2c[:, C2 + P:2 * C2],
                                               in0=psA[0][:, C2 + P:2 * C2],
                                               scalar=b1, in1=cS["mask_jlt"][:, :],
                                               op0=ALU.mult, op1=ALU.mult)
                        v.drain()
                        b.inc(v.tensor_copy(A2c[0:1, 0:1], A2c[0:1, 0:1]),
                              V + "dA2", 1)
                    elif pi == 5:
                        b.wait(v, V + "dGmm", 2 * (k + 1))
                        if k >= 1:
                            b.wait(v, "ab_ot2", 16 * k)
                        v.drain()
                        T2c = T2st[kp]
                        v.tensor_tensor(out=T2c[:, 0:P], in0=psA[1][:, 0:P],
                                        in1=cS["mask_jle"][:, :], op=ALU.mult)
                        v.tensor_copy(T2c[:, P:C2], psA[1][:, P:C2])
                        v.memset(T2c[:, C2:C2 + P], 0.0)
                        v.tensor_tensor(out=T2c[:, C2 + P:2 * C2],
                                        in0=psA[1][:, C2 + P:2 * C2],
                                        in1=cS["mask_jle"][:, :], op=ALU.mult)
                        v.drain()
                        b.inc(v.tensor_copy(T2c[0:1, 0:1], T2c[0:1, 0:1]),
                              V + "dT2", 1)
                    else:
                        b.wait(v, V + "dAtr", k + 1)
                        if k >= 2:
                            b.wait(v, "ab_omt", 16 * (k - 1))
                        v.drain()
                        A2c = A2st[kp]
                        v.tensor_copy(Pnat[kp][:, :], psA[0][:, 0:512])
                        v.tensor_copy(PnatT[kp][:, :], A2c[:, :])
                        v.tensor_tensor(out=Mst[kp][:, 0:P], in0=cS["ident16"][:, :],
                                        in1=A2c[:, 0:P], op=ALU.subtract)
                        v.tensor_scalar(out=Mst[kp][:, P:C2], in0=A2c[:, P:C2],
                                        scalar1=-1.0, scalar2=None, op0=ALU.mult)
                        v.memset(Mst[kp][:, C2:C2 + P], 0.0)
                        v.tensor_tensor(out=Mst[kp][:, C2 + P:2 * C2],
                                        in0=cS["ident16"][:, :],
                                        in1=A2c[:, C2 + P:2 * C2], op=ALU.subtract)
                        v.drain()
                        b.inc(v.tensor_copy(Mst[kp][0:1, 0:1], Mst[kp][0:1, 0:1]),
                              V + "dMl" + str(kp), 1)

                sched = [("p", 0), ("A", 0), ("p", 1), ("B", 3), ("p", 2),
                         ("A", 1), ("p", 3), ("B", 4), ("p", 4), ("A", 2),
                         ("p", 5), ("B", 5), ("p", 6)]
                for kind, idx in sched:
                    if kind == "p" and PRE:
                        vpiece(idx)
                    elif kind == "A" and 0 <= jA < NCK:
                        vlev(jA, idx)
                    elif kind == "B" and 0 <= jB < NCK:
                        vlev(jB, idx)

        @blk.sync
        def _(s):
            b.wait(s, V + "dKtr", 2 * NCK)
            b.wait(s, V + "dVtr", 2 * NCK)
            b.wait(s, V + "dGmm", 2 * NCK)
            b.wait(s, V + "dAtr", NCK)
            b.wait(s, "ab_okf", 16 * NCK)
            b.wait(s, "ab_ovt", 16 * NCK)
            b.wait(s, "ab_omt", 16 * NCK)
            b.wait(s, "ab_ot2", 16 * NCK)
            for k in range(NCK):
                if k >= 3:
                    b.wait(s, V + "dKS_mm", k - 2)
                if k >= 2:
                    b.wait(s, V + "dU_mm", 4 * (k - 1))
                    b.wait(s, V + "dO_mm", 2 * (k - 1))
                if k >= 1:
                    b.wait(s, V + "dS_mm", 16 * k)
                    b.wait(s, V + "dO_mm", 2 * k)
                    b.wait(s, V + "dWev", 2 * k)
                    for nm in ("ab_ikf", "ab_imt", "ab_it2", "ab_ikt",
                               "ab_iq", "ab_ivt"):
                        b.wait(s, nm, 16 * k)
                b.inc(s.dma_start(out=kfmS[k % 2],
                                  in_=kvscr[k * P:(k + 1) * P, :]), "ab_ikf", 16)
                b.inc(s.dma_start(out=Mld[k % 2],
                                  in_=selv_b[k * P:(k + 1) * P, 0:512]), "ab_imt", 16)
                b.inc(s.dma_start(out=T2ld[k % 2],
                                  in_=selv_b[k * P:(k + 1) * P, 512:1024]), "ab_it2", 16)
                b.inc(s.dma_start(
                    out=ktokS.rearrange("p (tm d) -> p tm d", tm=2),
                    in_=k_dram[k * C2:(k + 1) * C2, :].rearrange("(tm p) d -> p tm d", p=P)),
                    "ab_ikt", 16)
                b.inc(s.dma_start(
                    out=qS.rearrange("p (dt t) -> p dt t", dt=ND),
                    in_=q_dram[:, k * C2:(k + 1) * C2].rearrange("(dt p) t -> p dt t", p=P)),
                    "ab_iq", 16)
                b.inc(s.dma_start(out=vtokS,
                                  in_=kvscr[T + k * P:T + (k + 1) * P, :]), "ab_ivt", 16)

        b.free("ab_dKtr", "ab_dKtr_ev", "ab_dVtr", "ab_dVtr_ev", "ab_dGmm",
               "ab_dA2", "ab_dT2", "ab_dAtr", "ab_dMdone", "ab_dMl0", "ab_dMl1",
               "ab_dPmmA0", "ab_dPmmA1", "ab_dPmmB0", "ab_dPmmB1",
               "ab_dMu_mm0", "ab_dMu_mm1", "ab_dPev0", "ab_dPev1",
               "ab_dPTev0", "ab_dPTev1", "ab_dMuev0", "ab_dMuev1")

        @blk.tensor
        def _(t):
            b.wait(t, V + "wo_ld", 16)
            for k in range(NCK):
                first = (k == 0)
                b.wait(t, "ab_ikf", 16 * (k + 1))
                b.wait(t, "ab_imt", 16 * (k + 1))
                if not first:
                    b.wait(t, V + "dS16", 16 * k)
                    b.wait(t, V + "dOev", 2 * k)
                    b.wait(t, V + "dWo_ev", 4 * k)
                    for tm in range(2):
                        for nh in range(2):
                            for dkt in range(ND):
                                ii = t.matmul(psA[2 * tm + nh][:, 0:512],
                                              lhsT=kfmS[k % 2][:, dkt * C2 + tm * P:dkt * C2 + (tm + 1) * P],
                                              rhs=S16[:, dkt * 1024 + nh * 512:dkt * 1024 + (nh + 1) * 512],
                                              start=(dkt == 0), stop=(dkt == ND - 1))
                    b.inc(ii, V + "dKS_mm", 1)
                b.wait(t, V + "dWev", 2 * (k + 1))
                if not first:
                    b.wait(t, V + "dSev", 16 * k)
                for im in range(2):
                    for nh in range(2):
                        for jm in range(2):
                            ii = t.matmul(psA[2 + 2 * im + nh][:, 0:512],
                                          lhsT=Mld[k % 2][:, jm * C2 + im * P:jm * C2 + (im + 1) * P],
                                          rhs=W_c[:, jm * 1024 + nh * 512:jm * 1024 + (nh + 1) * 512],
                                          start=(jm == 0), stop=(jm == 1))
                        b.inc(ii, V + "dU_mm", 1)
                b.wait(t, V + "dUev", 4 * (k + 1))
                b.wait(t, "ab_ikt", 16 * (k + 1))
                for grp in range(4):
                    b.wait(t, V + "dSev", 16 * k + 4 * grp)
                    for gg in range(4):
                        dkt = grp * 2 + gg // 2
                        nh = gg % 2
                        for tm in range(2):
                            ii = t.matmul(psA[2 + gg][:, 0:512],
                                          lhsT=ktokS[:, tm * 1024 + dkt * P:tm * 1024 + (dkt + 1) * P],
                                          rhs=U_c[:, tm * 1024 + nh * 512:tm * 1024 + (nh + 1) * 512],
                                          start=(tm == 0), stop=(tm == 1))
                    b.inc(ii, V + "dS_mm", 4)
                b.wait(t, "ab_iq", 16 * (k + 1))
                b.wait(t, "ab_it2", 16 * (k + 1))
                b.wait(t, V + "dSev", 16 * (k + 1))
                for half in range(2):
                    if k > 0 or half > 0:
                        b.wait(t, V + "dOev", 2 * k + half)
                    for dvq in range(4):
                        dv = 4 * half + dvq
                        if not first:
                            for dkt in range(ND):
                                t.matmul(psA[dvq][:, 0:C2],
                                         lhsT=S16[:, dkt * 1024 + dv * P:dkt * 1024 + (dv + 1) * P],
                                         rhs=qS[:, dkt * C2:(dkt + 1) * C2],
                                         start=(dkt == 0), stop=False)
                        for jm in range(2):
                            ii = t.matmul(psA[dvq][:, 0:C2],
                                          lhsT=U_c[:, jm * 1024 + dv * P:jm * 1024 + (dv + 1) * P],
                                          rhs=T2ld[k % 2][:, jm * C2:(jm + 1) * C2],
                                          start=(first and jm == 0), stop=(jm == 1))
                    b.inc(ii, V + "dO_mm", 1)
                b.wait(t, V + "dOev", 2 * (k + 1))
                for tm in range(2):
                    for nh in range(2):
                        for dvt in range(ND):
                            ii = t.matmul(psR[:, nh * 512:(nh + 1) * 512] if tm == 0
                                          else psA[nh][:, 0:512],
                                          lhsT=O_c[:, dvt * C2 + tm * P:dvt * C2 + (tm + 1) * P],
                                          rhs=wbufA[:, dvt * D + nh * 512:dvt * D + (nh + 1) * 512],
                                          start=(dvt == 0), stop=(dvt == ND - 1))
                b.inc(ii, V + "dWo_mm", 1)

        @blk.vector
        def _(v):
            for k in range(NCK):
                first = (k == 0)
                b.wait(v, "ab_ivt", 16 * (k + 1))
                if first:
                    v.drain()
                    ii = v.tensor_copy(W_c[:, :], vtokS[:, :])
                    b.inc(ii, V + "dWev", 2)
                else:
                    b.wait(v, V + "dKS_mm", k)
                    v.drain()
                    for tm in range(2):
                        for nh in range(2):
                            ii = v.tensor_tensor(
                                out=W_c[:, tm * 1024 + nh * 512:tm * 1024 + (nh + 1) * 512],
                                in0=vtokS[:, tm * 1024 + nh * 512:tm * 1024 + (nh + 1) * 512],
                                in1=psA[2 * tm + nh][:, 0:512], op=ALU.subtract)
                    b.inc(ii, V + "dWev", 2)
                for im in range(2):
                    for nh in range(2):
                        b.wait(v, V + "dU_mm", 4 * k + 2 * im + nh + 1)
                        b.inc(v.tensor_scalar(
                            out=U_c[:, im * 1024 + nh * 512:im * 1024 + (nh + 1) * 512],
                            in0=psA[2 + 2 * im + nh][:, 0:512],
                            scalar1=ab_beta[:, 2 * k + im:2 * k + im + 1],
                            scalar2=None, op0=ALU.mult), V + "dUev", 1)
                for grp in range(4):
                    b.wait(v, V + "dS_mm", 16 * k + 4 * (grp + 1))
                    v.drain()
                    for gg in range(4):
                        dkt = grp * 2 + gg // 2
                        nh = gg % 2
                        off = dkt * 1024 + nh * 512
                        if first:
                            ii = v.tensor_copy(S32[:, off:off + 512], psA[2 + gg][:, 0:512])
                        else:
                            ii = v.tensor_tensor(out=S32[:, off:off + 512],
                                                 in0=psA[2 + gg][:, 0:512],
                                                 in1=S32[:, off:off + 512], op=ALU.add)
                        b.inc(ii, V + "dSev", 1)
                for half in range(2):
                    b.wait(v, V + "dO_mm", 2 * k + half + 1)
                    v.drain()
                    for dvq in range(4):
                        dv = 4 * half + dvq
                        ii = v.tensor_copy(O_c[:, dv * C2:(dv + 1) * C2],
                                           psA[dvq][:, 0:C2])
                    b.inc(ii, V + "dOev", 1)
                b.wait(v, V + "dWo_mm", k + 1)
                v.drain()
                for tm in range(2):
                    tt = 2 * k + tm
                    for nh in range(2):
                        src = psR[:, nh * 512:(nh + 1) * 512] if tm == 0 else psA[nh][:, 0:512]
                        b.inc(v.tensor_tensor(
                            out=x_tok[:, tt * D + nh * 512:tt * D + (nh + 1) * 512],
                            in0=src,
                            in1=x_tok[:, tt * D + nh * 512:tt * D + (nh + 1) * 512],
                            op=ALU.add), V + "dWo_ev", 1)

        @blk.scalar
        def _(s):
            for k in range(NCK):
                for nh in range(2):
                    first = True
                    for grp in range(4):
                        for gh in range(2):
                            gg = 2 * gh + nh
                            i16 = 4 * grp + gg
                            b.wait(s, V + "dSev", 16 * k + i16 + 1)
                            if first:
                                b.wait(s, V + "dO_mm", 2 * k + 1 + nh)
                                first = False
                            dkt = grp * 2 + gg // 2
                            off = dkt * 1024 + nh * 512
                            ii = s.activation(out=S16[:, off:off + 512],
                                              in_=S32[:, off:off + 512],
                                              func=AF.Identity)
                            b.inc(ii, V + "dS16", 1)

        emit_mlp()

    wg_buf = wbufA
    wu_buf = tmp_buf_.bitcast(BF16)[:, 0:8192]
    wd_buf = m_tok[:, 0:8192]
    gate_t = xb16[:, 0:4096]
    prod_t = xb16[:, 4096:8192]

    def emit_mlp():
        V = "ab_"
        NTB = NT // 4
        b.free("ab_dKtr", "ab_dKtr_ev", "ab_dVtr", "ab_dVtr_ev", "ab_dGmm",
               "ab_dA2", "ab_dAtr", "ab_dPtr", "ab_dPtr_ev", "ab_dMu_mm",
               "ab_dMu_ev", "ab_dP2_mm", "ab_dP2_ev", "ab_dMl", "ab_dMdone",
               "ab_dKS_mm", "ab_dU_mm", "ab_dUev", "ab_dOev")

        @blk.vector
        def _(v):
            b.wait(v, V + "dWo_ev", 4 * NCK)
            v.drain()
            for tt in range(NT):
                v.bn_stats(bn12[:, 12 * tt:12 * tt + 6], x_tok[:, tt * D:tt * D + 512])
                v.bn_stats(bn12[:, 12 * tt + 6:12 * tt + 12],
                           x_tok[:, tt * D + 512:(tt + 1) * D])
            v.drain()
            for tt in range(NT):
                ii = v.bn_aggr(ag2[:, 2 * tt:2 * tt + 2], bn12[:, 12 * tt:12 * tt + 12])
            v.drain()
            v.tensor_tensor(out=ab_ms[:, :], in0=ag2[:, 0::2], in1=ag2[:, 0::2],
                            op=ALU.mult)
            v.drain()
            b.inc(v.tensor_tensor(out=ab_ms[:, :], in0=ab_ms[:, :], in1=ag2[:, 1::2],
                                  op=ALU.add), V + "m_ms", 1)

        @blk.scalar
        def _(s):
            b.wait(s, V + "m_ms")
            s.activation(out=ab_rstd[:, :], in_=ab_ms[:, :], func=AF.Ln,
                         bias=cS["eps_rms"][:, :])
            s.drain()
            b.inc(s.activation(out=ab_rstd[:, :], in_=ab_rstd[:, :], func=AF.Exp,
                               scale=-0.5), V + "m_rstd", 1)
            s.drain()
            for tt in range(NT):
                if tt >= 2:
                    b.wait(s, V + "m_hev", 2 * (tt - 1))
                b.inc(s.activation(out=xln_rot[tt % 2], in_=x_tok[:, tt * D:(tt + 1) * D],
                                   func=AF.Identity, scale=ab_rstd[:, tt:tt + 1]),
                      V + "m_hem", 1)

        @blk.tensor
        def _(t):
            b.wait(t, V + "dS16", 16 * NCK)
            for tt in range(NT):
                b.wait(t, V + "m_hem", tt + 1)
                if tt >= 1:
                    b.wait(t, V + "m_hev", 2 * tt)
                for dt in range(ND):
                    ii = t.matmul(psA[4 + dt // 4][:, (dt % 4) * P:(dt % 4 + 1) * P],
                                  lhsT=xln_rot[tt % 2][:, dt * P:(dt + 1) * P],
                                  rhs=cS["ident16"][:, :], start=True, stop=True)
                b.inc(ii, V + "m_htr", 1)

        @blk.vector
        def _(v):
            hfm = xfm16[:, :].rearrange("p (dt t) -> p dt t", dt=ND)
            for tt in range(NT):
                b.wait(v, V + "m_htr", tt + 1)
                for h in range(2):
                    b.inc(v.tensor_copy(
                        hfm[:, 4 * h:4 * h + 4, tt * P:(tt + 1) * P],
                        psA[4 + h][:, 0:512].rearrange("p (dt t) -> p dt t", dt=4)),
                        V + "m_hev", 1)

        @blk.gpsimd
        def _(g):
            for fh in range(2):
                if fh == 0:
                    b.wait(g, V + "dWo_mm", NCK)
                else:
                    b.wait(g, V + "m_gmm", 8 * NTB)
                b.inc(g.dma_start(
                    out=wg_buf[:, :].rearrange("p (k n) -> p k n", k=ND),
                    in_=wD["b_gate"].rearrange("(k p) n -> p k n", p=P)[:, :, fh * D:(fh + 1) * D]),
                    V + f"wg{fh}", 16)
                if fh == 0:
                    b.wait(g, V + "dS16", 16 * NCK)
                    b.wait(g, V + "dO_mm", 2 * NCK)
                else:
                    b.wait(g, V + "m_umm", 8 * NTB)
                b.inc(g.dma_start(
                    out=wu_buf[:, :].rearrange("p (k n) -> p k n", k=ND),
                    in_=wD["b_up"].rearrange("(k p) n -> p k n", p=P)[:, :, fh * D:(fh + 1) * D]),
                    V + f"wu{fh}", 16)
                if fh == 0:
                    b.wait(g, V + "dSev", 16 * NCK)
                    b.wait(g, V + "dS16", 16 * NCK)
                else:
                    b.wait(g, V + "m_dmm", 4 * NTB)
                b.inc(g.dma_start(
                    out=wd_buf[:, :].rearrange("p (k n) -> p k n", k=ND),
                    in_=wD["b_down"].rearrange("(k p) n -> p k n", p=P)[:, fh * ND:(fh + 1) * ND, :]),
                    V + f"wd{fh}", 16)

        @blk.tensor
        def _(t):
            for fh in range(2):
                b.wait(t, V + f"wgs{fh}", 1)
                b.wait(t, V + f"wus{fh}", 1)
                b.wait(t, V + f"wd{fh}", 16)
                for tb in range(NTB):
                    gidx = fh * NTB + tb
                    for ft in range(ND):
                        fi = gidx * ND + ft
                        if fi >= 2:
                            b.wait(t, V + "m_sil", fi - 1)
                        for dt in range(ND):
                            ii = t.matmul(psA[ft % 2][:, 0:512],
                                          lhsT=wg_buf[:, dt * D + ft * P:dt * D + (ft + 1) * P],
                                          rhs=xfm16[:, dt * T + tb * 512:dt * T + (tb + 1) * 512],
                                          start=(dt == 0), stop=(dt == ND - 1))
                        b.inc(ii, V + "m_gmm", 1)
                    for ft in range(ND):
                        fi = gidx * ND + ft
                        if fi >= 2:
                            b.wait(t, V + "m_pev", fi - 1)
                        for dt in range(ND):
                            ii = t.matmul(psA[2 + ft % 2][:, 0:512],
                                          lhsT=wu_buf[:, dt * D + ft * P:dt * D + (ft + 1) * P],
                                          rhs=xfm16[:, dt * T + tb * 512:dt * T + (tb + 1) * 512],
                                          start=(dt == 0), stop=(dt == ND - 1))
                        b.inc(ii, V + "m_umm", 1)
                    b.wait(t, V + "m_pev", gidx * ND + ND)
                    for ttl in range(4):
                        di = 4 * gidx + ttl
                        if di >= 2:
                            b.wait(t, V + "m_xadd", 2 * (di - 1))
                        for nh in range(2):
                            dst = (psA[4 + ttl % 2][:, 0:512] if nh == 0
                                   else psR[:, (ttl % 2) * 512:(ttl % 2) * 512 + 512])
                            for ff in range(ND):
                                ii = t.matmul(dst,
                                              lhsT=prod_t[:, ff * 512 + ttl * P:ff * 512 + (ttl + 1) * P],
                                              rhs=wd_buf[:, ff * D + nh * 512:ff * D + (nh + 1) * 512],
                                              start=(ff == 0), stop=(ff == ND - 1))
                        b.inc(ii, V + "m_dmm", 1)

        @blk.scalar
        def _(s):
            for fh in range(2):
                for tb in range(NTB):
                    gidx = fh * NTB + tb
                    for ft in range(ND):
                        fi = gidx * ND + ft
                        b.wait(s, V + "m_gmm", fi + 1)
                        s.drain()
                        ii = s.activation(out=gate_t[:, ft * 512:(ft + 1) * 512],
                                          in_=psA[ft % 2][:, 0:512], func=AF.Silu)
                        b.inc(ii, V + "m_sil", 1)

        @blk.vector
        def _(v):
            for fh in range(2):
                b.wait(v, V + f"wg{fh}", 16)
                if fh == 0:
                    b.wait(v, V + "m_hev", 2 * NT)
                v.drain()
                scale_rows_by(v, wg_buf, n2w, ND, D, V + f"wgs{fh}")
                b.wait(v, V + f"wu{fh}", 16)
                scale_rows_by(v, wu_buf, n2w, ND, D, V + f"wus{fh}")
                for tb in range(NTB):
                    gidx = fh * NTB + tb
                    for ft in range(ND):
                        fi = gidx * ND + ft
                        b.wait(v, V + "m_umm", fi + 1)
                        b.wait(v, V + "m_sil", fi + 1)
                        v.drain()
                        ii = v.tensor_tensor(out=prod_t[:, ft * 512:(ft + 1) * 512],
                                             in0=psA[2 + ft % 2][:, 0:512],
                                             in1=gate_t[:, ft * 512:(ft + 1) * 512],
                                             op=ALU.mult)
                        b.inc(ii, V + "m_pev", 1)
                    for ttl in range(4):
                        b.wait(v, V + "m_dmm", 4 * gidx + ttl + 1)
                        v.drain()
                        tt = tb * 4 + ttl
                        for nh in range(2):
                            src = (psA[4 + ttl % 2][:, 0:512] if nh == 0
                                   else psR[:, (ttl % 2) * 512:(ttl % 2) * 512 + 512])
                            b.inc(v.tensor_tensor(
                                out=x_tok[:, tt * D + nh * 512:tt * D + (nh + 1) * 512],
                                in0=src,
                                in1=x_tok[:, tt * D + nh * 512:tt * D + (nh + 1) * 512],
                                op=ALU.add), V + "m_xadd", 1)

        pk_st = [xb_buf_[:, 0:D], xb_buf_[:, D:2 * D]]

        @blk.sync
        def _(s):
            b.wait(s, V + "m_xadd")
            b.wait(s, V + "m_dmm")
            for tt in range(NT):
                if tt >= 2:
                    b.wait(s, V + "pkadd", tt - 1)
                b.inc(s.dma_start(out=pk_st[tt % 2],
                                  in_=y_dram[tt * P:(tt + 1) * P, :]),
                      V + f"pk{tt % 2}", 16)

        @blk.vector
        def _(v):
            v.drain()
            for tt in range(NT):
                b.wait(v, V + f"pk{tt % 2}", 16 * (tt // 2 + 1))
                v.drain()
                b.inc(v.tensor_tensor(out=x_tok[:, tt * D:(tt + 1) * D],
                                      in0=pk_st[tt % 2],
                                      in1=x_tok[:, tt * D:(tt + 1) * D],
                                      op=ALU.add), V + "pkadd", 1)
                if N_SUB >= 4:
                    bn_tile_hook(v, tt)

    S32_dump = m_tok.bitcast(F32)
    bstat_dump = yst["vsum"]
    if N_SUB >= 3:
        b.free_prefix("v0_", keep=("v0_xadd",))
        emit_attnb()
    if N_SUB >= 4:
        b.free_prefix("ab_", keep=("ab_pkadd", "ab_m_xadd"))
        b.rebase("md_ldr", "md_ldfb2", "md_ldw1", "md_ldw2", "md_scat0",
                 "md_scat1", "md_selld", "md_upda0", "md_upda1", "md_gat0",
                 "md_gat1", "md_gat2", "md_gat3")
        emit_mod(1)

    @blk.gpsimd
    def _(g):
        if DBG_ATTNB and N_SUB == 3:
            b.wait(g, "ab_pkadd", b.n("ab_pkadd"))
            b.inc(g.dma_start(out=dbg["q"], in_=q_dram.ap()), "st_out", 16)
            b.inc(g.dma_start(out=dbg["k"], in_=k_dram.ap()), "st_out", 16)
            b.inc(g.dma_start(out=dbg["v"], in_=vema_dram.ap()), "st_out", 16)
            b.inc(g.dma_start(out=dbg["beta"], in_=bstat_dump[:, :]), "st_out", 16)
        ov_ = out.rearrange("(tt p) d -> p tt d", p=P)
        xv_ = x_tok[:, :].rearrange("p (tt d) -> p tt d", tt=NT)
        if N_SUB >= 4:
            for j in range(4):
                b.wait(g, "v1_xadd", 4 * (j + 1))
                b.inc(g.dma_start(out=ov_[:, 4 * j:4 * j + 4, :],
                                  in_=xv_[:, 4 * j:4 * j + 4, :]), f"st_out", 16)
        else:
            if N_SUB == 3:
                b.wait(g, "ab_m_xadd", b.n("ab_m_xadd"))
                b.wait(g, "ab_pkadd", b.n("ab_pkadd"))
            elif N_SUB == 2:
                b.wait(g, "v0_xadd", b.n("v0_xadd"))
            else:
                b.wait(g, "oa_ev")
            b.inc(g.dma_start(out=ov_, in_=xv_), "st_out", 16)
        b.wait(g, "st_out")


_CACHE = {}


def kernel(L0_attn_a_x, L0_attn_a_Wq, L0_attn_a_Wk, L0_attn_a_Wv,
           L0_attn_b_norm1_w, L0_attn_b_Wq, L0_attn_b_Wk, L0_attn_b_Wv,
           L0_attn_b_beta_w, L0_attn_b_beta_b, L0_attn_b_Wo, L0_attn_b_alpha_logit,
           L0_attn_b_norm2_w, L0_attn_b_w_gate, L0_attn_b_w_up, L0_attn_b_w_down,
           L0_ffn_w_router, L0_ffn_W1, L0_ffn_b1, L0_ffn_W2, L0_ffn_b2):
    x = np.ascontiguousarray(np.asarray(L0_attn_a_x, dtype=np.float32))
    consts = build_consts()
    if "nc" not in _CACHE:
        _CACHE["nc"] = build_nc()
    nc = _CACHE["nc"]

    common = {
        "x_all": x,
        "aWq": np.ascontiguousarray(L0_attn_a_Wq, dtype=np.float32),
        "aWk": np.ascontiguousarray(L0_attn_a_Wk, dtype=np.float32),
        "aWv": np.ascontiguousarray(L0_attn_a_Wv, dtype=np.float32),
        "f_router": np.ascontiguousarray(L0_ffn_w_router, dtype=np.float32).reshape(1, D),
        "fW1": np.ascontiguousarray(L0_ffn_W1, dtype=np.float32),
        "fb1": np.ascontiguousarray(np.asarray(L0_ffn_b1, dtype=np.float32).reshape(FF // P, P).T),
        "fW2": np.ascontiguousarray(L0_ffn_W2, dtype=np.float32),
        "fb2": np.ascontiguousarray(L0_ffn_b2, dtype=np.float32).reshape(1, D),
        "b_n1": np.ascontiguousarray(np.asarray(L0_attn_b_norm1_w, dtype=np.float32).reshape(D // P, P).T),
        "bWq": np.ascontiguousarray(L0_attn_b_Wq, dtype=np.float32),
        "bWk": np.ascontiguousarray(L0_attn_b_Wk, dtype=np.float32),
        "bWv": np.ascontiguousarray(L0_attn_b_Wv, dtype=np.float32),
        "b_betaw": np.ascontiguousarray(np.asarray(L0_attn_b_beta_w, dtype=np.float32).reshape(D // P, P).T),
        "b_betab": np.ascontiguousarray(L0_attn_b_beta_b, dtype=np.float32).reshape(1, 1),
        "bWo": np.ascontiguousarray(L0_attn_b_Wo, dtype=np.float32),
        "b_alpha": np.ascontiguousarray(np.asarray(L0_attn_b_alpha_logit, dtype=np.float32).reshape(D // P, P).T),
        "b_n2": np.ascontiguousarray(np.asarray(L0_attn_b_norm2_w, dtype=np.float32).reshape(D // P, P).T),
        "b_gate": np.ascontiguousarray(L0_attn_b_w_gate, dtype=np.float32),
        "b_up": np.ascontiguousarray(L0_attn_b_w_up, dtype=np.float32),
        "b_down": np.ascontiguousarray(L0_attn_b_w_down, dtype=np.float32),
    }
    for nm, arr in consts.items():
        common["c_" + nm] = np.ascontiguousarray(arr)

    in_maps = []
    for c in range(B):
        m = dict(common)
        m["x_own"] = np.ascontiguousarray(x[c])
        in_maps.append(m)

    res = run_bass_kernel_spmd(nc, in_maps, core_ids=list(range(B)))
    return np.stack([np.asarray(res.results[c]["out"]) for c in range(B)], axis=0)
